# version 02
# baseline (speedup 1.0000x reference)
import numpy as np
import ml_dtypes
from contextlib import ExitStack
import concourse.bass as bass
import concourse.mybir as mybir
from concourse.bass_utils import run_bass_kernel_spmd

F32 = mybir.dt.float32
BF16 = mybir.dt.bfloat16
AF = mybir.ActivationFunctionType
ALU = mybir.AluOpType
AX = mybir.AxisListType
BF = ml_dtypes.bfloat16

NCORES = 8
BATCH = 2
SEQ = 8192
DM = 2048
SPC = SEQ // NCORES
HALO = 3
T2B = SPC + HALO
T2 = 2 * T2B
TPC = 2 * SPC
PROJ = 8512
IDIM = 4096
NH = 64
HPD = 64
NSD = 128
CONV_DIM = 4352
CK = 128
NCKB = SPC // CK
EPS_LN = 1e-5
EPS_RMS = 1e-5

BTGS = [(0, 512), (512, 512), (1024, 3)]
BTGS = [(0, 512), (512, 512), (1024, 3)]

RG8 = [[0, 1, 2, 3, 4, 5, 6, 7]]
PHX = [0]


class Prog:
    ENGS = ("sync", "vector", "scalar", "tensor", "gpsimd")

    def __init__(self):
        self.items = {e: [] for e in self.ENGS}

    def on(self, eng, fn):
        self.items[eng].append(fn)

    def emit(self, block):
        def mk(name):
            def body(eng):
                for fn in self.items[name]:
                    fn(eng)
            return body
        block.sync(mk("sync"))
        if self.items["vector"]:
            block.vector(mk("vector"))
        if self.items["scalar"]:
            block.scalar(mk("scalar"))
        if self.items["tensor"]:
            block.tensor(mk("tensor"))
        if self.items["gpsimd"]:
            block.gpsimd(mk("gpsimd"))


def phase1(nc, io):
    xh = io["xh"]; w_inT = io["w_inT"]; conv_w = io["conv_wr"]; conv_b = io["conv_br"]
    dt_bias = io["dt_bias"]; cvec = io["cvec"]
    o_xnT = io["d_xnT"]; o_xnTlo = io["d_xnTlo"]; o_sg = io["d_sg"]
    o_xs = io["d_xs"]; o_bc = io["d_bc"]; o_dt = io["d_dt"]

    prog = Prog(); PHX[0] += 1
    with ExitStack() as st:
        _ctr = [0]
        def sb(shape, dt=F32):
            _ctr[0] += 1
            return st.enter_context(nc.sbuf_tensor("sb%d_%d" % (PHX[0], _ctr[0]), list(shape), dt))
        def ps(shape, dt=F32):
            _ctr[0] += 1
            return st.enter_context(nc.psum_tensor("ps%d_%d" % (PHX[0], _ctr[0]), list(shape), dt))
        def sem(name):
            return st.enter_context(nc.semaphore("s%d_%s" % (PHX[0], name)))

        xnT = [sb((128, T2), BF16) for _ in range(16)]
        xnTlo = [sb((128, T2), BF16) for _ in range(16)]
        sx = [sb((128, DM)) for _ in range(2)]
        sxn = [sb((128, DM), BF16) for _ in range(2)]
        sxnlo = [sb((128, DM), BF16) for _ in range(2)]
        sxf = sb((128, DM))
        sqscr = None
        sumc, sqc, meanc, nvc, stdc, rstdc = (sb((128, 1)) for _ in range(6))
        identb = sb((128, 128), BF16)
        iota_p = sb((128, 1), mybir.dt.int32)
        iota_f = sb((128, 128), mybir.dt.int32)
        fidp = sb((128, 1))
        fidf = sb((128, 128))
        wbf = [sb((128, 16 * 128), BF16) for _ in range(2)]
        hbuf = sb((128, T2))
        conv_acc = sb((128, SPC))
        hbf = [sb((128, SPC), BF16) for _ in range(2)]
        sgst = [sb((128, SPC), BF16) for _ in range(2)]
        cw_sb = sb((128, 34 * 4))
        cb_sb = sb((128, 34))
        dtb_sb = sb((NH, 1))
        cv_sb = sb((128, 8))
        dtbuf = sb((NH, T2))
        dtv = dtbuf
        ptr = [ps((128, 128), BF16) for _ in range(2)]
        pproj = [ps((128, 512)) for _ in range(6)]

        ld = sem("ld"); gio = sem("gio"); idn = sem("idn")
        vA = sem("vA"); sA = sem("sA"); xnv = sem("xnv")
        trs = sem("trs"); evs = sem("evs")
        wld = sem("wld"); mmd = sem("mmd"); cons = sem("cons")
        csl = sem("csl"); dst = sem("dst"); dsg = sem("dsg"); dxb = sem("dxb")
        block = st.enter_context(nc.Block())

        def g_iota(e):
            e.iota(iota_p[:], pattern=[[0, 1]], base=0, channel_multiplier=1)
            e.iota(iota_f[:], pattern=[[1, 128]], base=0, channel_multiplier=0).then_inc(gio, 1)
        prog.on("gpsimd", g_iota)

        def v_ident(e):
            e.wait_ge(gio, 1)
            e.tensor_copy(out=fidp[:], in_=iota_p[:])
            e.tensor_copy(out=fidf[:], in_=iota_f[:])
            e.tensor_scalar(out=identb[:], in0=fidf[:], scalar1=fidp[:], scalar2=None,
                            op0=ALU.is_equal).then_inc(idn, 1)
        prog.on("vector", v_ident)

        def s_loads(e):
            e.dma_start(out=cw_sb[:, 0:136], in_=conv_w).then_inc(ld, 16)
            e.dma_start(out=cb_sb[:, 0:34], in_=conv_b).then_inc(ld, 16)
            e.dma_start(out=dtb_sb[:, :], in_=dt_bias).then_inc(ld, 16)
            e.dma_start(out=cv_sb[:, :], in_=cvec.to_broadcast((128, 8))).then_inc(ld, 16)
        prog.on("sync", s_loads)
        n_ld = 64
        C_EPSLN, C_ONE = 0, 1

        ttiles = []
        for b in range(BATCH):
            for i in range(9):
                ttiles.append((b, i * 128, 128 if i < 8 else T2B - 8 * 128))

        n_xn = 0
        n_tr = 0
        n_ev = 0
        n_vA = 0
        n_sA = 0
        gidx = 0

        for it, (b, r0, rows) in enumerate(ttiles):
            buf = it % 2
            need_xn = max(0, it - 1)
            def f_load(e, b=b, r0=r0, rows=rows, buf=buf, need_xn=need_xn, ldv=n_ld + 16):
                e.wait_ge(xnv, need_xn)
                e.dma_start(out=sx[buf][0:rows, :], in_=xh[b, r0:r0 + rows, :]).then_inc(ld, 16)
            prog.on("sync", f_load)
            n_ld += 16
            ldv = n_ld

            n_sA += 1
            def f_sq(e, rows=rows, buf=buf, ldv=ldv):
                e.wait_ge(ld, ldv)
                e.activation(out=hbuf[0:rows, 0:DM], in_=sx[buf][0:rows, :], func=AF.Square,
                             accum_out=sqc[0:rows, :]).then_inc(sA, 1)
            prog.on("scalar", f_sq)
            sAv = n_sA

            n_vA += 2
            def f_stats(e, rows=rows, buf=buf, ldv=ldv, sAv=sAv, w0=n_vA - 1):
                e.wait_ge(ld, ldv)
                e.reduce_sum(out=sumc[0:rows, :], in_=sx[buf][0:rows, :], axis=AX.X).then_inc(vA, 1)
                e.wait_ge(vA, w0)
                e.tensor_scalar(out=meanc[0:rows, :], in0=sumc[0:rows, :], scalar1=1.0 / DM,
                                scalar2=None, op0=ALU.mult)
                e.wait_ge(sA, sAv)
                e.tensor_scalar(out=nvc[0:rows, :], in0=sqc[0:rows, :], scalar1=1.0 / DM,
                                scalar2=None, op0=ALU.mult)
                e.scalar_tensor_tensor(out=nvc[0:rows, :], in0=meanc[0:rows, :],
                                       scalar=meanc[0:rows, :], in1=nvc[0:rows, :],
                                       op0=ALU.mult, op1=ALU.subtract).then_inc(vA, 1)
            prog.on("vector", f_stats)
            vAv = n_vA

            n_sA += 1
            def f_std(e, rows=rows, vAv=vAv):
                e.wait_ge(vA, vAv)
                e.activation(out=stdc[0:rows, :], in_=nvc[0:rows, :], func=AF.Sqrt,
                             scale=-1.0, bias=cv_sb[0:rows, C_EPSLN:C_EPSLN + 1]).then_inc(sA, 1)
            prog.on("scalar", f_std)
            sAv2 = n_sA

            need_tr = 32 * max(0, it - 1)
            n_xn += 1
            def f_xn(e, rows=rows, buf=buf, sAv2=sAv2, need_tr=need_tr):
                e.wait_ge(sA, sAv2)
                e.wait_ge(trs, need_tr)
                e.reciprocal(out=rstdc[0:rows, :], in_=stdc[0:rows, :])
                e.tensor_scalar(out=sxf[0:rows, :], in0=sx[buf][0:rows, :],
                                scalar1=meanc[0:rows, :], scalar2=rstdc[0:rows, :],
                                op0=ALU.subtract, op1=ALU.mult)
                e.tensor_copy(out=sxn[buf][0:rows, :], in_=sxf[0:rows, :])
                e.scalar_tensor_tensor(out=sxnlo[buf][0:rows, :], in0=sxn[buf][0:rows, :],
                                       scalar=-1.0, in1=sxf[0:rows, :],
                                       op0=ALU.mult, op1=ALU.add).then_inc(xnv, 1)
            prog.on("vector", f_xn)
            xnvv = n_xn

            tcol = b * T2B + r0
            for kg in range(16):
                lo = kg >= 8
                src_t = (sxnlo if lo else sxn)
                dst_t = (xnTlo if lo else xnT)
                kb = kg % 8
                need_ev = 2 * gidx
                n_tr += 2
                def f_tr(e, rows=rows, buf=buf, kb=kb, src_t=src_t, xnvv=xnvv, need_ev=need_ev):
                    e.wait_ge(xnv, xnvv)
                    e.wait_ge(idn, 1)
                    e.wait_ge(evs, need_ev)
                    last = None
                    for j in range(2):
                        last = e.transpose(ptr[j][0:128, 0:rows],
                                           src_t[buf][0:rows, (kb * 2 + j) * 128:(kb * 2 + j + 1) * 128],
                                           identb[0:rows, 0:rows])
                    last.then_inc(trs, 2)
                prog.on("tensor", f_tr)
                trv = n_tr
                n_ev += 2
                def f_ev(e, rows=rows, kb=kb, dst_t=dst_t, tcol=tcol, trv=trv):
                    e.wait_ge(trs, trv)
                    last = None
                    for j in range(2):
                        last = e.copy(out=dst_t[kb * 2 + j][:, tcol:tcol + rows],
                                      in_=ptr[j][0:128, 0:rows])
                    last.then_inc(evs, 2)
                prog.on("scalar", f_ev)
                gidx += 1

        all_ev = n_ev
        n_dst = 0
        for k in range(16):
            n_dst += 32
            def f_spill(e, k=k, all_ev=all_ev):
                e.wait_ge(evs, all_ev)
                e.dma_start(out=o_xnT[k * 128:(k + 1) * 128, :], in_=xnT[k][:, :]).then_inc(dst, 16)
                e.dma_start(out=o_xnTlo[k * 128:(k + 1) * 128, :], in_=xnTlo[k][:, :]).then_inc(dst, 16)
            prog.on("sync", f_spill)

        n_wld = 0
        n_mm = 0
        n_cons = 0
        n_csl = 0
        n_dsg = 0
        n_dxb = 0

        for ob in range(67):
            rows = 64 if ob == 66 else 128
            buf = ob % 2
            need_mm = 2 * max(0, ob - 1)
            n_wld += 16
            def f_wdma(e, ob=ob, rows=rows, buf=buf, need_mm=need_mm):
                e.wait_ge(mmd, need_mm)
                e.dma_start(out=wbf[buf][:, 0:16 * rows].rearrange("d (k o) -> d k o", o=rows),
                            in_=w_inT.rearrange("(k d) o -> d k o", d=128)[:, :, ob * 128:ob * 128 + rows]
                            ).then_inc(wld, 16)
            prog.on("gpsimd", f_wdma)
            wldv = n_wld

            for b in range(BATCH):
                u = 2 * ob + b
                pset = u % 2
                pbanks = pproj[pset * 3: pset * 3 + 3]
                need_cons = max(0, u - 1)
                n_mm += 1
                def f_mm(e, ob=ob, rows=rows, buf=buf, b=b, pbanks=pbanks, wldv=wldv,
                         need_cons=need_cons, all_ev=all_ev):
                    e.wait_ge(wld, wldv)
                    e.wait_ge(cons, need_cons)
                    e.wait_ge(evs, all_ev)
                    last = None
                    for tgi, (t0, tw) in enumerate(BTGS):
                        for k in range(16):
                            last = e.matmul(pbanks[tgi][0:rows, 0:tw],
                                            wbf[buf][:, k * rows:(k + 1) * rows],
                                            xnT[k][:, b * T2B + t0: b * T2B + t0 + tw],
                                            start=(k == 0), stop=(k == 15))
                    last.then_inc(mmd, 1)
                prog.on("tensor", f_mm)
                mmv = n_mm

                if ob < 32:
                    gu = 2 * ob + b
                    n_cons += 1
                    def f_gate(e, b=b, pbanks=pbanks, mmv=mmv, gu=gu):
                        e.wait_ge(mmd, mmv)
                        e.wait_ge(dsg, 16 * max(0, gu - 1))
                        e.activation(out=sgst[b][:, 0:509], in_=pbanks[0][:, 3:512], func=AF.Silu)
                        e.activation(out=sgst[b][:, 509:1021], in_=pbanks[1][:, 0:512], func=AF.Silu)
                        e.activation(out=sgst[b][:, 1021:1024], in_=pbanks[2][:, 0:3],
                                     func=AF.Silu).then_inc(cons, 1)
                    prog.on("scalar", f_gate)
                    def f_sgdma(e, ob=ob, b=b, need=n_cons):
                        e.wait_ge(cons, need)
                        e.dma_start(out=o_sg[ob * 128:(ob + 1) * 128, b * SPC:(b + 1) * SPC],
                                    in_=sgst[b][:, :]).then_inc(dsg, 16)
                    prog.on("sync", f_sgdma)
                    n_dsg += 16
                elif ob < 66:
                    hb = ob - 32
                    n_cons += 1
                    def f_hcp(e, b=b, pbanks=pbanks, mmv=mmv):
                        e.wait_ge(mmd, mmv)
                        last = None
                        for tgi, (t0, tw) in enumerate(BTGS):
                            last = e.tensor_copy(out=hbuf[:, b * T2B + t0:b * T2B + t0 + tw],
                                                 in_=pbanks[tgi][:, 0:tw])
                        last.then_inc(cons, 1)
                    prog.on("vector", f_hcp)
                    need_csl = n_csl
                    def f_conv(e, hb=hb, b=b, need_csl=need_csl):
                        e.wait_ge(csl, need_csl)
                        e.tensor_scalar(out=conv_acc[:, :], in0=hbuf[:, b * T2B + 0:b * T2B + SPC],
                                        scalar1=cw_sb[:, hb * 4:hb * 4 + 1],
                                        scalar2=cb_sb[:, hb:hb + 1], op0=ALU.mult, op1=ALU.add)
                        for k in range(1, 4):
                            e.scalar_tensor_tensor(out=conv_acc[:, :],
                                                   in0=hbuf[:, b * T2B + k:b * T2B + k + SPC],
                                                   scalar=cw_sb[:, hb * 4 + k:hb * 4 + k + 1],
                                                   in1=conv_acc[:, :], op0=ALU.mult, op1=ALU.add)
                        e.tensor_copy(out=conv_acc[0:1, 0:1], in_=conv_acc[0:1, 0:1]).then_inc(vA, 1)
                    prog.on("vector", f_conv)
                    n_vA += 1
                    cvv = n_vA
                    hu = 2 * hb + b
                    n_csl += 1
                    def f_hsilu(e, b=b, cvv=cvv, hu=hu):
                        e.wait_ge(vA, cvv)
                        e.wait_ge(dxb, 16 * max(0, hu - 1))
                        e.activation(out=hbf[b][:, :], in_=conv_acc[:, :], func=AF.Silu).then_inc(csl, 1)
                    prog.on("scalar", f_hsilu)
                    cslv = n_csl
                    if hb < 32:
                        tgt = o_xs[hb * 128:(hb + 1) * 128, b * SPC:(b + 1) * SPC]
                    else:
                        tgt = o_bc[(hb - 32) * 128:(hb - 31) * 128, b * SPC:(b + 1) * SPC]
                    def f_hdma(e, tgt=tgt, b=b, cslv=cslv):
                        e.wait_ge(csl, cslv)
                        e.dma_start(out=tgt, in_=hbf[b][:, :]).then_inc(dxb, 16)
                    prog.on("sync", f_hdma)
                    n_dxb += 16
                else:
                    n_cons += 1
                    def f_dtcp(e, b=b, pbanks=pbanks, mmv=mmv, rows=rows):
                        e.wait_ge(mmd, mmv)
                        last = None
                        for tgi, (t0, tw) in enumerate(BTGS):
                            last = e.tensor_copy(out=dtbuf[0:rows, b * T2B + t0:b * T2B + t0 + tw],
                                                 in_=pbanks[tgi][0:rows, 0:tw])
                        last.then_inc(cons, 1)
                    prog.on("vector", f_dtcp)
                    if b == 1:
                        need = n_cons
                        n_sA += 1
                        def f_dtsp(e, need=need):
                            e.wait_ge(cons, need)
                            e.activation(out=dtv[:, :], in_=dtbuf[:, :], func=AF.Exp,
                                         scale=1.0, bias=dtb_sb[:, :])
                            e.activation(out=dtv[:, :], in_=dtv[:, :], func=AF.Ln,
                                         scale=1.0, bias=cv_sb[0:NH, C_ONE:C_ONE + 1]).then_inc(sA, 1)
                        prog.on("scalar", f_dtsp)
                        sfin = n_sA
                        n_dst += 32
                        def f_dtdma(e, sfin=sfin):
                            e.wait_ge(sA, sfin)
                            for bb in range(2):
                                e.dma_start(out=o_dt[:, bb * SPC:(bb + 1) * SPC],
                                            in_=dtv[:, bb * T2B + HALO:bb * T2B + HALO + SPC]
                                            ).then_inc(dst, 16)
                        prog.on("sync", f_dtdma)

        def f_fin(e, a=n_dst, bb=n_dsg, c=n_dxb):
            e.wait_ge(dst, a)
            e.wait_ge(dsg, bb)
            e.wait_ge(dxb, c)
        prog.on("sync", f_fin)
        prog.emit(block)
    return nc



def phase2(nc, io):
    i_xs = io["d_xs"]; i_bc = io["d_bc"]; i_dt = io["d_dt"]; i_alog = io["alog"]
    o_xp = io["d_xp"]; o_sj = io["d_sj"]
    o_span = io["spn_loc"]; o_dspan = io["dsp_loc"]

    prog = Prog(); PHX[0] += 1
    with ExitStack() as st:
        _ctr = [0]
        def sb(shape, dt=F32):
            _ctr[0] += 1
            return st.enter_context(nc.sbuf_tensor("sb%d_%d" % (PHX[0], _ctr[0]), list(shape), dt))
        def ps(shape, dt=F32):
            _ctr[0] += 1
            return st.enter_context(nc.psum_tensor("ps%d_%d" % (PHX[0], _ctr[0]), list(shape), dt))
        def sem(name):
            return st.enter_context(nc.semaphore("s%d_%s" % (PHX[0], name)))

        dtv = sb((NH, TPC))
        alog = sb((NH, 1))
        acol = sb((NH, 1))
        dA = sb((NH, TPC))
        acum = sb((NH, TPC))
        dteh = sb((NH, CK))
        dteF = sb((NH, CK))
        dteT = sb((CK, NH))
        cdrow = sb((1, NH))
        cdB = sb((128, NH))
        onesf = sb((1, 128))
        dsum = sb((NH, BATCH))
        dspan = sb((NH, BATCH))
        identb = sb((128, 128), BF16)
        identf = sb((128, 128), F32)
        iota_p = sb((128, 1), mybir.dt.int32)
        iota_f = sb((128, 128), mybir.dt.int32)
        fidp = sb((128, 1))
        fidf = sb((128, 128))
        bc_sb = sb((128, TPC), BF16)
        xsch = [sb((128, IDIM), BF16) for _ in range(2)]
        xp_sb = [sb((128, IDIM), BF16) for _ in range(2)]
        xpp_sb = sb((128, IDIM), BF16)
        bt_sb = sb((128, 128), BF16)
        ssb = [sb((128, IDIM)) for _ in range(2)]
        tmp_sb = sb((128, 1024))
        SC = [sb((128, IDIM)) for _ in range(BATCH)]
        ptb = [ps((128, 128), BF16) for _ in range(2)]
        pcd = ps((128, 128))
        pst = [ps((128, 1024)) for _ in range(2)]

        ld = sem("ld"); gio = sem("gio")
        vv = sem("vv"); sv = sem("sv"); tv = sem("tv")
        mmv = sem("mmv"); cons = sem("cons"); dst = sem("dst"); dxp = sem("dxp")
        block = st.enter_context(nc.Block())

        n = dict(vv=0, sv=0, tv=0, mm=0, cons=0, dst=0, dxp=0, ld=0)

        def g_iota(e):
            e.iota(iota_p[:], pattern=[[0, 1]], base=0, channel_multiplier=1)
            e.iota(iota_f[:], pattern=[[1, 128]], base=0, channel_multiplier=0).then_inc(gio, 1)
        prog.on("gpsimd", g_iota)

        n["vv"] += 1
        def v_init(e):
            e.wait_ge(gio, 1)
            e.tensor_copy(out=fidp[:], in_=iota_p[:])
            e.tensor_copy(out=fidf[:], in_=iota_f[:])
            e.tensor_scalar(out=identb[:], in0=fidf[:], scalar1=fidp[:], scalar2=None,
                            op0=ALU.is_equal)
            e.tensor_scalar(out=identf[:], in0=fidf[:], scalar1=fidp[:], scalar2=None,
                            op0=ALU.is_equal)
            e.memset(onesf[:], 1.0)
            e.memset(SC[0][:, :], 0.0)
            e.memset(SC[1][:, :], 0.0).then_inc(vv, 1)
        prog.on("vector", v_init)
        init_v = n["vv"]

        def s_loads(e):
            e.dma_start(out=dtv[:, :], in_=i_dt).then_inc(ld, 16)
            e.dma_start(out=alog[:, :], in_=i_alog).then_inc(ld, 16)
            e.dma_start(out=bc_sb[:, :], in_=i_bc[0:128, :]).then_inc(ld, 16)
        prog.on("sync", s_loads)
        n["ld"] += 48

        n["sv"] += 1
        def s_A(e):
            e.wait_ge(ld, 48)
            e.activation(out=acol[:, :], in_=alog[:, :], func=AF.Exp, scale=1.0).then_inc(sv, 1)
        prog.on("scalar", s_A)

        n["vv"] += 3
        def v_dA(e, w0=n["vv"]):
            e.wait_ge(sv, 1)
            e.tensor_scalar(out=dA[:, :], in0=dtv[:, :], scalar1=acol[:, :], scalar2=-1.0,
                            op0=ALU.mult, op1=ALU.mult)
            for u in range(16):
                c0 = u * CK
                e.tensor_tensor_scan(out=acum[:, c0:c0 + CK], data0=dA[:, c0:c0 + CK],
                                     data1=dA[:, c0:c0 + CK], initial=0.0,
                                     op0=ALU.add, op1=ALU.bypass)
            e.reduce_sum(out=dsum[:, 0:1], in_=dA[:, 0:SPC], axis=AX.X).then_inc(vv, 1)
            e.reduce_sum(out=dsum[:, 1:2], in_=dA[:, SPC:TPC], axis=AX.X).then_inc(vv, 1)
            e.wait_ge(vv, w0 - 1)
            e.tensor_copy(out=dsum[0:1, 0:1], in_=dsum[0:1, 0:1]).then_inc(vv, 1)
        prog.on("vector", v_dA)
        acum_v = n["vv"]

        n["sv"] += 1
        def s_dspan(e, w=acum_v):
            e.wait_ge(vv, w)
            e.activation(out=dspan[:, :], in_=dsum[:, :], func=AF.Exp).then_inc(sv, 1)
        prog.on("scalar", s_dspan)
        n["dst"] += 16
        def f_dspan_dma(e, w=n["sv"]):
            e.wait_ge(sv, w)
            e.dma_start(out=o_dspan[:, 0:2], in_=dspan[:, :]).then_inc(dst, 16)
        prog.on("sync", f_dspan_dma)

        trdone = {}
        dxp_after = {}
        sj_after = {}
        sev_after = []
        pcd_free = [n["sv"]]

        for u in range(16):
            b, j = u // NCKB, u % NCKB
            c0 = b * SPC + j * CK
            buf = u % 2

            need_tv = trdone.get(u - 2, 0)
            n["ld"] += 16
            def f_xld(e, c0=c0, buf=buf, need_tv=need_tv):
                e.wait_ge(tv, need_tv)
                e.dma_start(out=xsch[buf][:, :].rearrange("p (pb s) -> p pb s", pb=32),
                            in_=i_xs.rearrange("(pb p) t -> p pb t", p=128)[:, :, c0:c0 + CK]
                            ).then_inc(ld, 16)
            prog.on("sync", f_xld)
            xldv = n["ld"]

            n["sv"] += 1
            def f_dteh(e, c0=c0, w=acum_v):
                e.wait_ge(vv, w)
                e.activation(out=dteh[:, :], in_=acum[:, c0:c0 + CK], func=AF.Exp,
                             scale=-1.0, bias=acum[:, c0 + CK - 1:c0 + CK]).then_inc(sv, 1)
            prog.on("scalar", f_dteh)
            dteh_v = n["sv"]

            n["vv"] += 1
            def f_dteF(e, c0=c0, w=dteh_v):
                e.wait_ge(sv, w)
                e.tensor_tensor(out=dteF[:, :], in0=dteh[:, :], in1=dtv[:, c0:c0 + CK],
                                op=ALU.mult).then_inc(vv, 1)
            prog.on("vector", f_dteF)
            dteF_v = n["vv"]

            n["tv"] += 1
            def f_dtetr(e, w=dteF_v, wfree=pcd_free[0]):
                e.wait_ge(vv, w)
                e.wait_ge(sv, wfree)
                e.transpose(pcd[0:128, 0:NH], dteF[:, :], identf[0:NH, 0:NH]).then_inc(tv, 1)
            prog.on("tensor", f_dtetr)
            dtetr_v = n["tv"]

            n["sv"] += 1
            def f_dteT_ev(e, w=dtetr_v):
                e.wait_ge(tv, w)
                e.copy(out=dteT[:, :], in_=pcd[0:128, 0:NH]).then_inc(sv, 1)
            prog.on("scalar", f_dteT_ev)
            dteT_v = n["sv"]
            pcd_free[0] = dteT_v

            need_dxp = dxp_after.get(u - 2, 0)
            for pairi in range(16):
                evg = n["vv"]
                n["tv"] += 2
                def f_tr(e, buf=buf, pairi=pairi, xldv=xldv, evg=evg, need_dxp=need_dxp):
                    e.wait_ge(ld, xldv)
                    e.wait_ge(vv, evg)
                    e.wait_ge(dxp, need_dxp)
                    last = None
                    for q in range(2):
                        pb = pairi * 2 + q
                        last = e.transpose(ptb[q][0:128, 0:128],
                                           xsch[buf][:, pb * 128:(pb + 1) * 128], identb[:])
                    last.then_inc(tv, 2)
                prog.on("tensor", f_tr)
                trv = n["tv"]
                n["vv"] += 1
                def f_ev(e, buf=buf, pairi=pairi, trv=trv):
                    e.wait_ge(tv, trv)
                    last = None
                    for q in range(2):
                        pb = pairi * 2 + q
                        last = e.tensor_copy(out=xp_sb[buf][:, pb * 128:(pb + 1) * 128],
                                             in_=ptb[q][0:128, 0:128])
                    last.then_inc(vv, 1)
                prog.on("vector", f_ev)
            trdone[u] = n["tv"]
            xptr_v = n["vv"]

            n["dxp"] += 16
            def f_xpdma(e, u=u, buf=buf, w=xptr_v):
                e.wait_ge(vv, w)
                e.dma_start(out=o_xp[u, :, :], in_=xp_sb[buf][:, :]).then_inc(dxp, 16)
            prog.on("sync", f_xpdma)
            dxp_after[u] = n["dxp"]

            need_mm = 4 * u
            n["vv"] += 1
            def f_xpp(e, buf=buf, w2=dteT_v, need_mm=need_mm):
                e.wait_ge(sv, w2)
                e.wait_ge(mmv, need_mm)
                e.tensor_tensor(out=xpp_sb[:, :].rearrange("s (h p) -> s h p", h=NH),
                                in0=xp_sb[buf][:, :].rearrange("s (h p) -> s h p", h=NH),
                                in1=dteT[:, :].unsqueeze(2).to_broadcast((128, NH, HPD)),
                                op=ALU.mult).then_inc(vv, 1)
            prog.on("vector", f_xpp)
            xpp_v = n["vv"]

            n["tv"] += 1
            def f_btr(e, c0=c0, evg=xptr_v):
                e.wait_ge(vv, evg)
                e.transpose(ptb[0][0:128, 0:128], bc_sb[:, c0:c0 + CK], identb[:]).then_inc(tv, 1)
            prog.on("tensor", f_btr)
            btr_v = n["tv"]
            n["vv"] += 1
            def f_btev(e, w=btr_v, need_mm=need_mm):
                e.wait_ge(tv, w)
                e.wait_ge(mmv, need_mm)
                e.tensor_copy(out=bt_sb[:, :], in_=ptb[0][0:128, 0:128]).then_inc(vv, 1)
            prog.on("vector", f_btev)
            bt_v = n["vv"]

            n["tv"] += 1
            def f_cdtr(e, c0=c0, wfree=pcd_free[0], wa=acum_v):
                e.wait_ge(vv, wa)
                e.wait_ge(sv, wfree)
                e.transpose(pcd[0:1, 0:NH], acum[:, c0 + CK - 1:c0 + CK],
                            identf[0:NH, 0:NH]).then_inc(tv, 1)
            prog.on("tensor", f_cdtr)
            cdtr_v = n["tv"]
            n["sv"] += 1
            def f_cdrow(e, w=cdtr_v):
                e.wait_ge(tv, w)
                e.copy(out=cdrow[:, :], in_=pcd[0:1, 0:NH]).then_inc(sv, 1)
            prog.on("scalar", f_cdrow)
            cdrow_v = n["sv"]
            pcd_free[0] = cdrow_v

            n["tv"] += 1
            def f_cdmm(e, w=cdrow_v, wfree=pcd_free[0]):
                e.wait_ge(sv, wfree)
                e.matmul(pcd[0:128, 0:NH], onesf[:, :], cdrow[:, :], start=True,
                         stop=True).then_inc(tv, 1)
            prog.on("tensor", f_cdmm)
            cdmm_v = n["tv"]
            n["sv"] += 1
            need_scans = n["cons"]
            def f_cdB(e, w=cdmm_v, need_scans=need_scans):
                e.wait_ge(tv, w)
                e.wait_ge(cons, need_scans)
                e.activation(out=cdB[:, :], in_=pcd[0:128, 0:NH], func=AF.Exp).then_inc(sv, 1)
            prog.on("scalar", f_cdB)
            cdB_v = n["sv"]
            pcd_free[0] = cdB_v

            need_sj = sj_after.get(u - 2, 0)
            for hg in range(4):
                g = u * 4 + hg
                pz = pst[g % 2]
                n["mm"] += 1
                def f_smm(e, hg=hg, pz=pz, w1=bt_v, w2=xpp_v, g=g,
                          need_cons=max(0, g - 1),
                          need_sev=(sev_after[g - 2] if g >= 2 else 0)):
                    e.wait_ge(vv, max(w1, w2))
                    e.wait_ge(cons, need_cons)
                    e.wait_ge(sv, need_sev)
                    last = None
                    for hh in range(16):
                        h = hg * 16 + hh
                        last = e.matmul(pz[0:128, hh * 64:(hh + 1) * 64], bt_sb[:, :],
                                        xpp_sb[:, h * 64:(h + 1) * 64], start=True, stop=True)
                    last.then_inc(mmv, 1)
                prog.on("tensor", f_smm)
                smm_v = n["mm"]

                n["sv"] += 1
                def f_sev(e, hg=hg, pz=pz, buf=buf, w=smm_v, need_sj=need_sj):
                    e.wait_ge(mmv, w)
                    e.wait_ge(dst, need_sj)
                    e.copy(out=ssb[buf][:, hg * 1024:(hg + 1) * 1024], in_=pz[:, :]).then_inc(sv, 1)
                prog.on("scalar", f_sev)
                sev_after.append(n["sv"])

                n["vv"] += 1
                n["cons"] += 1
                def f_scan(e, b=b, hg=hg, pz=pz, w=smm_v, wcd=cdB_v):
                    e.wait_ge(mmv, w)
                    e.wait_ge(sv, wcd)
                    e.tensor_tensor(out=tmp_sb[:, :].rearrange("n (h p) -> n h p", h=16),
                                    in0=SC[b][:, hg * 1024:(hg + 1) * 1024].rearrange(
                                        "n (h p) -> n h p", h=16),
                                    in1=cdB[:, hg * 16:(hg + 1) * 16].unsqueeze(2)
                                    .to_broadcast((128, 16, HPD)),
                                    op=ALU.mult).then_inc(vv, 1)
                    e.tensor_tensor(out=SC[b][:, hg * 1024:(hg + 1) * 1024], in0=tmp_sb[:, :],
                                    in1=pz[:, :], op=ALU.add).then_inc(cons, 1)
                prog.on("vector", f_scan)

            n["dst"] += 16
            def f_sjdma(e, u=u, buf=buf, w=n["sv"]):
                e.wait_ge(sv, w)
                e.dma_start(out=o_sj[u, :, :], in_=ssb[buf][:, :]).then_inc(dst, 16)
            prog.on("sync", f_sjdma)
            sj_after[u] = n["dst"]

        for b in range(BATCH):
            n["dst"] += 16
            def f_span(e, b=b, w=n["cons"]):
                e.wait_ge(cons, w)
                e.dma_start(out=o_span[b * 128:(b + 1) * 128, :], in_=SC[b][:, :]).then_inc(dst, 16)
            prog.on("sync", f_span)

        def f_fin(e, a=n["dst"], bx=n["dxp"]):
            e.wait_ge(dst, a)
            e.wait_ge(dxp, bx)
        prog.on("sync", f_fin)
        prog.emit(block)
    return nc


def phase3(nc, io):
    i_xnT = io["d_xnT"]; i_xnTlo = io["d_xnTlo"]
    i_wg = io["w_gate"]; i_wi0 = io["wi_0"]; i_wi1 = io["wi_1"]; i_wo = io["wo"]
    o_g = io["d_g"]; o_dbg = io["d_dbg"]
    wi0b = io["wi0b_h"]; wi1b = io["wi1b_h"]; wob = io["wob_h"]

    prog = Prog(); PHX[0] += 1
    with ExitStack() as st:
        _ctr = [0]
        def sb(shape, dt=F32):
            _ctr[0] += 1
            return st.enter_context(nc.sbuf_tensor("sb%d_%d" % (PHX[0], _ctr[0]), list(shape), dt))
        def ps(shape, dt=F32):
            _ctr[0] += 1
            return st.enter_context(nc.psum_tensor("ps%d_%d" % (PHX[0], _ctr[0]), list(shape), dt))
        def sem(name):
            return st.enter_context(nc.semaphore("s%d_%s" % (PHX[0], name)))

        xnT = [sb((128, T2), BF16) for _ in range(16)]
        lo_sb = [sb((128, DM), BF16) for _ in range(2)]
        wgf = sb((128, 128))
        wghi = sb((128, 128), BF16)
        wglo = sb((128, 128), BF16)
        lg = sb((128, 8))
        m1 = sb((128, 1)); nm1 = sb((128, 1)); m2 = sb((128, 1))
        ge1 = sb((128, 8)); lmask = sb((128, 8)); usel = sb((128, 8))
        e2c = sb((128, 1)); denc = sb((128, 1)); recipc = sb((128, 1))
        wall = sb((128, 128))
        w0buf = [sb((128, 16 * 128), BF16) for _ in range(2)]
        w1buf = [sb((128, 16 * 128), BF16) for _ in range(2)]
        wobuf = [sb((128, 8 * 512), BF16) for _ in range(2)]
        sil = sb((128, 512), BF16)
        inter = sb((128, 8 * 1024), BF16)
        g_acc = sb((128, 8 * 2048))
        gst = [sb((128, DM), BF16) for _ in range(2)]
        plog = ps((128, 8))
        ph0 = ps((128, 512))
        ph1 = ps((128, 512))
        pout = [ps((128, 512)) for _ in range(2)]

        gv = sem("gv"); ld = sem("ld"); llo = sem("llo")
        vv = sem("vv"); sv = sem("sv"); mmv = sem("mmv")
        cons = sem("cons"); sps = sem("sps"); dgs = sem("dgs")
        lwi = sem("lwi"); lwo = sem("lwo")
        agS = sem("agS"); agD = sem("agD")
        block = st.enter_context(nc.Block())

        n = dict(gv=0, ld=0, llo=0, vv=0, sv=0, mm=0, cons=0, sps=0, dgs=0, lwi=0, lwo=0)

        def g_ag(e):
            e.collective_compute(
                "AllGather", ALU.bypass, replica_groups=RG8,
                ins=[io["spn_loc"].opt()], outs=[io["spn_all"].opt()]).then_inc(agS)
            e.collective_compute(
                "AllGather", ALU.bypass, replica_groups=RG8,
                ins=[io["dsp_loc"].opt()], outs=[io["dsp_all"].opt()]).then_inc(agD)
        prog.on("gpsimd", g_ag)

        def g_cvt(e):
            e.dma_start(out=wi0b[:, :, :], in_=i_wi0).then_inc(gv, 16)
            e.dma_start(out=wi1b[:, :, :], in_=i_wi1).then_inc(gv, 16)
            e.dma_start(out=wob[:, :, :], in_=i_wo).then_inc(gv, 16)
        prog.on("gpsimd", g_cvt)
        n["gv"] = 48

        def s_loads(e):
            for k in range(16):
                e.dma_start(out=xnT[k][:, :], in_=i_xnT[k * 128:(k + 1) * 128, :]).then_inc(ld, 16)
            e.dma_start(out=wgf[:, :].rearrange("d (k e8) -> d k e8", e8=8),
                        in_=i_wg.rearrange("(k d) e8 -> d k e8", d=128)).then_inc(ld, 16)
        prog.on("sync", s_loads)
        n["ld"] = 17 * 16

        n["vv"] += 1
        def v_wg(e):
            e.wait_ge(ld, 17 * 16)
            e.tensor_copy(out=wghi[:, :], in_=wgf[:, :])
            e.scalar_tensor_tensor(out=wglo[:, :], in0=wghi[:, :], scalar=-1.0, in1=wgf[:, :],
                                   op0=ALU.mult, op1=ALU.add).then_inc(vv, 1)
        prog.on("vector", v_wg)
        wg_v = n["vv"]

        def tcol_of(tti):
            b, tt = tti // 8, tti % 8
            return b * T2B + HALO + tt * 128

        for tti in range(16):
            tcol = tcol_of(tti)
            buf = tti % 2
            n["llo"] += 16
            def f_lolod(e, tcol=tcol, buf=buf, need_mm=max(0, n["mm"] - 1)):
                e.wait_ge(mmv, need_mm)
                e.dma_start(out=lo_sb[buf][:, :].rearrange("d (k t) -> d k t", k=16),
                            in_=i_xnTlo.rearrange("(k d) t -> d k t", d=128)[:, :, tcol:tcol + 128]
                            ).then_inc(llo, 16)
            prog.on("sync", f_lolod)
            lov = n["llo"]

            n["mm"] += 1
            def f_lgmm(e, tcol=tcol, buf=buf, lov=lov, wgv=wg_v, need_t2=n["vv"]):
                e.wait_ge(llo, lov)
                e.wait_ge(ld, 17 * 16)
                e.wait_ge(vv, need_t2)
                combos = []
                for k in range(16):
                    combos += [(xnT[k][:, tcol:tcol + 128], wghi[:, k * 8:(k + 1) * 8]),
                               (xnT[k][:, tcol:tcol + 128], wglo[:, k * 8:(k + 1) * 8]),
                               (lo_sb[buf][:, k * 128:(k + 1) * 128], wghi[:, k * 8:(k + 1) * 8]),
                               (lo_sb[buf][:, k * 128:(k + 1) * 128], wglo[:, k * 8:(k + 1) * 8])]
                last = None
                for ci, (stat, mov) in enumerate(combos):
                    last = e.matmul(plog[:, :], stat, mov, start=(ci == 0),
                                    stop=(ci == len(combos) - 1))
                last.then_inc(mmv, 1)
            prog.on("tensor", f_lgmm)
            lgv = n["mm"]

            base = n["vv"]
            n["vv"] += 6
            def f_top2a(e, lgv=lgv, base=base):
                e.wait_ge(mmv, lgv)
                e.tensor_copy(out=lg[:, :], in_=plog[:, :]).then_inc(vv, 1)
                e.wait_ge(vv, base + 1)
                e.reduce_max(out=m1[:, :], in_=lg[:, :], axis=AX.X).then_inc(vv, 1)
                e.wait_ge(vv, base + 2)
                e.tensor_scalar(out=ge1[:, :], in0=lg[:, :], scalar1=m1[:, :], scalar2=None,
                                op0=ALU.is_ge)
                e.tensor_scalar(out=nm1[:, :], in0=m1[:, :], scalar1=-1.0, scalar2=None,
                                op0=ALU.mult).then_inc(vv, 1)
                e.wait_ge(vv, base + 3)
                e.scalar_tensor_tensor(out=lmask[:, :], in0=ge1[:, :], scalar=-1e30, in1=lg[:, :],
                                       op0=ALU.mult, op1=ALU.add).then_inc(vv, 1)
                e.wait_ge(vv, base + 4)
                e.reduce_max(out=m2[:, :], in_=lmask[:, :], axis=AX.X).then_inc(vv, 1)
                e.wait_ge(vv, base + 5)
                e.tensor_copy(out=m2[0:1, 0:1], in_=m2[0:1, 0:1]).then_inc(vv, 1)
            prog.on("vector", f_top2a)
            t2a = n["vv"]

            n["sv"] += 1
            def f_top2s(e, w=t2a):
                e.wait_ge(vv, w)
                e.activation(out=usel[:, :], in_=lg[:, :], func=AF.Exp, scale=1.0, bias=nm1[:, :])
                e.activation(out=e2c[:, :], in_=m2[:, :], func=AF.Exp, scale=1.0,
                             bias=nm1[:, :]).then_inc(sv, 1)
            prog.on("scalar", f_top2s)
            t2s = n["sv"]

            base2 = n["vv"]
            n["vv"] += 5
            def f_top2b(e, tti=tti, w=t2s, base2=base2):
                e.wait_ge(sv, w)
                e.tensor_scalar(out=denc[:, :], in0=e2c[:, :], scalar1=1.0, scalar2=None,
                                op0=ALU.add).then_inc(vv, 1)
                e.wait_ge(vv, base2 + 1)
                e.reciprocal(out=recipc[:, :], in_=denc[:, :]).then_inc(vv, 1)
                e.wait_ge(vv, base2 + 2)
                e.tensor_scalar(out=lmask[:, :], in0=lg[:, :], scalar1=m2[:, :], scalar2=None,
                                op0=ALU.is_ge).then_inc(vv, 1)
                e.wait_ge(vv, base2 + 3)
                e.tensor_tensor(out=usel[:, :], in0=usel[:, :], in1=lmask[:, :],
                                op=ALU.mult).then_inc(vv, 1)
                e.wait_ge(vv, base2 + 4)
                e.tensor_scalar(out=wall[:, tti * 8:(tti + 1) * 8], in0=usel[:, :],
                                scalar1=recipc[:, :], scalar2=None, op0=ALU.mult).then_inc(vv, 1)
            prog.on("vector", f_top2b)
        top2_done_v = n["vv"]
        top2_done_mm = n["mm"]

        gdma_after_half = {}
        silu_after_half = {}
        sps_hist = []
        cons_hist = []
        ga_hist = []
        wo_used = []
        hm_after = {}
        out_done = {0: top2_done_mm}

        for half in range(2):
            for ei in range(8):
                for ib in range(8):
                    ibu = (half * 8 + ei) * 8 + ib
                    wb = ibu % 2
                    n["lwi"] += 32
                    def f_wil(e, ei=ei, ib=ib, wb=wb, need_mm=hm_after.get(ibu - 2, 0)):
                        e.wait_ge(gv, 32)
                        e.wait_ge(mmv, need_mm)
                        e.dma_start(out=w0buf[wb][:, :].rearrange("d (k i) -> d k i", k=16),
                                    in_=wi0b.ap().rearrange("e8 (k d) i -> e8 d k i", d=128)[
                                        ei, :, :, ib * 128:(ib + 1) * 128]).then_inc(lwi, 16)
                        e.dma_start(out=w1buf[wb][:, :].rearrange("d (k i) -> d k i", k=16),
                                    in_=wi1b.ap().rearrange("e8 (k d) i -> e8 d k i", d=128)[
                                        ei, :, :, ib * 128:(ib + 1) * 128]).then_inc(lwi, 16)
                    prog.on("sync", f_wil)
                    wiv = n["lwi"]

                    for tg in range(2):
                        tc0 = half * T2B * 0 + (half * T2B) + HALO + tg * 512
                        n["mm"] += 1
                        def f_h0(e, ib=ib, wb=wb, tc0=tc0, wiv=wiv,
                                 need_sps=(sps_hist[-1] if sps_hist else 0),
                                 need_cons=(cons_hist[-1] if cons_hist else 0),
                                 w2v=top2_done_v):
                            e.wait_ge(lwi, wiv)
                            e.wait_ge(vv, w2v)
                            e.wait_ge(sps, need_sps)
                            e.wait_ge(cons, need_cons)
                            last = None
                            for k in range(16):
                                last = e.matmul(ph0[:, :], w0buf[wb][:, k * 128:(k + 1) * 128],
                                                xnT[k][:, tc0:tc0 + 512], start=(k == 0), stop=(k == 15))
                            for k in range(16):
                                last = e.matmul(ph1[:, :], w1buf[wb][:, k * 128:(k + 1) * 128],
                                                xnT[k][:, tc0:tc0 + 512], start=(k == 0), stop=(k == 15))
                            last.then_inc(mmv, 1)
                        prog.on("tensor", f_h0)
                        hv = n["mm"]

                        n["sps"] += 1
                        def f_sil(e, w=hv):
                            e.wait_ge(mmv, w)
                            e.activation(out=sil[:, :], in_=ph0[:, :], func=AF.Silu).then_inc(sps, 1)
                        prog.on("scalar", f_sil)
                        sps_hist.append(n["sps"])

                        n["cons"] += 1
                        def f_int(e, ib=ib, tg=tg, w=hv, ws=n["sps"],
                                  need_out=out_done.get(half * 8 + ei - 1, 0) if True else 0):
                            e.wait_ge(mmv, max(w, need_out))
                            e.wait_ge(sps, ws)
                            e.tensor_tensor(out=inter[:, ib * 1024 + tg * 512: ib * 1024 + (tg + 1) * 512],
                                            in0=sil[:, :], in1=ph1[:, :], op=ALU.mult).then_inc(cons, 1)
                        prog.on("vector", f_int)
                        cons_hist.append(n["cons"])
                    hm_after[ibu] = n["mm"]

                inter_done = n["cons"]
                for dg in range(4):
                    dgu = ((half * 8 + ei) * 4 + dg)
                    wb = dgu % 2
                    n["lwo"] += 16
                    def f_wol(e, ei=ei, dg=dg, wb=wb,
                              need_mm=(wo_used[dgu - 2] if dgu >= 2 else 0)):
                        e.wait_ge(gv, 48)
                        e.wait_ge(mmv, need_mm)
                        e.dma_start(out=wobuf[wb][:, :].rearrange("i (ib d) -> i ib d", ib=8),
                                    in_=wob.ap().rearrange("e8 (ib i) d -> e8 i ib d", i=128)[
                                        ei, :, :, dg * 512:(dg + 1) * 512]).then_inc(lwo, 16)
                    prog.on("sync", f_wol)
                    wov = n["lwo"]

                    for tt in range(8):
                        pz = pout[(dgu * 8 + tt) % 2]
                        n["mm"] += 1
                        def f_omm(e, tt=tt, wb=wb, pz=pz, wov=wov, wint=inter_done,
                                  need_ga=(ga_hist[-2] if len(ga_hist) >= 2 else 0)):
                            e.wait_ge(lwo, wov)
                            e.wait_ge(cons, wint)
                            e.wait_ge(vv, need_ga)
                            last = None
                            for ib in range(8):
                                last = e.matmul(pz[:, :], inter[:, ib * 1024 + tt * 128: ib * 1024 + tt * 128 + 128],
                                                wobuf[wb][:, ib * 512:(ib + 1) * 512],
                                                start=(ib == 0), stop=(ib == 7))
                            last.then_inc(mmv, 1)
                        prog.on("tensor", f_omm)
                        ov = n["mm"]

                        n["vv"] += 1
                        def f_ga(e, half=half, ei=ei, dg=dg, tt=tt, pz=pz, w=ov,
                                 need_dg=gdma_after_half.get(half - 1, 0)):
                            e.wait_ge(mmv, w)
                            e.wait_ge(dgs, need_dg)
                            tti = half * 8 + tt
                            wcol = wall[:, tti * 8 + ei: tti * 8 + ei + 1]
                            tgt = g_acc[:, tt * 2048 + dg * 512: tt * 2048 + (dg + 1) * 512]
                            if ei == 0:
                                e.tensor_scalar(out=tgt, in0=pz[:, :], scalar1=wcol, scalar2=None,
                                                op0=ALU.mult).then_inc(vv, 1)
                            else:
                                e.scalar_tensor_tensor(out=tgt, in0=pz[:, :], scalar=wcol, in1=tgt,
                                                       op0=ALU.mult, op1=ALU.add).then_inc(vv, 1)
                        prog.on("vector", f_ga)
                        ga_hist.append(n["vv"])
                    wo_used.append(n["mm"])
                out_done[half * 8 + ei] = n["mm"]

            ga_done = n["vv"]
            for tt in range(8):
                gb = tt % 2
                n["sv"] += 1
                def f_gsil(e, tt=tt, gb=gb, w=ga_done, need_dg=max(0, n["dgs"] - 16)):
                    e.wait_ge(vv, w)
                    e.wait_ge(dgs, need_dg)
                    e.activation(out=gst[gb][:, :], in_=g_acc[:, tt * 2048:(tt + 1) * 2048],
                                 func=AF.Silu).then_inc(sv, 1)
                prog.on("scalar", f_gsil)
                gsv = n["sv"]
                n["dgs"] += 16
                def f_gdma(e, half=half, tt=tt, gb=gb, w=gsv):
                    e.wait_ge(sv, w)
                    e.dma_start(out=o_g[(half * 8 + tt) * 128:(half * 8 + tt + 1) * 128, :],
                                in_=gst[gb][:, :]).then_inc(dgs, 16)
                prog.on("sync", f_gdma)
            gdma_after_half[half] = n["dgs"]

        def f_dbg(e, w=top2_done_v):
            e.wait_ge(vv, w)
            e.dma_start(out=o_dbg, in_=wall[:, :]).then_inc(dgs, 16)
        prog.on("sync", f_dbg)
        n["dgs"] += 16

        def f_fin(e, a=n["dgs"]):
            e.wait_ge(dgs, a)
        prog.on("sync", f_fin)

        def g_agw(e):
            e.wait_ge(agS, 1)
            e.wait_ge(agD, 1)
        prog.on("gpsimd", g_agw)
        prog.emit(block)
    return nc


def phase4a(nc, io):
    i_xp = io["d_xp"]; i_sj = io["d_sj"]; i_xs = io["d_xs"]; i_bc = io["d_bc"]
    i_sg = io["d_sg"]; i_dt = io["d_dt"]; i_alog = io["alog"]
    i_sall = io["spn_all"]; i_dall = io["dsp_all"]
    i_rm = io["rmask16"]; i_im = io["invmask16"]; i_drep = io["Drep"]
    i_nw = io["normw"]; i_mneg = io["masknegT"]; i_cv = io["cvec"]
    o_yn = io["d_yn"]; acsp = io["acsp_h"]

    prog = Prog(); PHX[0] += 1
    with ExitStack() as st:
        _ctr = [0]
        def sb(shape, dt=F32):
            _ctr[0] += 1
            return st.enter_context(nc.sbuf_tensor("sb%d_%d" % (PHX[0], _ctr[0]), list(shape), dt))
        def ps(shape, dt=F32):
            _ctr[0] += 1
            return st.enter_context(nc.psum_tensor("ps%d_%d" % (PHX[0], _ctr[0]), list(shape), dt))
        def sem(name):
            return st.enter_context(nc.semaphore("s%d_%s" % (PHX[0], name)))

        dtv = sb((NH, TPC))
        alog = sb((NH, 1))
        acum = sb((NH, TPC))
        rsplit = sb((NH, TPC))
        lndt = rsplit
        splA = sb((NH, TPC), BF16)
        splB = sb((NH, TPC), BF16)
        dall = sb((NH, 16)); rm16 = sb((NH, 16)); im16 = sb((NH, 16))
        dprm = sb((NH, 16)); qsuf = sb((NH, 16)); qm = sb((NH, 16))
        qrow = sb((1, NH))
        cdrow = sb((1, NH)); cdB = sb((128, NH))
        lndtT = sb((CK, NH))
        drep = sb((128, 32)); nw = sb((128, 32)); cv = sb((128, 8))
        mnegf = sb((128, 128)); mnegb = sb((128, 128), BF16)
        ones3 = sb((3, 128), BF16)
        onesf = sb((1, 128))
        onescol = sb((128, 1), BF16)
        identb = sb((128, 128), BF16)
        identf = sb((128, 128), F32)
        iota_p = sb((128, 1), mybir.dt.int32); iota_f = sb((128, 128), mybir.dt.int32)
        fidp = sb((128, 1)); fidf = sb((128, 128))
        b_sb = sb((128, TPC), BF16); c_sb = sb((128, TPC), BF16)
        SC = [sb((128, IDIM)) for _ in range(BATCH)]
        prevb = sb((128, IDIM), BF16)
        sld = sb((128, IDIM))
        tmp_sb = sb((128, IDIM))
        xpch = sb((128, IDIM), BF16)
        sjch = sb((128, IDIM))
        xsch = sb((128, IDIM), BF16)
        sgch = sb((128, IDIM), BF16)
        stat6 = sb((6, 16 * CK), BF16)
        mov6 = sb((6, 16 * CK), BF16)
        etop = sb((128, 512), BF16)
        ebot = sb((128, 512), BF16)
        mtop = sb((128, 512), BF16)
        mbot = sb((128, 512), BF16)
        cbt = sb((128, CK), BF16)
        ych = sb((128, IDIM))
        ysq = sb((128, IDIM), BF16)
        ynst = ysq
        stdrow = sb((1, CK)); rstdrow = sb((1, CK))
        pcd = ps((128, 128))
        pcb = ps((128, 128))
        pd = ps((128, 512))
        pr = ps((128, 512))
        pY = [ps((128, 512)) for _ in range(2)]
        prms = ps((1, CK))
        prstdB = ps((128, CK))

        ld = sem("ld"); gio = sem("gio")
        vv = sem("vv"); sv = sem("sv"); tv = sem("tv"); mmv = sem("mmv")
        dyn = sem("dyn"); fin = sem("fin")
        block = st.enter_context(nc.Block())

        n = dict(ld=0, vv=0, sv=0, tv=0, mm=0, dyn=0, fin=0)

        def vstep(e, instr):
            n["vv"] += 1
        def g_iota(e):
            e.iota(iota_p[:], pattern=[[0, 1]], base=0, channel_multiplier=1)
            e.iota(iota_f[:], pattern=[[1, 128]], base=0, channel_multiplier=0).then_inc(gio, 1)
        prog.on("gpsimd", g_iota)

        def s_loads(e):
            for t, s in [(dtv[:, :], i_dt), (alog[:, :], i_alog),
                         (rm16[:, :], i_rm), (im16[:, :], i_im), (drep[:, :], i_drep),
                         (nw[:, :], i_nw), (mnegf[:, :], i_mneg),
                         (b_sb[:, :], i_bc[0:128, :]), (c_sb[:, :], i_bc[128:256, :])]:
                e.dma_start(out=t, in_=s).then_inc(ld, 16)
            for j in range(NCORES):
                e.dma_start(out=dall[:, 2 * j:2 * j + 2],
                            in_=i_dall[j * NH:(j + 1) * NH, 0:2]).then_inc(ld, 16)
            e.dma_start(out=cv[:, :], in_=i_cv.to_broadcast((128, 8))).then_inc(ld, 16)
        prog.on("sync", s_loads)
        n["ld"] = 18 * 16
        ld0 = n["ld"]

        n["vv"] += 1
        def v_init(e):
            e.wait_ge(gio, 1)
            e.wait_ge(ld, ld0)
            e.tensor_copy(out=fidp[:], in_=iota_p[:])
            e.tensor_copy(out=fidf[:], in_=iota_f[:])
            e.tensor_scalar(out=identb[:], in0=fidf[:], scalar1=fidp[:], scalar2=None, op0=ALU.is_equal)
            e.tensor_scalar(out=identf[:], in0=fidf[:], scalar1=fidp[:], scalar2=None, op0=ALU.is_equal)
            e.memset(ones3[:, :], 1.0)
            e.memset(onesf[:, :], 1.0)
            e.memset(onescol[:, :], 1.0)
            e.memset(stat6[:, :], 1.0)
            e.memset(mov6[:, :], 1.0)
            e.tensor_copy(out=mnegb[:, :], in_=mnegf[:, :]).then_inc(vv, 1)
        prog.on("vector", v_init)
        init_v = n["vv"]

        n["sv"] += 1
        def s_A(e):
            e.wait_ge(ld, ld0)
            e.activation(out=alog[:, :], in_=alog[:, :], func=AF.Exp, scale=1.0).then_inc(sv, 1)
        prog.on("scalar", s_A)

        bse = n["vv"]
        n["vv"] += 2
        def v_acum(e, bse=bse):
            e.wait_ge(sv, 1)
            e.tensor_scalar(out=rsplit[:, :], in0=dtv[:, :], scalar1=alog[:, :], scalar2=-1.0,
                            op0=ALU.mult, op1=ALU.mult).then_inc(vv, 1)
            e.wait_ge(vv, bse + 1)
            for u in range(16):
                c0 = u * CK
                e.tensor_tensor_scan(out=acum[:, c0:c0 + CK], data0=rsplit[:, c0:c0 + CK],
                                     data1=rsplit[:, c0:c0 + CK], initial=0.0,
                                     op0=ALU.add, op1=ALU.bypass)
            e.tensor_copy(out=rsplit[:, :], in_=acum[:, :]).then_inc(vv, 1)
        prog.on("vector", v_acum)
        for q in range(3):
            bq = n["vv"]
            n["vv"] += 3
            need_ld = n["ld"]
            def v_split(e, bq=bq, need_ld=need_ld):
                e.wait_ge(vv, bq)
                e.wait_ge(ld, need_ld)
                e.tensor_copy(out=splA[:, :], in_=rsplit[:, :]).then_inc(vv, 1)
                e.wait_ge(vv, bq + 1)
                e.tensor_scalar(out=splB[:, :], in0=splA[:, :], scalar1=-1.0, scalar2=None,
                                op0=ALU.mult)
                e.scalar_tensor_tensor(out=rsplit[:, :], in0=splA[:, :], scalar=-1.0,
                                       in1=rsplit[:, :], op0=ALU.mult,
                                       op1=ALU.add).then_inc(vv, 1)
                e.wait_ge(vv, bq + 2)
                e.tensor_copy(out=splB[0:1, 0:1], in_=splB[0:1, 0:1]).then_inc(vv, 1)
            prog.on("vector", v_split)
            spq = n["vv"]
            n["ld"] += 32
            def f_spdma(e, q=q, w=spq):
                e.wait_ge(vv, w)
                e.dma_start(out=acsp[q, :, :], in_=splA[:, :]).then_inc(ld, 16)
                e.dma_start(out=acsp[3 + q, :, :], in_=splB[:, :]).then_inc(ld, 16)
            prog.on("sync", f_spdma)
        spl_v = n["vv"]
        acsp_ld = n["ld"]

        n["sv"] += 1
        def s_lndt(e, w=spl_v):
            e.wait_ge(vv, w)
            e.activation(out=lndt[:, :], in_=dtv[:, :], func=AF.Ln, scale=1.0).then_inc(sv, 1)
        prog.on("scalar", s_lndt)
        lndt_v = n["sv"]

        bse3 = n["vv"]
        n["vv"] += 3 + 7 * 2 + 1
        def v_relay(e, bse3=bse3):
            e.wait_ge(ld, ld0)
            e.wait_ge(vv, init_v)
            k = bse3
            e.tensor_tensor(out=dprm[:, :], in0=dall[:, :], in1=rm16[:, :], op=ALU.mult).then_inc(vv, 1); k += 1
            e.wait_ge(vv, k)
            e.tensor_tensor(out=dprm[:, :], in0=dprm[:, :], in1=im16[:, :], op=ALU.add).then_inc(vv, 1); k += 1
            e.wait_ge(vv, k)
            e.memset(qsuf[:, 14:16], 1.0).then_inc(vv, 1); k += 1
            for j in range(6, -1, -1):
                e.wait_ge(vv, k)
                e.tensor_tensor(out=qsuf[:, 2 * j:2 * j + 2], in0=qsuf[:, 2 * j + 2:2 * j + 4],
                                in1=dprm[:, 2 * j + 2:2 * j + 4], op=ALU.mult).then_inc(vv, 1); k += 1
                e.wait_ge(vv, k)
                e.tensor_copy(out=qsuf[0:1, 0:1], in_=qsuf[0:1, 0:1]).then_inc(vv, 1); k += 1
            e.wait_ge(vv, k)
            e.tensor_tensor(out=qm[:, :], in0=qsuf[:, :], in1=rm16[:, :], op=ALU.mult).then_inc(vv, 1)
        prog.on("vector", v_relay)
        qm_v = n["vv"]

        sall_done = {}
        pcd_free = [0]
        for b in range(BATCH):
            for j in range(NCORES):
                it = b * NCORES + j
                n["ld"] += 16
                def f_sld(e, b=b, j=j, need_vv=sall_done.get(it - 1, 0)):
                    e.wait_ge(vv, need_vv)
                    e.dma_start(out=sld[:, :],
                                in_=i_sall[j * 256 + b * 128:j * 256 + b * 128 + 128, :]
                                ).then_inc(ld, 16)
                prog.on("sync", f_sld)
                sldv = n["ld"]
                n["tv"] += 1
                def f_qtr(e, b=b, j=j, w=qm_v, wfree=pcd_free[0],
                          wterm=sall_done.get(it - 1, 0)):
                    e.wait_ge(vv, max(w, wterm))
                    e.wait_ge(sv, wfree)
                    e.transpose(pcd[0:1, 0:NH], qm[:, 2 * j + b:2 * j + b + 1],
                                identf[0:NH, 0:NH]).then_inc(tv, 1)
                prog.on("tensor", f_qtr)
                qtr_v = n["tv"]
                n["sv"] += 1
                def f_qrow(e, w=qtr_v):
                    e.wait_ge(tv, w)
                    e.copy(out=qrow[:, :], in_=pcd[0:1, 0:NH]).then_inc(sv, 1)
                prog.on("scalar", f_qrow)
                qrow_v = n["sv"]
                pcd_free[0] = qrow_v
                n["tv"] += 1
                def f_qmm(e, w=qrow_v):
                    e.wait_ge(sv, w)
                    e.matmul(pcd[0:128, 0:NH], onesf[:, :], qrow[:, :], start=True,
                             stop=True).then_inc(tv, 1)
                prog.on("tensor", f_qmm)
                qmm_v = n["tv"]
                n["vv"] += 2
                def f_term(e, b=b, j=j, w1=qmm_v, w2=sldv, k0=n["vv"] - 1):
                    e.wait_ge(tv, w1)
                    e.wait_ge(ld, w2)
                    e.tensor_tensor(out=tmp_sb[:, :].rearrange("x (h p) -> x h p", h=NH),
                                    in0=sld[:, :].rearrange("x (h p) -> x h p", h=NH),
                                    in1=pcd[0:128, 0:NH].unsqueeze(2).to_broadcast((128, NH, HPD)),
                                    op=ALU.mult).then_inc(vv, 1)
                    e.wait_ge(vv, k0)
                    if j == 0:
                        e.tensor_copy(out=SC[b][:, :], in_=tmp_sb[:, :]).then_inc(vv, 1)
                    else:
                        e.tensor_tensor(out=SC[b][:, :], in0=SC[b][:, :], in1=tmp_sb[:, :],
                                        op=ALU.add).then_inc(vv, 1)
                prog.on("vector", f_term)
                sall_done[it] = n["vv"]
                pcd_free[0] = qrow_v
        state_v = n["vv"]

        ych_done = {}
        yn_dma = {}
        for u in range(16):
            b, j = u // NCKB, u % NCKB
            c0 = b * SPC + j * CK
            prev_fin = ych_done.get(u - 1, state_v)

            lds = {}
            for nm, dst, srcap in [("xp", xpch, i_xp[u, :, :]),
                                   ("sj", sjch, i_sj[u, :, :])]:
                n["ld"] += 16
                def f_cl(e, dst=dst, srcap=srcap, w=prev_fin):
                    e.wait_ge(fin, 0)
                    e.wait_ge(vv, w)
                    e.dma_start(out=dst[:, :], in_=srcap).then_inc(ld, 16)
                prog.on("sync", f_cl)
                lds[nm] = n["ld"]
            for nm, dst, src in [("xs", xsch, i_xs), ("sg", sgch, i_sg)]:
                n["ld"] += 16
                def f_cl2(e, dst=dst, src=src, c0=c0, w=prev_fin):
                    e.wait_ge(vv, w)
                    e.dma_start(out=dst[:, :].rearrange("p (pb s) -> p pb s", pb=32),
                                in_=src.rearrange("(pb p) t -> p pb t", p=128)[:, :, c0:c0 + CK]
                                ).then_inc(ld, 16)
                prog.on("sync", f_cl2)
                lds[nm] = n["ld"]

            n["tv"] += 1
            def f_cdtr(e, c0=c0, w=spl_v, wfree=pcd_free[0], wp=prev_fin):
                e.wait_ge(vv, max(w, wp))
                e.wait_ge(sv, wfree)
                e.transpose(pcd[0:1, 0:NH], acum[:, c0 + CK - 1:c0 + CK],
                            identf[0:NH, 0:NH]).then_inc(tv, 1)
            prog.on("tensor", f_cdtr)
            cdtr_v = n["tv"]
            n["sv"] += 1
            def f_cdrow(e, w=cdtr_v):
                e.wait_ge(tv, w)
                e.copy(out=cdrow[:, :], in_=pcd[0:1, 0:NH]).then_inc(sv, 1)
            prog.on("scalar", f_cdrow)
            cdrow_v = n["sv"]
            n["tv"] += 1
            def f_cdmm(e, w=cdrow_v):
                e.wait_ge(sv, w)
                e.matmul(pcd[0:128, 0:NH], onesf[:, :], cdrow[:, :], start=True,
                         stop=True).then_inc(tv, 1)
            prog.on("tensor", f_cdmm)
            cdmm_v = n["tv"]
            n["sv"] += 1
            def f_cdB(e, w=cdmm_v):
                e.wait_ge(tv, w)
                e.activation(out=cdB[:, :], in_=pcd[0:128, 0:NH], func=AF.Exp).then_inc(sv, 1)
            prog.on("scalar", f_cdB)
            cdB_v = n["sv"]
            pcd_free[0] = cdB_v

            n["tv"] += 1
            def f_ldtr(e, c0=c0, w=lndt_v, wfree=pcd_free[0]):
                e.wait_ge(sv, max(w, wfree))
                e.transpose(pcd[0:128, 0:NH], lndt[:, c0:c0 + CK], identf[0:NH, 0:NH]).then_inc(tv, 1)
            prog.on("tensor", f_ldtr)
            ldtr_v = n["tv"]
            n["sv"] += 1
            def f_ldT(e, w=ldtr_v):
                e.wait_ge(tv, w)
                e.copy(out=lndtT[:, :], in_=pcd[0:128, 0:NH]).then_inc(sv, 1)
            prog.on("scalar", f_ldT)
            lndtT_v = n["sv"]
            pcd_free[0] = lndtT_v

            bse4 = n["vv"]
            n["vv"] += 3
            def f_scan(e, b=b, bse4=bse4, wcd=cdB_v, wsj=lds["sj"], wp=prev_fin):
                e.wait_ge(vv, wp)
                e.wait_ge(sv, wcd)
                e.wait_ge(ld, wsj)
                e.tensor_copy(out=prevb[:, :], in_=SC[b][:, :]).then_inc(vv, 1)
                e.wait_ge(vv, bse4 + 1)
                e.tensor_tensor(out=tmp_sb[:, :].rearrange("x (h p) -> x h p", h=NH),
                                in0=SC[b][:, :].rearrange("x (h p) -> x h p", h=NH),
                                in1=cdB[:, :].unsqueeze(2).to_broadcast((128, NH, HPD)),
                                op=ALU.mult).then_inc(vv, 1)
                e.wait_ge(vv, bse4 + 2)
                e.tensor_tensor(out=SC[b][:, :], in0=tmp_sb[:, :], in1=sjch[:, :],
                                op=ALU.add).then_inc(vv, 1)
            prog.on("vector", f_scan)
            scan_v = n["vv"]

            n["tv"] += 1
            def f_cbmm(e, c0=c0, w=init_v, wcons=ych_done.get(u - 1, 0)):
                e.wait_ge(vv, max(w, wcons))
                e.matmul(pcb[0:128, 0:CK], b_sb[:, c0:c0 + CK], c_sb[:, c0:c0 + CK],
                         start=True, stop=True).then_inc(tv, 1)
            prog.on("tensor", f_cbmm)
            cb_v = n["tv"]
            n["vv"] += 1
            def f_cbev(e, w=cb_v):
                e.wait_ge(tv, w)
                e.tensor_copy(out=cbt[:, :], in_=pcb[0:128, 0:CK]).then_inc(vv, 1)
            prog.on("vector", f_cbev)
            cbt_v = n["vv"]

            for hb in range(4):
                need_mm = n["mm"]
                n["ld"] += 32
                def f_smld(e, c0=c0, hb=hb, need_mm=need_mm, wsp=acsp_ld, wi=init_v):
                    e.wait_ge(ld, wsp)
                    e.wait_ge(vv, wi)
                    e.wait_ge(mmv, need_mm)
                    e.dma_start(out=stat6[3:6, :].rearrange("q (h s) -> q h s", h=16),
                                in_=acsp.ap().rearrange("q h t -> q h t")[3:6, hb * 16:(hb + 1) * 16,
                                                                          c0:c0 + CK]).then_inc(ld, 16)
                    e.dma_start(out=mov6[0:3, :].rearrange("q (h s) -> q h s", h=16),
                                in_=acsp.ap()[0:3, hb * 16:(hb + 1) * 16, c0:c0 + CK]).then_inc(ld, 16)
                prog.on("sync", f_smld)
                smld_v = n["ld"]


                for sub in range(4):
                    g4 = hb * 4 + sub
                    n["mm"] += 1
                    def f_dmm(e, sub=sub, w=smld_v, wE=n["sv"], winit=init_v):
                        e.wait_ge(ld, w)
                        e.wait_ge(vv, winit)
                        e.wait_ge(sv, wE)
                        last = None
                        for hh in range(4):
                            hcol = sub * 4 + hh
                            last = e.matmul(pd[:, hh * 128:(hh + 1) * 128],
                                            stat6[:, hcol * 128:(hcol + 1) * 128],
                                            mov6[:, hcol * 128:(hcol + 1) * 128],
                                            start=True, stop=False)
                            last = e.matmul(pd[:, hh * 128:(hh + 1) * 128], mnegb[:, :],
                                            identb[:, :], start=False, stop=True)
                            last = e.matmul(pr[:, hh * 128:(hh + 1) * 128], ones3[:, :],
                                            mov6[0:3, hcol * 128:(hcol + 1) * 128],
                                            start=True, stop=True)
                        last.then_inc(mmv, 1)
                    prog.on("tensor", f_dmm)
                    dmm_v = n["mm"]

                    n["sv"] += 1
                    def f_exps(e, hb=hb, sub=sub, w=dmm_v, wl=lndtT_v, wcons=n["vv"]):
                        e.wait_ge(mmv, w)
                        e.wait_ge(sv, wl)
                        e.wait_ge(vv, wcons)
                        for hh in range(4):
                            h = hb * 16 + sub * 4 + hh
                            e.activation(out=etop[:, hh * 128:(hh + 1) * 128],
                                         in_=pd[:, hh * 128:(hh + 1) * 128], func=AF.Exp,
                                         scale=1.0, bias=lndtT[:, h:h + 1])
                        e.activation(out=ebot[:, :], in_=pr[:, :], func=AF.Exp).then_inc(sv, 1)
                    prog.on("scalar", f_exps)
                    exps_v = n["sv"]

                    bse5 = n["vv"]
                    n["vv"] += 2
                    def f_mtt(e, w=exps_v, wcb=cbt_v, bse5=bse5, wY=n["mm"], c0=c0):
                        e.wait_ge(sv, w)
                        e.wait_ge(vv, wcb)
                        e.wait_ge(mmv, wY)
                        e.tensor_tensor(out=mtop[:, :].rearrange("s (q t) -> s q t", q=4),
                                        in0=etop[:, :].rearrange("s (q t) -> s q t", q=4),
                                        in1=cbt[:, :].unsqueeze(1).to_broadcast((128, 4, CK)),
                                        op=ALU.mult).then_inc(vv, 1)
                        e.wait_ge(vv, bse5 + 1)
                        e.tensor_tensor(out=mbot[:, :].rearrange("n (q t) -> n q t", q=4),
                                        in0=ebot[:, :].rearrange("n (q t) -> n q t", q=4),
                                        in1=c_sb[:, c0:c0 + CK].unsqueeze(1).to_broadcast((128, 4, CK)),
                                        op=ALU.mult).then_inc(vv, 1)
                    prog.on("vector", f_mtt)
                    mtt_v = n["vv"]

                    pz = pY[(g4 // 2) % 2]
                    n["mm"] += 1
                    def f_ymm(e, hb=hb, sub=sub, pz=pz, w=mtt_v, wx=lds["xp"], wsc=scan_v,
                              wev=n["vv"]):
                        e.wait_ge(vv, w)
                        e.wait_ge(ld, wx)
                        e.wait_ge(vv, wsc)
                        last = None
                        for hh in range(4):
                            h = hb * 16 + sub * 4 + hh
                            ph = h % 2
                            col = ((h % 8) // 2) * 128
                            tp = (0, 64) if ph else None
                            last = e.matmul(pz[ph * 64:ph * 64 + 64, col:col + CK],
                                            xpch[:, h * 64:(h + 1) * 64],
                                            mtop[:, hh * 128:(hh + 1) * 128],
                                            start=True, stop=False, tile_position=tp)
                            last = e.matmul(pz[ph * 64:ph * 64 + 64, col:col + CK],
                                            prevb[:, h * 64:(h + 1) * 64],
                                            mbot[:, hh * 128:(hh + 1) * 128],
                                            start=False, stop=True, tile_position=tp)
                        last.then_inc(mmv, 1)
                    prog.on("tensor", f_ymm)
                    ymm_v = n["mm"]

                    if sub % 2 == 1:
                        gg = hb * 2 + sub // 2
                        n["vv"] += 2
                        def f_yev(e, u=u, gg=gg, pz=pz, w=ymm_v, wxs=lds["xs"], k0=n["vv"] - 1):
                            e.wait_ge(mmv, w)
                            e.wait_ge(ld, wxs)
                            e.tensor_tensor(
                                out=tmp_sb[:, 0:512].rearrange("p (q s) -> p q s", q=4),
                                in0=xsch[:, gg * 512:(gg + 1) * 512].rearrange("p (q s) -> p q s", q=4),
                                in1=drep[:, gg * 4:(gg + 1) * 4].unsqueeze(2).to_broadcast((128, 4, CK)),
                                op=ALU.mult).then_inc(vv, 1)
                            e.wait_ge(vv, k0)
                            e.tensor_tensor(out=ych[:, gg * 512:(gg + 1) * 512],
                                            in0=tmp_sb[:, 0:512], in1=pz[:, :],
                                            op=ALU.add).then_inc(vv, 1)
                        prog.on("vector", f_yev)
            ych_v = n["vv"]

            bse6 = n["vv"]
            n["vv"] += 1
            def f_yg(e, w=lds["sg"], bse6=bse6):
                e.wait_ge(ld, w)
                e.tensor_tensor(out=ych[:, :], in0=ych[:, :], in1=sgch[:, :],
                                op=ALU.mult).then_inc(vv, 1)
            prog.on("vector", f_yg)
            yg_v = n["vv"]

            n["sv"] += 1
            def f_ysq(e, w=yg_v, need_dyn=yn_dma.get(u - 1, 0)):
                e.wait_ge(vv, w)
                e.wait_ge(dyn, need_dyn)
                e.activation(out=ysq[:, :], in_=ych[:, :], func=AF.Square).then_inc(sv, 1)
            prog.on("scalar", f_ysq)
            ysq_v = n["sv"]

            n["mm"] += 1
            def f_rmsmm(e, w=ysq_v, wfree=n["sv"]):
                e.wait_ge(sv, w)
                last = None
                for pb in range(32):
                    last = e.matmul(prms[0:1, 0:CK], onescol[:, :],
                                    ysq[:, pb * 128:(pb + 1) * 128],
                                    start=(pb == 0), stop=(pb == 31))
                last.then_inc(mmv, 1)
            prog.on("tensor", f_rmsmm)
            rmsmm_v = n["mm"]

            n["sv"] += 1
            def f_std(e, w=rmsmm_v):
                e.wait_ge(mmv, w)
                e.activation(out=stdrow[:, :], in_=prms[0:1, 0:CK], func=AF.Sqrt,
                             scale=1.0 / IDIM, bias=cv[0:1, 0:1]).then_inc(sv, 1)
            prog.on("scalar", f_std)
            std_v = n["sv"]

            n["vv"] += 1
            def f_rstd(e, w=std_v):
                e.wait_ge(sv, w)
                e.reciprocal(out=rstdrow[:, :], in_=stdrow[:, :]).then_inc(vv, 1)
            prog.on("vector", f_rstd)
            rstd_v = n["vv"]

            n["mm"] += 1
            def f_rbmm(e, w=rstd_v):
                e.wait_ge(vv, w)
                e.matmul(prstdB[0:128, 0:CK], onesf[:, :], rstdrow[:, :], start=True,
                         stop=True).then_inc(mmv, 1)
            prog.on("tensor", f_rbmm)
            rb_v = n["mm"]

            need_dyn = yn_dma.get(u - 1, 0)
            bse7 = n["vv"]
            n["vv"] += 32
            def f_yn(e, w=rb_v, bse7=bse7, need_dyn=need_dyn):
                e.wait_ge(mmv, w)
                e.wait_ge(dyn, need_dyn)
                for pb in range(32):
                    e.scalar_tensor_tensor(out=ynst[:, pb * 128:(pb + 1) * 128],
                                           in0=ych[:, pb * 128:(pb + 1) * 128],
                                           scalar=nw[:, pb:pb + 1],
                                           in1=prstdB[0:128, 0:CK],
                                           op0=ALU.mult, op1=ALU.mult).then_inc(vv, 1)
            prog.on("vector", f_yn)
            yn_v = n["vv"]
            ych_done[u] = yn_v

            n["dyn"] += 16
            def f_yndma(e, u=u, w=yn_v, c0=c0):
                e.wait_ge(vv, w)
                e.dma_start(out=o_yn.rearrange("(pb p) t -> p pb t", p=128)[:, :, c0:c0 + CK],
                            in_=ynst[:, :].rearrange("p (pb s) -> p pb s", pb=32)).then_inc(dyn, 16)
            prog.on("sync", f_yndma)
            yn_dma[u] = n["dyn"]

        def f_fin(e, a=n["dyn"]):
            e.wait_ge(dyn, a)
        prog.on("sync", f_fin)
        prog.emit(block)
    return nc


def phase4b(nc, io):
    i_yn = io["d_yn"]; i_wouT = io["w_outT"]; i_g = io["d_g"]
    i_xr = io["xh"]; i_cv = io["cvec"]
    o_x1 = io["d_x1"]; o_xn1T = io["d_xn1T"]

    prog = Prog(); PHX[0] += 1
    with ExitStack() as st:
        _ctr = [0]
        def sb(shape, dt=F32):
            _ctr[0] += 1
            return st.enter_context(nc.sbuf_tensor("sb%d_%d" % (PHX[0], _ctr[0]), list(shape), dt))
        def ps(shape, dt=F32):
            _ctr[0] += 1
            return st.enter_context(nc.psum_tensor("ps%d_%d" % (PHX[0], _ctr[0]), list(shape), dt))
        def sem(name):
            return st.enter_context(nc.semaphore("s%d_%s" % (PHX[0], name)))

        xn1T = [sb((128, TPC), BF16) for _ in range(16)]
        ynb = [sb((128, 32 * 512), BF16) for _ in range(2)]
        wst = [sb((128, 32 * 128), BF16) for _ in range(2)]
        s0blk = sb((128, 512), BF16)
        s0T = sb((128, 4 * 16 * 128), BF16)
        xt = sb((128, DM))
        gt = sb((128, DM), BF16)
        x1f = sb((128, DM))
        sqs = sb((128, DM))
        sxn1 = sb((128, DM), BF16)
        sumc, sqc, meanc, nvc, stdc, rstdc = (sb((128, 1)) for _ in range(6))
        cv = sb((128, 8))
        identb = sb((128, 128), BF16)
        iota_p = sb((128, 1), mybir.dt.int32); iota_f = sb((128, 128), mybir.dt.int32)
        fidp = sb((128, 1)); fidf = sb((128, 128))
        ps0 = [ps((128, 512)) for _ in range(2)]
        ptr = [ps((128, 128), BF16) for _ in range(2)]

        ld = sem("ld"); gio = sem("gio"); wld = sem("wld")
        vv = sem("vv"); sv = sem("sv"); tv = sem("tv"); mmv = sem("mmv")
        dst = sem("dst")
        block = st.enter_context(nc.Block())

        n = dict(ld=0, wld=0, vv=0, sv=0, tv=0, mm=0, dst=0)

        def g_iota(e):
            e.iota(iota_p[:], pattern=[[0, 1]], base=0, channel_multiplier=1)
            e.iota(iota_f[:], pattern=[[1, 128]], base=0, channel_multiplier=0).then_inc(gio, 1)
        prog.on("gpsimd", g_iota)

        def s_cv(e):
            e.dma_start(out=cv[:, :], in_=i_cv.to_broadcast((128, 8))).then_inc(ld, 16)
        prog.on("sync", s_cv)
        n["ld"] = 16

        n["vv"] += 1
        def v_init(e):
            e.wait_ge(gio, 1)
            e.tensor_copy(out=fidp[:], in_=iota_p[:])
            e.tensor_copy(out=fidf[:], in_=iota_f[:])
            e.tensor_scalar(out=identb[:], in0=fidf[:], scalar1=fidp[:], scalar2=None,
                            op0=ALU.is_equal).then_inc(vv, 1)
        prog.on("vector", v_init)
        init_v = n["vv"]

        mm_after_tg = {}
        ev_after = [0]
        for tg in range(4):
            tok0 = tg * 512
            ybuf = tg % 2
            n["ld"] += 16
            def f_ynld(e, tok0=tok0, ybuf=ybuf, need_mm=mm_after_tg.get(tg - 2, 0)):
                e.wait_ge(mmv, need_mm)
                e.dma_start(out=ynb[ybuf][:, :].rearrange("i (ib t) -> i ib t", ib=32),
                            in_=i_yn.rearrange("(ib i) t -> i ib t", i=128)[:, :, tok0:tok0 + 512]
                            ).then_inc(ld, 16)
            prog.on("sync", f_ynld)
            ynldv = n["ld"]

            for db in range(16):
                dbu = tg * 16 + db
                wbuf = dbu % 2
                n["wld"] += 16
                def f_wld(e, db=db, wbuf=wbuf, need_mm=mm_after_tg.get(tg - 1, 0) if db < 2 else 0,
                          nm=max(0, n["mm"] - 1)):
                    e.wait_ge(mmv, nm)
                    e.dma_start(out=wst[wbuf][:, :].rearrange("i (ib d) -> i ib d", ib=32),
                                in_=i_wouT.rearrange("(ib i) d -> i ib d", i=128)[
                                    :, :, db * 128:(db + 1) * 128]).then_inc(wld, 16)
                prog.on("gpsimd", f_wld)
                wldv = n["wld"]

                pz = ps0[dbu % 2]
                n["mm"] += 1
                def f_mm(e, ybuf=ybuf, wbuf=wbuf, pz=pz, w1=ynldv, w2=wldv,
                         need_ev=ev_after[0]):
                    e.wait_ge(ld, w1)
                    e.wait_ge(wld, w2)
                    e.wait_ge(vv, need_ev)
                    last = None
                    for ib in range(32):
                        last = e.matmul(pz[:, :], wst[wbuf][:, ib * 128:(ib + 1) * 128],
                                        ynb[ybuf][:, ib * 512:(ib + 1) * 512],
                                        start=(ib == 0), stop=(ib == 31))
                    last.then_inc(mmv, 1)
                prog.on("tensor", f_mm)
                mmvv = n["mm"]

                n["vv"] += 1
                def f_ev(e, pz=pz, w=mmvv, wtr=n["tv"]):
                    e.wait_ge(mmv, w)
                    e.wait_ge(tv, wtr)
                    e.tensor_copy(out=s0blk[:, :], in_=pz[:, :]).then_inc(vv, 1)
                prog.on("vector", f_ev)
                evv = n["vv"]
                ev_after[0] = evv

                for tsub in range(4):
                    pj = ptr[tsub % 2]
                    n["tv"] += 1
                    def f_tr(e, tsub=tsub, pj=pj, w=evv, wev=max(0, n["vv"] - (1 if tsub >= 2 else 0))):
                        e.wait_ge(vv, wev)
                        e.transpose(pj[0:128, 0:128], s0blk[:, tsub * 128:(tsub + 1) * 128],
                                    identb[:]).then_inc(tv, 1)
                    prog.on("tensor", f_tr)
                    trv = n["tv"]
                    n["vv"] += 1
                    def f_trev(e, tsub=tsub, db=db, pj=pj, w=trv):
                        e.wait_ge(tv, w)
                        e.tensor_copy(out=s0T[:, (tsub * 16 + db) * 128:(tsub * 16 + db + 1) * 128],
                                      in_=pj[0:128, 0:128]).then_inc(vv, 1)
                    prog.on("vector", f_trev)
            s0T_v = n["vv"]
            mm_after_tg[tg] = n["mm"]

            for tsub in range(4):
                t0 = tok0 + tsub * 128
                b, pos0 = t0 // SPC, t0 % SPC
                n["ld"] += 32
                def f_xg(e, b=b, pos0=pos0, t0=t0, w=max(s0T_v, n["vv"])):
                    e.wait_ge(vv, w)
                    e.dma_start(out=xt[:, :],
                                in_=i_xr[b, HALO + pos0:HALO + pos0 + 128, :]).then_inc(ld, 16)
                    e.dma_start(out=gt[:, :], in_=i_g[t0:t0 + 128, :]).then_inc(ld, 16)
                prog.on("sync", f_xg)
                xgv = n["ld"]

                bse = n["vv"]
                n["vv"] += 3
                def f_res(e, tsub=tsub, bse=bse, w=xgv, wd=n["dst"]):
                    e.wait_ge(ld, w)
                    e.wait_ge(dst, wd)
                    e.tensor_tensor(out=x1f[:, :].rearrange("t (db d) -> t db d", db=16),
                                    in0=gt[:, :].rearrange("t (db d) -> t db d", db=16),
                                    in1=s0T[:, :].rearrange("t (ts db d) -> t ts db d",
                                                            ts=4, db=16)[:, tsub, :, :],
                                    op=ALU.mult).then_inc(vv, 1)
                    e.wait_ge(vv, bse + 1)
                    e.tensor_tensor(out=x1f[:, :], in0=x1f[:, :], in1=xt[:, :],
                                    op=ALU.add).then_inc(vv, 1)
                    e.wait_ge(vv, bse + 2)
                    e.reduce_sum(out=sumc[:, :], in_=x1f[:, :], axis=AX.X).then_inc(vv, 1)
                prog.on("vector", f_res)
                res_v = n["vv"]

                n["dst"] += 16
                def f_x1dma(e, b=b, pos0=pos0, w=res_v):
                    e.wait_ge(vv, w)
                    e.dma_start(out=o_x1[b, pos0:pos0 + 128, :], in_=x1f[:, :]).then_inc(dst, 16)
                prog.on("sync", f_x1dma)
                x1d_v = n["dst"]

                n["sv"] += 1
                def f_sq(e, w=res_v):
                    e.wait_ge(vv, w)
                    e.activation(out=sqs[:, :], in_=x1f[:, :], func=AF.Square,
                                 accum_out=sqc[:, :]).then_inc(sv, 1)
                prog.on("scalar", f_sq)
                sqv = n["sv"]

                bse2 = n["vv"]
                n["vv"] += 2
                def f_stats(e, w=sqv, bse2=bse2):
                    e.wait_ge(sv, w)
                    e.tensor_scalar(out=meanc[:, :], in0=sumc[:, :], scalar1=1.0 / DM,
                                    scalar2=None, op0=ALU.mult)
                    e.tensor_scalar(out=nvc[:, :], in0=sqc[:, :], scalar1=1.0 / DM,
                                    scalar2=None, op0=ALU.mult).then_inc(vv, 1)
                    e.wait_ge(vv, bse2 + 1)
                    e.scalar_tensor_tensor(out=nvc[:, :], in0=meanc[:, :], scalar=meanc[:, :],
                                           in1=nvc[:, :], op0=ALU.mult,
                                           op1=ALU.subtract).then_inc(vv, 1)
                prog.on("vector", f_stats)
                st_v = n["vv"]

                n["sv"] += 1
                def f_std(e, w=st_v):
                    e.wait_ge(vv, w)
                    e.activation(out=stdc[:, :], in_=nvc[:, :], func=AF.Sqrt, scale=-1.0,
                                 bias=cv[:, 0:1]).then_inc(sv, 1)
                prog.on("scalar", f_std)
                stdv = n["sv"]

                bse3 = n["vv"]
                n["vv"] += 2
                def f_xn1(e, w=stdv, wd=x1d_v, bse3=bse3):
                    e.wait_ge(sv, w)
                    e.wait_ge(dst, wd)
                    e.reciprocal(out=rstdc[:, :], in_=stdc[:, :]).then_inc(vv, 1)
                    e.wait_ge(vv, bse3 + 1)
                    e.tensor_scalar(out=sxn1[:, :], in0=x1f[:, :], scalar1=meanc[:, :],
                                    scalar2=rstdc[:, :], op0=ALU.subtract,
                                    op1=ALU.mult).then_inc(vv, 1)
                prog.on("vector", f_xn1)
                xn1_v = n["vv"]

                for kg in range(8):
                    pj = ptr[kg % 2]
                    n["tv"] += 2
                    def f_tr2(e, kg=kg, pj=pj, w=xn1_v, wev=n["vv"]):
                        e.wait_ge(vv, wev)
                        last = None
                        for q in range(2):
                            last = e.transpose(ptr[q][0:128, 0:128],
                                               sxn1[:, (kg * 2 + q) * 128:(kg * 2 + q + 1) * 128],
                                               identb[:])
                        last.then_inc(tv, 2)
                    prog.on("tensor", f_tr2)
                    trv2 = n["tv"]
                    n["vv"] += 1
                    def f_tre2(e, kg=kg, t0=t0, w=trv2):
                        e.wait_ge(tv, w)
                        last = None
                        for q in range(2):
                            last = e.tensor_copy(out=xn1T[kg * 2 + q][:, t0:t0 + 128],
                                                 in_=ptr[q][0:128, 0:128])
                        last.then_inc(vv, 1)
                    prog.on("vector", f_tre2)

        all_v = n["vv"]
        for k in range(16):
            n["dst"] += 16
            def f_spill(e, k=k, w=all_v):
                e.wait_ge(vv, w)
                e.dma_start(out=o_xn1T[k * 128:(k + 1) * 128, :], in_=xn1T[k][:, :]).then_inc(dst, 16)
            prog.on("sync", f_spill)

        def f_fin(e, a=n["dst"]):
            e.wait_ge(dst, a)
        prog.on("sync", f_fin)
        prog.emit(block)
    return nc


def phase5a(nc, io):
    i_xn1T = io["d_xn1T"]; i_wqkvT = io["wqkvT"]; i_fw = io["ffnW"]
    i_sc = io["fscale"]; i_cv = io["cvec"]
    o_qkv = io["d_qkv"]; o_g1 = io["d_g1"]; ofd = io["ofd_h"]

    prog = Prog(); PHX[0] += 1
    with ExitStack() as st:
        _ctr = [0]
        def sb(shape, dt=F32):
            _ctr[0] += 1
            return st.enter_context(nc.sbuf_tensor("sb%d_%d" % (PHX[0], _ctr[0]), list(shape), dt))
        def ps(shape, dt=F32):
            _ctr[0] += 1
            return st.enter_context(nc.psum_tensor("ps%d_%d" % (PHX[0], _ctr[0]), list(shape), dt))
        def sem(name):
            return st.enter_context(nc.semaphore("s%d_%s" % (PHX[0], name)))

        xn1T = [sb((128, TPC), BF16) for _ in range(16)]
        wqb = [sb((128, 16 * 128), BF16) for _ in range(2)]
        qst = sb((128, TPC), BF16)
        ffnb = [sb((128, 16 * 512), BF16) for _ in range(2)]
        scaleB = sb((128, DM))
        srow = sb((1, DM))
        ofst = sb((128, 512), BF16)
        ofrow = sb((128, DM), BF16)
        sqp = sb((128, 64))
        ss = sb((128, 1)); vn = sb((128, 1)); th = sb((128, 1)); dn = sb((128, 1))
        rc = sb((128, 1)); f1 = sb((128, 1)); onm = sb((128, 1)); rc2 = sb((128, 1))
        ff = sb((128, 1))
        g1pre = sb((128, DM))
        g1st = sb((128, DM), BF16)
        cv = sb((128, 8))
        onesf = sb((1, 128))
        psq = [ps((128, 512)) for _ in range(2)]
        psf = [ps((128, 512)) for _ in range(2)]
        psb = ps((128, 512))

        ld = sem("ld"); wld = sem("wld")
        vv = sem("vv"); sv = sem("sv"); mmv = sem("mmv"); dst = sem("dst")
        block = st.enter_context(nc.Block())

        n = dict(ld=0, wld=0, vv=0, sv=0, mm=0, dst=0)

        def s_loads(e):
            for k in range(16):
                e.dma_start(out=xn1T[k][:, :], in_=i_xn1T[k * 128:(k + 1) * 128, :]).then_inc(ld, 16)
            e.dma_start(out=srow[:, :], in_=i_sc).then_inc(ld, 16)
            e.dma_start(out=cv[:, :], in_=i_cv.to_broadcast((128, 8))).then_inc(ld, 16)
        prog.on("sync", s_loads)
        n["ld"] = 18 * 16
        ld0 = n["ld"]

        n["vv"] += 1
        def v_init(e):
            e.memset(onesf[:, :], 1.0).then_inc(vv, 1)
        prog.on("vector", v_init)
        init_v = n["vv"]

        ev_hist = {}
        for ob in range(24):
            wbuf = ob % 2
            n["wld"] += 16
            def f_wq(e, ob=ob, wbuf=wbuf, need=max(0, n["mm"] - 1)):
                e.wait_ge(mmv, need)
                e.dma_start(out=wqb[wbuf][:, :].rearrange("d (k o) -> d k o", k=16),
                            in_=i_wqkvT.rearrange("(k d) o -> d k o", d=128)[
                                :, :, ob * 128:(ob + 1) * 128]).then_inc(wld, 16)
            prog.on("gpsimd", f_wq)
            wqv = n["wld"]
            for tg in range(4):
                pz = psq[(ob * 4 + tg) % 2]
                n["mm"] += 1
                def f_qmm(e, wbuf=wbuf, tg=tg, pz=pz, w=wqv,
                          need_ev=ev_hist.get(ob * 4 + tg - 2, 0)):
                    e.wait_ge(wld, w)
                    e.wait_ge(ld, ld0)
                    e.wait_ge(vv, need_ev)
                    last = None
                    for k in range(16):
                        last = e.matmul(pz[:, :], wqb[wbuf][:, k * 128:(k + 1) * 128],
                                        xn1T[k][:, tg * 512:(tg + 1) * 512],
                                        start=(k == 0), stop=(k == 15))
                    last.then_inc(mmv, 1)
                prog.on("tensor", f_qmm)
                qmmv = n["mm"]
                n["vv"] += 1
                def f_qev(e, tg=tg, pz=pz, w=qmmv, wd=n["dst"]):
                    e.wait_ge(mmv, w)
                    e.wait_ge(dst, wd)
                    e.tensor_copy(out=qst[:, tg * 512:(tg + 1) * 512], in_=pz[:, :]).then_inc(vv, 1)
                prog.on("vector", f_qev)
                ev_hist[ob * 4 + tg] = n["vv"]
            n["dst"] += 16
            def f_qdma(e, ob=ob, w=n["vv"]):
                e.wait_ge(vv, w)
                e.dma_start(out=o_qkv[ob * 128:(ob + 1) * 128, :], in_=qst[:, :]).then_inc(dst, 16)
            prog.on("sync", f_qdma)
        qkv_done_mm = n["mm"]

        n["mm"] += 1
        def f_scmm(e, w=init_v):
            e.wait_ge(vv, w)
            e.wait_ge(ld, ld0)
            last = None
            for q in range(4):
                last = e.matmul(psb[:, :] if False else psb[:, 0:512], onesf[:, :],
                                srow[:, q * 512:(q + 1) * 512], start=True, stop=True)
            last.then_inc(mmv, 1)
        prog.on("tensor", None)
        prog.items["tensor"].pop()
        n["mm"] -= 1
        for q in range(4):
            n["mm"] += 1
            def f_scmm(e, q=q, w=init_v, wev=n["vv"]):
                e.wait_ge(vv, max(w, wev))
                e.wait_ge(ld, ld0)
                e.matmul(psb[:, 0:512], onesf[:, :], srow[:, q * 512:(q + 1) * 512],
                         start=True, stop=True).then_inc(mmv, 1)
            prog.on("tensor", f_scmm)
            scv = n["mm"]
            n["vv"] += 1
            def f_scev(e, q=q, w=scv):
                e.wait_ge(mmv, w)
                e.tensor_copy(out=scaleB[:, q * 512:(q + 1) * 512], in_=psb[:, 0:512]).then_inc(vv, 1)
            prog.on("vector", f_scev)
        scB_v = n["vv"]

        of_hist = {}
        for kg in range(4):
            fbuf = kg % 2
            n["wld"] += 16
            def f_fl(e, kg=kg, fbuf=fbuf, need=of_hist.get(kg - 2, qkv_done_mm)):
                e.wait_ge(mmv, need)
                e.dma_start(out=ffnb[fbuf][:, :].rearrange("nd (k c) -> nd k c", k=16),
                            in_=i_fw.rearrange("(k nd) c -> nd k c", nd=128)[
                                :, :, kg * 512:(kg + 1) * 512]).then_inc(wld, 16)
            prog.on("gpsimd", f_fl)
            flv = n["wld"]
            for tt in range(16):
                pz = psf[(kg * 16 + tt) % 2]
                n["mm"] += 1
                def f_fmm(e, fbuf=fbuf, tt=tt, pz=pz, w=flv, need_ev=max(0, n["vv"] - 1)):
                    e.wait_ge(wld, w)
                    e.wait_ge(vv, need_ev)
                    last = None
                    for k in range(16):
                        last = e.matmul(pz[:, :], xn1T[k][:, tt * 128:(tt + 1) * 128],
                                        ffnb[fbuf][:, k * 512:(k + 1) * 512],
                                        start=(k == 0), stop=(k == 15))
                    last.then_inc(mmv, 1)
                prog.on("tensor", f_fmm)
                fmmv = n["mm"]
                bsf = n["vv"]
                n["vv"] += 1
                def f_oev(e, kg=kg, tt=tt, pz=pz, w=fmmv, wsc=scB_v, wd=n["dst"]):
                    e.wait_ge(mmv, w)
                    e.wait_ge(vv, wsc)
                    e.wait_ge(dst, wd)
                    e.tensor_tensor(out=ofst[:, :], in0=pz[:, :],
                                    in1=scaleB[:, kg * 512:(kg + 1) * 512],
                                    op=ALU.mult).then_inc(vv, 1)
                prog.on("vector", f_oev)
                oevv = n["vv"]
                n["sv"] += 1
                def f_osq(e, kg=kg, tt=tt, w=oevv):
                    e.wait_ge(vv, w)
                    e.activation(out=g1pre[:, 0:512], in_=ofst[:, :], func=AF.Square,
                                 accum_out=sqp[:, tt * 4 + kg:tt * 4 + kg + 1]).then_inc(sv, 1)
                prog.on("scalar", f_osq)
                osqv = n["sv"]
                n["dst"] += 16
                def f_odma(e, kg=kg, tt=tt, w=oevv, ws=osqv):
                    e.wait_ge(vv, w)
                    e.wait_ge(sv, ws)
                    e.dma_start(out=ofd[tt * 128:(tt + 1) * 128, kg * 512:(kg + 1) * 512],
                                in_=ofst[:, :]).then_inc(dst, 16)
                prog.on("sync", f_odma)
            of_hist[kg] = n["mm"]
        ffn_dst = n["dst"]
        sq_done = n["sv"]

        g1_hist = {}
        for tt in range(16):
            n["ld"] += 16
            def f_ofl(e, tt=tt, w=ffn_dst, need_v=g1_hist.get(tt - 1, 0)):
                e.wait_ge(dst, w)
                e.wait_ge(vv, need_v)
                e.dma_start(out=ofrow[:, :], in_=ofd[tt * 128:(tt + 1) * 128, :]).then_inc(ld, 16)
            prog.on("sync", f_ofl)
            oflv = n["ld"]

            bsn = n["vv"]
            n["vv"] += 3
            def f_n1(e, tt=tt, bsn=bsn, ws=sq_done):
                e.wait_ge(sv, ws)
                e.tensor_tensor(out=ss[:, :], in0=sqp[:, tt * 4:tt * 4 + 1],
                                in1=sqp[:, tt * 4 + 1:tt * 4 + 2], op=ALU.add).then_inc(vv, 1)
                e.wait_ge(vv, bsn + 1)
                e.tensor_tensor(out=ss[:, :], in0=ss[:, :], in1=sqp[:, tt * 4 + 2:tt * 4 + 3],
                                op=ALU.add).then_inc(vv, 1)
                e.wait_ge(vv, bsn + 2)
                e.tensor_tensor(out=ss[:, :], in0=ss[:, :], in1=sqp[:, tt * 4 + 3:tt * 4 + 4],
                                op=ALU.add).then_inc(vv, 1)
            prog.on("vector", f_n1)
            n1v = n["vv"]

            n["sv"] += 1
            def f_vn(e, w=n1v):
                e.wait_ge(vv, w)
                e.activation(out=vn[:, :], in_=ss[:, :], func=AF.Sqrt, scale=1.0).then_inc(sv, 1)
            prog.on("scalar", f_vn)
            vnv = n["sv"]

            bsn2 = n["vv"]
            n["vv"] += 2
            def f_n2(e, bsn2=bsn2, w=vnv):
                e.wait_ge(sv, w)
                e.tensor_scalar(out=vn[:, :], in0=vn[:, :], scalar1=1e-7, scalar2=None,
                                op0=ALU.max).then_inc(vv, 1)
                e.wait_ge(vv, bsn2 + 1)
                e.tensor_scalar(out=dn[:, :], in0=vn[:, :], scalar1=1e-7, scalar2=None,
                                op0=ALU.add).then_inc(vv, 1)
            prog.on("vector", f_n2)
            n2v = n["vv"]

            n["sv"] += 1
            def f_th(e, w=n2v):
                e.wait_ge(vv, w)
                e.activation(out=th[:, :], in_=vn[:, :], func=AF.Tanh).then_inc(sv, 1)
            prog.on("scalar", f_th)
            thv = n["sv"]

            bsn3 = n["vv"]
            n["vv"] += 6
            def f_n3(e, bsn3=bsn3, w=thv):
                e.wait_ge(sv, w)
                e.reciprocal(out=rc[:, :], in_=dn[:, :]).then_inc(vv, 1)
                e.wait_ge(vv, bsn3 + 1)
                e.tensor_tensor(out=f1[:, :], in0=th[:, :], in1=rc[:, :], op=ALU.mult).then_inc(vv, 1)
                e.wait_ge(vv, bsn3 + 2)
                e.scalar_tensor_tensor(out=onm[:, :], in0=f1[:, :], scalar=1.0, in1=vn[:, :],
                                       op0=ALU.mult, op1=ALU.mult).then_inc(vv, 1)
                e.wait_ge(vv, bsn3 + 3)
                e.tensor_scalar(out=onm[:, :], in0=onm[:, :], scalar1=1e-7, scalar2=None,
                                op0=ALU.max).then_inc(vv, 1)
                e.wait_ge(vv, bsn3 + 4)
                e.reciprocal(out=rc2[:, :], in_=onm[:, :]).then_inc(vv, 1)
                e.wait_ge(vv, bsn3 + 5)
                e.tensor_scalar(out=rc2[:, :], in0=rc2[:, :], scalar1=0.99, scalar2=1.0,
                                op0=ALU.mult, op1=ALU.min).then_inc(vv, 1)
            prog.on("vector", f_n3)
            n3v = n["vv"]

            bsn4 = n["vv"]
            n["vv"] += 2
            def f_g1p(e, tt=tt, bsn4=bsn4, w=n3v, wl=oflv):
                e.wait_ge(vv, w)
                e.wait_ge(ld, wl)
                e.tensor_tensor(out=ff[:, :], in0=f1[:, :], in1=rc2[:, :], op=ALU.mult).then_inc(vv, 1)
                e.wait_ge(vv, bsn4 + 1)
                e.tensor_scalar(out=g1pre[:, :], in0=ofrow[:, :], scalar1=ff[:, :], scalar2=None,
                                op0=ALU.mult).then_inc(vv, 1)
            prog.on("vector", f_g1p)
            g1pv = n["vv"]

            n["sv"] += 1
            def f_g1s(e, w=g1pv, wd=n["dst"]):
                e.wait_ge(vv, w)
                e.wait_ge(dst, wd)
                e.activation(out=g1st[:, :], in_=g1pre[:, :], func=AF.Silu).then_inc(sv, 1)
            prog.on("scalar", f_g1s)
            g1sv = n["sv"]
            n["dst"] += 16
            def f_g1dma(e, tt=tt, w=g1sv):
                e.wait_ge(sv, w)
                e.dma_start(out=o_g1[tt * 128:(tt + 1) * 128, :], in_=g1st[:, :]).then_inc(dst, 16)
            prog.on("sync", f_g1dma)
            g1_hist[tt] = n["vv"]

        def f_fin(e, a=n["dst"]):
            e.wait_ge(dst, a)
        prog.on("sync", f_fin)
        prog.emit(block)
    return nc


def phase5b(nc, io):
    i_qkv = io["d_qkv"]; i_g1 = io["d_g1"]; i_x1 = io["d_x1"]
    i_woT = io["wo1T"]; i_sl = io["slrow"]
    o_out = io["outp"]

    prog = Prog(); PHX[0] += 1
    with ExitStack() as st:
        _ctr = [0]
        def sb(shape, dt=F32):
            _ctr[0] += 1
            return st.enter_context(nc.sbuf_tensor("sb%d_%d" % (PHX[0], _ctr[0]), list(shape), dt))
        def ps(shape, dt=F32):
            _ctr[0] += 1
            return st.enter_context(nc.psum_tensor("ps%d_%d" % (PHX[0], _ctr[0]), list(shape), dt))
        def sem(name):
            return st.enter_context(nc.semaphore("s%d_%s" % (PHX[0], name)))

        q1 = [sb((128, SPC), BF16) for _ in range(16)]
        kv = [sb((128, TPC), BF16) for _ in range(8)]
        o1f = [sb((128, SPC), BF16) for _ in range(16)]
        tmp0 = sb((128, SPC), BF16)
        tmp1 = sb((128, SPC), BF16)
        p10 = sb((1, SPC))
        dvt = sb((128, SPC), BF16)
        slrow = sb((1, 16))
        onesf = sb((1, 128))
        onescol = sb((128, 1), BF16)
        negcol = sb((128, 1), BF16)
        wst = [sb((128, 16 * 128), BF16) for _ in range(2)]
        s0blk = sb((128, 512), BF16)
        s1T = sb((128, 4 * 16 * 128), BF16)
        xt = sb((128, DM))
        gt = sb((128, DM), BF16)
        x2f = sb((128, DM))
        identb = sb((128, 128), BF16)
        iota_p = sb((128, 1), mybir.dt.int32); iota_f = sb((128, 128), mybir.dt.int32)
        fidp = sb((128, 1)); fidf = sb((128, 128))
        pdif = ps((1, 512))
        pb10 = ps((128, 512))
        pso = [ps((128, 512)) for _ in range(2)]
        ptr = [ps((128, 128), BF16) for _ in range(2)]

        ld = sem("ld"); wld = sem("wld"); gio = sem("gio")
        vv = sem("vv"); sv = sem("sv"); tv = sem("tv"); mmv = sem("mmv"); dst = sem("dst")
        block = st.enter_context(nc.Block())

        n = dict(ld=0, wld=0, vv=0, sv=0, tv=0, mm=0, dst=0)

        def g_iota(e):
            e.iota(iota_p[:], pattern=[[0, 1]], base=0, channel_multiplier=1)
            e.iota(iota_f[:], pattern=[[1, 128]], base=0, channel_multiplier=0).then_inc(gio, 1)
        prog.on("gpsimd", g_iota)

        def s_loads(e):
            for h in range(16):
                e.dma_start(out=q1[h][:, :], in_=i_qkv[h * 128:(h + 1) * 128, SPC:TPC]).then_inc(ld, 16)
            for z in range(8):
                e.dma_start(out=kv[z][:, :], in_=i_qkv[2048 + z * 128:2048 + (z + 1) * 128, :]).then_inc(ld, 16)
            e.dma_start(out=slrow[:, :], in_=i_sl).then_inc(ld, 16)
        prog.on("sync", s_loads)
        n["ld"] = 25 * 16
        ld0 = n["ld"]

        n["vv"] += 1
        def v_init(e):
            e.wait_ge(gio, 1)
            e.tensor_copy(out=fidp[:], in_=iota_p[:])
            e.tensor_copy(out=fidf[:], in_=iota_f[:])
            e.tensor_scalar(out=identb[:], in0=fidf[:], scalar1=fidp[:], scalar2=None,
                            op0=ALU.is_equal)
            e.memset(onesf[:, :], 1.0)
            e.memset(onescol[:, :], 1.0)
            e.memset(negcol[:, :], -1.0).then_inc(vv, 1)
        prog.on("vector", v_init)
        init_v = n["vv"]

        ATT_SCALE = float(1.0 / np.sqrt(128.0))
        for h in range(16):
            g = h // 4
            k0 = kv[2 * g]; v0 = kv[2 * g + 1]
            bsa = n["vv"]
            n["vv"] += 2
            def f_tt(e, h=h, k0=k0, v0=v0, bsa=bsa, w=init_v, wmm=n["mm"]):
                e.wait_ge(ld, ld0)
                e.wait_ge(vv, w)
                e.wait_ge(mmv, wmm)
                e.tensor_tensor(out=tmp0[:, :], in0=q1[h][:, :], in1=k0[:, 0:SPC],
                                op=ALU.mult).then_inc(vv, 1)
                e.wait_ge(vv, bsa + 1)
                e.tensor_tensor(out=tmp1[:, :], in0=q1[h][:, :], in1=k0[:, SPC:TPC],
                                op=ALU.mult).then_inc(vv, 1)
            prog.on("vector", f_tt)
            ttv = n["vv"]

            for half in range(2):
                n["mm"] += 1
                def f_dmm(e, half=half, w=ttv, ws=n["sv"]):
                    e.wait_ge(vv, w)
                    e.wait_ge(sv, ws)
                    e.matmul(pdif[0:1, 0:512], onescol[:, :], tmp0[:, half * 512:(half + 1) * 512],
                             start=True, stop=False)
                    e.matmul(pdif[0:1, 0:512], negcol[:, :], tmp1[:, half * 512:(half + 1) * 512],
                             start=False, stop=True).then_inc(mmv, 1)
                prog.on("tensor", f_dmm)
                dmmv = n["mm"]
                n["sv"] += 1
                def f_sig(e, h=h, half=half, w=dmmv):
                    e.wait_ge(mmv, w)
                    e.activation(out=p10[0:1, half * 512:(half + 1) * 512], in_=pdif[0:1, 0:512],
                                 func=AF.Sigmoid, scale=ATT_SCALE,
                                 bias=slrow[0:1, h:h + 1]).then_inc(sv, 1)
                prog.on("scalar", f_sig)
            sigv = n["sv"]

            bsb = n["vv"]
            n["vv"] += 1
            def f_dv(e, v0=v0, bsb=bsb):
                e.tensor_tensor(out=dvt[:, :], in0=v0[:, 0:SPC], in1=v0[:, SPC:TPC],
                                op=ALU.subtract).then_inc(vv, 1)
            prog.on("vector", f_dv)
            dvv = n["vv"]

            for half in range(2):
                n["mm"] += 1
                def f_bmm(e, half=half, w=sigv, wev=n["vv"]):
                    e.wait_ge(sv, w)
                    e.wait_ge(vv, wev)
                    e.matmul(pb10[:, 0:512], onesf[:, :], p10[0:1, half * 512:(half + 1) * 512],
                             start=True, stop=True).then_inc(mmv, 1)
                prog.on("tensor", f_bmm)
                bmmv = n["mm"]
                bsc = n["vv"]
                n["vv"] += 2
                def f_o1(e, h=h, half=half, v0=v0, w=bmmv, wd=dvv, bsc=bsc):
                    e.wait_ge(mmv, w)
                    e.wait_ge(vv, wd)
                    e.tensor_tensor(out=o1f[h][:, half * 512:(half + 1) * 512],
                                    in0=dvt[:, half * 512:(half + 1) * 512], in1=pb10[:, 0:512],
                                    op=ALU.mult).then_inc(vv, 1)
                    e.wait_ge(vv, bsc + 1)
                    e.tensor_tensor(out=o1f[h][:, half * 512:(half + 1) * 512],
                                    in0=o1f[h][:, half * 512:(half + 1) * 512],
                                    in1=v0[:, SPC + half * 512:SPC + (half + 1) * 512],
                                    op=ALU.add).then_inc(vv, 1)
                prog.on("vector", f_o1)
        att_v = n["vv"]
        att_mm = n["mm"]

        ev_after = [0]
        for tg in range(4):
            tok0 = tg * 512
            b = tok0 // SPC
            for db in range(16):
                dbu = tg * 16 + db
                wbuf = dbu % 2
                n["wld"] += 16
                def f_wl(e, db=db, wbuf=wbuf, nm=max(0, n["mm"] - att_mm - 1) + att_mm if n["mm"] > att_mm else att_mm):
                    e.wait_ge(mmv, nm)
                    e.dma_start(out=wst[wbuf][:, :].rearrange("o (k d) -> o k d", k=16),
                                in_=i_woT.rearrange("(k o) d -> o k d", o=128)[
                                    :, :, db * 128:(db + 1) * 128]).then_inc(wld, 16)
                prog.on("gpsimd", f_wl)
                wlv = n["wld"]

                pz = pso[dbu % 2]
                n["mm"] += 1
                def f_mm(e, tg=tg, db=db, wbuf=wbuf, pz=pz, w=wlv, wa=att_v,
                         need_ev=ev_after[0]):
                    e.wait_ge(wld, w)
                    e.wait_ge(vv, max(wa, need_ev))
                    last = None
                    for ob in range(16):
                        gg = ob // 4
                        if tg < 2:
                            mov = kv[2 * gg + 1][:, tg * 512:(tg + 1) * 512]
                        else:
                            mov = o1f[ob][:, (tg - 2) * 512:(tg - 1) * 512]
                        last = e.matmul(pz[:, :], wst[wbuf][:, ob * 128:(ob + 1) * 128], mov,
                                        start=(ob == 0), stop=(ob == 15))
                    last.then_inc(mmv, 1)
                prog.on("tensor", f_mm)
                mmvv = n["mm"]

                n["vv"] += 1
                def f_ev(e, pz=pz, w=mmvv, wtr=n["tv"]):
                    e.wait_ge(mmv, w)
                    e.wait_ge(tv, wtr)
                    e.tensor_copy(out=s0blk[:, :], in_=pz[:, :]).then_inc(vv, 1)
                prog.on("vector", f_ev)
                evv = n["vv"]
                ev_after[0] = evv

                for tsub in range(4):
                    pj = ptr[tsub % 2]
                    n["tv"] += 1
                    def f_tr(e, tsub=tsub, pj=pj, w=evv, wev=n["vv"]):
                        e.wait_ge(vv, wev)
                        e.transpose(pj[0:128, 0:128], s0blk[:, tsub * 128:(tsub + 1) * 128],
                                    identb[:]).then_inc(tv, 1)
                    prog.on("tensor", f_tr)
                    trv = n["tv"]
                    n["vv"] += 1
                    def f_trev(e, tsub=tsub, db=db, pj=pj, w=trv):
                        e.wait_ge(tv, w)
                        e.tensor_copy(out=s1T[:, (tsub * 16 + db) * 128:(tsub * 16 + db + 1) * 128],
                                      in_=pj[0:128, 0:128]).then_inc(vv, 1)
                    prog.on("vector", f_trev)
            s1T_v = n["vv"]

            for tsub in range(4):
                t0 = tok0 + tsub * 128
                b2, pos0 = t0 // SPC, t0 % SPC
                n["ld"] += 32
                def f_xg(e, b2=b2, pos0=pos0, t0=t0, w=n["vv"], wd=n["dst"]):
                    e.wait_ge(vv, w)
                    e.wait_ge(dst, wd)
                    e.dma_start(out=xt[:, :], in_=i_x1[b2, pos0:pos0 + 128, :]).then_inc(ld, 16)
                    e.dma_start(out=gt[:, :], in_=i_g1[t0:t0 + 128, :]).then_inc(ld, 16)
                prog.on("sync", f_xg)
                xgv = n["ld"]

                bsr = n["vv"]
                n["vv"] += 2
                def f_res(e, tsub=tsub, bsr=bsr, w=xgv, wd=n["dst"]):
                    e.wait_ge(ld, w)
                    e.wait_ge(dst, wd)
                    e.tensor_tensor(out=x2f[:, :].rearrange("t (db d) -> t db d", db=16),
                                    in0=gt[:, :].rearrange("t (db d) -> t db d", db=16),
                                    in1=s1T[:, :].rearrange("t (ts db d) -> t ts db d",
                                                            ts=4, db=16)[:, tsub, :, :],
                                    op=ALU.mult).then_inc(vv, 1)
                    e.wait_ge(vv, bsr + 1)
                    e.tensor_tensor(out=x2f[:, :], in0=x2f[:, :], in1=xt[:, :],
                                    op=ALU.add).then_inc(vv, 1)
                prog.on("vector", f_res)
                resv = n["vv"]

                n["dst"] += 16
                def f_odma(e, b2=b2, pos0=pos0, w=resv):
                    e.wait_ge(vv, w)
                    e.dma_start(out=o_out[b2, pos0:pos0 + 128, :], in_=x2f[:, :]).then_inc(dst, 16)
                prog.on("sync", f_odma)

        def f_fin(e, a=n["dst"]):
            e.wait_ge(dst, a)
        prog.on("sync", f_fin)
        prog.emit(block)
    return nc


def build_mega():
    nc = bass.Bass("TRN2", target_bir_lowering=False)
    io = {}

    def EI(nm, shp, dt):
        return nc.dram_tensor(nm, shp, dt, kind="ExternalInput").ap()

    io["xh"] = EI("xh", [BATCH, T2B, DM], F32)
    io["w_inT"] = EI("w_inT", [DM, PROJ], F32)
    io["conv_wr"] = EI("conv_wr", [128, 34 * 4], F32)
    io["conv_br"] = EI("conv_br", [128, 34], F32)
    io["dt_bias"] = EI("dt_bias", [NH, 1], F32)
    io["cvec"] = EI("cvec", [1, 8], F32)
    io["alog"] = EI("alog", [NH, 1], F32)
    io["w_gate"] = EI("w_gate", [DM, 8], F32)
    io["wi_0"] = EI("wi_0", [8, DM, 1024], F32)
    io["wi_1"] = EI("wi_1", [8, DM, 1024], F32)
    io["wo"] = EI("wo", [8, 1024, DM], F32)
    io["rmask16"] = EI("rmask16", [NH, 16], F32)
    io["invmask16"] = EI("invmask16", [NH, 16], F32)
    io["Drep"] = EI("Drep", [128, 32], F32)
    io["normw"] = EI("normw", [128, 32], F32)
    io["masknegT"] = EI("masknegT", [128, 128], F32)
    io["w_outT"] = EI("w_outT", [IDIM, DM], F32)
    io["wqkvT"] = EI("wqkvT", [DM, 3072], F32)
    io["ffnW"] = EI("ffnW", [DM, DM], F32)
    io["fscale"] = EI("fscale", [1, DM], F32)
    io["wo1T"] = EI("wo1T", [DM, DM], F32)
    io["slrow"] = EI("slrow", [1, 16], F32)

    io["outp"] = nc.dram_tensor("outp", [BATCH, SPC, DM], F32, kind="ExternalOutput").ap()

    def IN(nm, shp, dt, **kw):
        return nc.dram_tensor(nm, shp, dt, kind="Internal", **kw)

    io["d_xnT"] = IN("d_xnT", [DM, T2], BF16).ap()
    io["d_xnTlo"] = IN("d_xnTlo", [DM, T2], BF16).ap()
    io["d_sg"] = IN("d_sg", [IDIM, TPC], BF16).ap()
    io["d_xs"] = IN("d_xs", [IDIM, TPC], BF16).ap()
    io["d_bc"] = IN("d_bc", [2 * NSD, TPC], BF16).ap()
    io["d_dt"] = IN("d_dt", [NH, TPC], F32).ap()
    io["d_xp"] = IN("d_xp", [16, 128, IDIM], BF16).ap()
    io["d_sj"] = IN("d_sj", [16, 128, IDIM], F32).ap()
    io["spn_loc"] = IN("spn_loc", [BATCH * 128, IDIM], F32).ap()
    io["spn_all"] = IN("spn_all", [NCORES * BATCH * 128, IDIM], F32).ap()
    io["dsp_loc"] = IN("dsp_loc", [NH, 8], F32).ap()
    io["dsp_all"] = IN("dsp_all", [NCORES * NH, 8], F32).ap()
    io["d_g"] = IN("d_g", [TPC, DM], BF16).ap()
    io["d_dbg"] = IN("d_dbg", [128, 128], F32).ap()
    io["wi0b_h"] = IN("wi0b", [8, DM, 1024], BF16)
    io["wi1b_h"] = IN("wi1b", [8, DM, 1024], BF16)
    io["wob_h"] = IN("wob", [8, 1024, DM], BF16)
    io["acsp_h"] = IN("acsp", [6, NH, TPC], BF16)
    io["d_yn"] = IN("d_yn", [IDIM, TPC], BF16).ap()
    io["d_x1"] = IN("d_x1", [BATCH, SPC, DM], F32).ap()
    io["d_xn1T"] = IN("d_xn1T", [DM, TPC], BF16).ap()
    io["d_qkv"] = IN("d_qkv", [3072, TPC], BF16).ap()
    io["d_g1"] = IN("d_g1", [TPC, DM], BF16).ap()
    io["ofd_h"] = IN("ofd", [TPC, DM], BF16)

    nc.all_engine_barrier()
    phase1(nc, io)
    nc.reset()
    phase2(nc, io)
    nc.reset()
    phase3(nc, io)
    nc.reset()
    phase4a(nc, io)
    nc.reset()
    phase4b(nc, io)
    nc.reset()
    phase5a(nc, io)
    nc.reset()
    phase5b(nc, io)
    return nc


def _shard_x(x):
    out = []
    xz = np.concatenate([np.zeros((BATCH, HALO, DM), np.float32), np.ascontiguousarray(x)], axis=1)
    for c in range(NCORES):
        out.append(np.ascontiguousarray(xz[:, c * SPC: c * SPC + T2B, :]))
    return out


_MEGA = {}


def kernel(**inputs):
    if "nc" not in _MEGA:
        _MEGA["nc"] = build_mega()
    nc = _MEGA["nc"]
    w_inT = np.ascontiguousarray(inputs["L0_attn_W_in"].T)
    conv_w = np.ascontiguousarray(
        inputs["L0_attn_conv_w"].reshape(34, 128, 4).transpose(1, 0, 2).reshape(128, 136))
    conv_b = np.ascontiguousarray(inputs["L0_attn_conv_b"].reshape(34, 128).T)
    dt_bias = np.ascontiguousarray(inputs["L0_attn_dt_bias"]).reshape(NH, 1)
    xs_sh = _shard_x(inputs["L0_attn_x"])
    cvec = np.array([[EPS_LN, 1.0, 0, 0, 0, 0, 0, 0]], np.float32)
    alog = np.ascontiguousarray(inputs["L0_attn_A_log"]).reshape(NH, 1)
    D = np.asarray(inputs["L0_attn_D"], np.float32)
    Drep = np.ascontiguousarray(np.concatenate(
        [np.tile(D[0::2], (64, 1)), np.tile(D[1::2], (64, 1))], axis=0))
    normw = np.ascontiguousarray(np.asarray(inputs["L0_attn_norm_w"], np.float32)
                                 .reshape(32, 128).T)
    mneg = np.where(np.arange(128)[None, :] >= np.arange(128)[:, None], 0.0,
                    -1e30).astype(np.float32).T.copy()
    w_outT = np.ascontiguousarray(inputs["L0_attn_W_out"].T)
    wqkvT = np.ascontiguousarray(inputs["L1_attn_wqkv"].T)
    wo1T = np.ascontiguousarray(inputs["L1_attn_wo"].T)
    ffnW = np.ascontiguousarray(inputs["L1_ffn_weight"])
    fscale = np.ascontiguousarray(inputs["L1_ffn_scale"]).reshape(1, DM)
    slopes = (2.0 ** (-0.5)) ** np.arange(1, 17)
    slrow = np.ascontiguousarray((-slopes).reshape(1, 16)).astype(np.float32)
    w_gate = np.ascontiguousarray(inputs["L0_ffn_w_gate"])
    wi_0 = np.ascontiguousarray(inputs["L0_ffn_wi_0"])
    wi_1 = np.ascontiguousarray(inputs["L0_ffn_wi_1"])
    wo = np.ascontiguousarray(inputs["L0_ffn_wo"])
    in_maps = []
    for c in range(NCORES):
        rm = np.zeros((NH, 16), np.float32)
        rm[:, 0:2 * c] = 1.0
        in_maps.append({
            "xh": xs_sh[c], "w_inT": w_inT, "conv_wr": conv_w, "conv_br": conv_b,
            "dt_bias": dt_bias, "cvec": cvec, "alog": alog,
            "w_gate": w_gate, "wi_0": wi_0, "wi_1": wi_1, "wo": wo,
            "rmask16": rm, "invmask16": np.ascontiguousarray(1.0 - rm),
            "Drep": Drep, "normw": normw, "masknegT": mneg,
            "w_outT": w_outT, "wqkvT": wqkvT, "ffnW": ffnW, "fscale": fscale,
            "wo1T": wo1T, "slrow": slrow})
    res = run_bass_kernel_spmd(nc, in_maps, core_ids=list(range(NCORES)))
    out = np.concatenate([np.asarray(res.results[c]["outp"]) for c in range(NCORES)], axis=1)
    return np.ascontiguousarray(out.astype(np.float32))


# version 03
# speedup vs baseline: 1.1573x; 1.1573x over previous
import numpy as np
import ml_dtypes
from contextlib import ExitStack
import concourse.bass as bass
import concourse.mybir as mybir
from concourse.bass_utils import run_bass_kernel_spmd

F32 = mybir.dt.float32
BF16 = mybir.dt.bfloat16
AF = mybir.ActivationFunctionType
ALU = mybir.AluOpType
AX = mybir.AxisListType
BF = ml_dtypes.bfloat16

NCORES = 8
BATCH = 2
SEQ = 8192
DM = 2048
SPC = SEQ // NCORES
HALO = 3
T2B = SPC + HALO
T2 = 2 * T2B
TPC = 2 * SPC
PROJ = 8512
IDIM = 4096
NH = 64
HPD = 64
NSD = 128
CONV_DIM = 4352
CK = 128
NCKB = SPC // CK
EPS_LN = 1e-5
EPS_RMS = 1e-5

BTGS = [(0, 512), (512, 512), (1024, 3)]
BTGS = [(0, 512), (512, 512), (1024, 3)]

RG8 = [[0, 1, 2, 3, 4, 5, 6, 7]]
PHX = [0]


class Prog:
    ENGS = ("sync", "vector", "scalar", "tensor", "gpsimd")

    def __init__(self):
        self.items = {e: [] for e in self.ENGS}

    def on(self, eng, fn):
        self.items[eng].append(fn)

    def emit(self, block):
        def mk(name):
            def body(eng):
                for fn in self.items[name]:
                    fn(eng)
            return body
        block.sync(mk("sync"))
        if self.items["vector"]:
            block.vector(mk("vector"))
        if self.items["scalar"]:
            block.scalar(mk("scalar"))
        if self.items["tensor"]:
            block.tensor(mk("tensor"))
        if self.items["gpsimd"]:
            block.gpsimd(mk("gpsimd"))


def phase1(nc, io):
    xh = io["xh"]; w_inT = io["w_inT"]; conv_w = io["conv_wr"]; conv_b = io["conv_br"]
    dt_bias = io["dt_bias"]; cvec = io["cvec"]
    o_xnT = io["d_xnT"]; o_xnTlo = io["d_xnTlo"]; o_sg = io["d_sg"]
    o_xs = io["d_xs"]; o_bc = io["d_bc"]; o_dt = io["d_dt"]

    prog = Prog(); PHX[0] += 1
    with ExitStack() as st:
        _ctr = [0]
        def sb(shape, dt=F32):
            _ctr[0] += 1
            return st.enter_context(nc.sbuf_tensor("sb%d_%d" % (PHX[0], _ctr[0]), list(shape), dt))
        def ps(shape, dt=F32):
            _ctr[0] += 1
            return st.enter_context(nc.psum_tensor("ps%d_%d" % (PHX[0], _ctr[0]), list(shape), dt))
        def sem(name):
            return st.enter_context(nc.semaphore("s%d_%s" % (PHX[0], name)))

        xnT = [sb((128, T2), BF16) for _ in range(16)]
        xnTlo = [sb((128, T2), BF16) for _ in range(16)]
        sx = [sb((128, DM)) for _ in range(2)]
        sxn = [sb((128, DM), BF16) for _ in range(2)]
        sxnlo = [sb((128, DM), BF16) for _ in range(2)]
        sxf = sb((128, DM))
        sqscr = None
        sumc, sqc, meanc, nvc, stdc, rstdc = (sb((128, 1)) for _ in range(6))
        identb = sb((128, 128), BF16)
        iota_p = sb((128, 1), mybir.dt.int32)
        iota_f = sb((128, 128), mybir.dt.int32)
        fidp = sb((128, 1))
        fidf = sb((128, 128))
        wbf = [sb((128, 16 * 128), BF16) for _ in range(2)]
        hbuf = sb((128, T2))
        conv_acc = sb((128, SPC))
        hbf = [sb((128, SPC), BF16) for _ in range(2)]
        sgst = [sb((128, SPC), BF16) for _ in range(2)]
        cw_sb = sb((128, 34 * 4))
        cb_sb = sb((128, 34))
        dtb_sb = sb((NH, 1))
        cv_sb = sb((128, 8))
        dtbuf = sb((NH, T2))
        dtv = dtbuf
        ptr = [ps((128, 128), BF16) for _ in range(2)]
        pproj = [ps((128, 512)) for _ in range(6)]

        ld = sem("ld"); gio = sem("gio"); idn = sem("idn")
        vA = sem("vA"); sA = sem("sA"); xnv = sem("xnv")
        trs = sem("trs"); evs = sem("evs")
        wld = sem("wld"); mmd = sem("mmd"); cons = sem("cons")
        csl = sem("csl"); dst = sem("dst"); dsg = sem("dsg"); dxb = sem("dxb")
        block = st.enter_context(nc.Block())

        def g_iota(e):
            e.iota(iota_p[:], pattern=[[0, 1]], base=0, channel_multiplier=1)
            e.iota(iota_f[:], pattern=[[1, 128]], base=0, channel_multiplier=0).then_inc(gio, 1)
        prog.on("gpsimd", g_iota)

        def v_ident(e):
            e.wait_ge(gio, 1)
            e.tensor_copy(out=fidp[:], in_=iota_p[:])
            e.tensor_copy(out=fidf[:], in_=iota_f[:])
            e.tensor_scalar(out=identb[:], in0=fidf[:], scalar1=fidp[:], scalar2=None,
                            op0=ALU.is_equal).then_inc(idn, 1)
        prog.on("vector", v_ident)

        def s_loads(e):
            e.dma_start(out=cw_sb[:, 0:136], in_=conv_w).then_inc(ld, 16)
            e.dma_start(out=cb_sb[:, 0:34], in_=conv_b).then_inc(ld, 16)
            e.dma_start(out=dtb_sb[:, :], in_=dt_bias).then_inc(ld, 16)
            e.dma_start(out=cv_sb[:, :], in_=cvec.to_broadcast((128, 8))).then_inc(ld, 16)
        prog.on("sync", s_loads)
        n_ld = 64
        C_EPSLN, C_ONE = 0, 1

        ttiles = []
        for b in range(BATCH):
            for i in range(9):
                ttiles.append((b, i * 128, 128 if i < 8 else T2B - 8 * 128))

        n_xn = 0
        n_tr = 0
        n_ev = 0
        n_vA = 0
        n_sA = 0
        gidx = 0

        for it, (b, r0, rows) in enumerate(ttiles):
            buf = it % 2
            need_xn = max(0, it - 1)
            def f_load(e, b=b, r0=r0, rows=rows, buf=buf, need_xn=need_xn, ldv=n_ld + 16):
                e.wait_ge(xnv, need_xn)
                e.dma_start(out=sx[buf][0:rows, :], in_=xh[b, r0:r0 + rows, :]).then_inc(ld, 16)
            prog.on("sync", f_load)
            n_ld += 16
            ldv = n_ld

            n_sA += 1
            def f_sq(e, rows=rows, buf=buf, ldv=ldv):
                e.wait_ge(ld, ldv)
                e.activation(out=hbuf[0:rows, 0:DM], in_=sx[buf][0:rows, :], func=AF.Square,
                             accum_out=sqc[0:rows, :]).then_inc(sA, 1)
            prog.on("scalar", f_sq)
            sAv = n_sA

            n_vA += 2
            def f_stats(e, rows=rows, buf=buf, ldv=ldv, sAv=sAv, w0=n_vA - 1):
                e.wait_ge(ld, ldv)
                e.reduce_sum(out=sumc[0:rows, :], in_=sx[buf][0:rows, :], axis=AX.X).then_inc(vA, 1)
                e.wait_ge(vA, w0)
                e.tensor_scalar(out=meanc[0:rows, :], in0=sumc[0:rows, :], scalar1=1.0 / DM,
                                scalar2=None, op0=ALU.mult)
                e.wait_ge(sA, sAv)
                e.tensor_scalar(out=nvc[0:rows, :], in0=sqc[0:rows, :], scalar1=1.0 / DM,
                                scalar2=None, op0=ALU.mult)
                e.scalar_tensor_tensor(out=nvc[0:rows, :], in0=meanc[0:rows, :],
                                       scalar=meanc[0:rows, :], in1=nvc[0:rows, :],
                                       op0=ALU.mult, op1=ALU.subtract).then_inc(vA, 1)
            prog.on("vector", f_stats)
            vAv = n_vA

            n_sA += 1
            def f_std(e, rows=rows, vAv=vAv):
                e.wait_ge(vA, vAv)
                e.activation(out=stdc[0:rows, :], in_=nvc[0:rows, :], func=AF.Sqrt,
                             scale=-1.0, bias=cv_sb[0:rows, C_EPSLN:C_EPSLN + 1]).then_inc(sA, 1)
            prog.on("scalar", f_std)
            sAv2 = n_sA

            need_tr = 32 * max(0, it - 1)
            n_xn += 1
            def f_xn(e, rows=rows, buf=buf, sAv2=sAv2, need_tr=need_tr):
                e.wait_ge(sA, sAv2)
                e.wait_ge(trs, need_tr)
                e.reciprocal(out=rstdc[0:rows, :], in_=stdc[0:rows, :])
                e.tensor_scalar(out=sxf[0:rows, :], in0=sx[buf][0:rows, :],
                                scalar1=meanc[0:rows, :], scalar2=rstdc[0:rows, :],
                                op0=ALU.subtract, op1=ALU.mult)
                e.tensor_copy(out=sxn[buf][0:rows, :], in_=sxf[0:rows, :])
                e.scalar_tensor_tensor(out=sxnlo[buf][0:rows, :], in0=sxn[buf][0:rows, :],
                                       scalar=-1.0, in1=sxf[0:rows, :],
                                       op0=ALU.mult, op1=ALU.add).then_inc(xnv, 1)
            prog.on("vector", f_xn)
            xnvv = n_xn

            tcol = b * T2B + r0
            for kg in range(16):
                lo = kg >= 8
                src_t = (sxnlo if lo else sxn)
                dst_t = (xnTlo if lo else xnT)
                kb = kg % 8
                need_ev = 2 * gidx
                n_tr += 2
                def f_tr(e, rows=rows, buf=buf, kb=kb, src_t=src_t, xnvv=xnvv, need_ev=need_ev):
                    e.wait_ge(xnv, xnvv)
                    e.wait_ge(idn, 1)
                    e.wait_ge(evs, need_ev)
                    last = None
                    for j in range(2):
                        last = e.transpose(ptr[j][0:128, 0:rows],
                                           src_t[buf][0:rows, (kb * 2 + j) * 128:(kb * 2 + j + 1) * 128],
                                           identb[0:rows, 0:rows])
                    last.then_inc(trs, 2)
                prog.on("tensor", f_tr)
                trv = n_tr
                n_ev += 2
                def f_ev(e, rows=rows, kb=kb, dst_t=dst_t, tcol=tcol, trv=trv):
                    e.wait_ge(trs, trv)
                    last = None
                    for j in range(2):
                        last = e.copy(out=dst_t[kb * 2 + j][:, tcol:tcol + rows],
                                      in_=ptr[j][0:128, 0:rows])
                    last.then_inc(evs, 2)
                prog.on("scalar", f_ev)
                gidx += 1

        all_ev = n_ev
        n_dst = 0
        for k in range(16):
            n_dst += 32
            def f_spill(e, k=k, all_ev=all_ev):
                e.wait_ge(evs, all_ev)
                e.dma_start(out=o_xnT[k * 128:(k + 1) * 128, :], in_=xnT[k][:, :]).then_inc(dst, 16)
                e.dma_start(out=o_xnTlo[k * 128:(k + 1) * 128, :], in_=xnTlo[k][:, :]).then_inc(dst, 16)
            prog.on("sync", f_spill)

        n_wld = 0
        n_mm = 0
        n_cons = 0
        n_csl = 0
        n_dsg = 0
        n_dxb = 0

        for ob in range(67):
            rows = 64 if ob == 66 else 128
            buf = ob % 2
            need_mm = 2 * max(0, ob - 1)
            n_wld += 16
            def f_wdma(e, ob=ob, rows=rows, buf=buf, need_mm=need_mm):
                e.wait_ge(mmd, need_mm)
                e.dma_start(out=wbf[buf][:, 0:16 * rows].rearrange("d (k o) -> d k o", o=rows),
                            in_=w_inT.rearrange("(k d) o -> d k o", d=128)[:, :, ob * 128:ob * 128 + rows]
                            ).then_inc(wld, 16)
            prog.on("gpsimd", f_wdma)
            wldv = n_wld

            for b in range(BATCH):
                u = 2 * ob + b
                pset = u % 2
                pbanks = pproj[pset * 3: pset * 3 + 3]
                need_cons = max(0, u - 1)
                n_mm += 1
                def f_mm(e, ob=ob, rows=rows, buf=buf, b=b, pbanks=pbanks, wldv=wldv,
                         need_cons=need_cons, all_ev=all_ev):
                    e.wait_ge(wld, wldv)
                    e.wait_ge(cons, need_cons)
                    e.wait_ge(evs, all_ev)
                    last = None
                    for tgi, (t0, tw) in enumerate(BTGS):
                        for k in range(16):
                            last = e.matmul(pbanks[tgi][0:rows, 0:tw],
                                            wbf[buf][:, k * rows:(k + 1) * rows],
                                            xnT[k][:, b * T2B + t0: b * T2B + t0 + tw],
                                            start=(k == 0), stop=(k == 15))
                    last.then_inc(mmd, 1)
                prog.on("tensor", f_mm)
                mmv = n_mm

                if ob < 32:
                    gu = 2 * ob + b
                    n_cons += 1
                    def f_gate(e, b=b, pbanks=pbanks, mmv=mmv, gu=gu):
                        e.wait_ge(mmd, mmv)
                        e.wait_ge(dsg, 16 * max(0, gu - 1))
                        e.activation(out=sgst[b][:, 0:509], in_=pbanks[0][:, 3:512], func=AF.Silu)
                        e.activation(out=sgst[b][:, 509:1021], in_=pbanks[1][:, 0:512], func=AF.Silu)
                        e.activation(out=sgst[b][:, 1021:1024], in_=pbanks[2][:, 0:3],
                                     func=AF.Silu).then_inc(cons, 1)
                    prog.on("scalar", f_gate)
                    def f_sgdma(e, ob=ob, b=b, need=n_cons):
                        e.wait_ge(cons, need)
                        e.dma_start(out=o_sg[ob * 128:(ob + 1) * 128, b * SPC:(b + 1) * SPC],
                                    in_=sgst[b][:, :]).then_inc(dsg, 16)
                    prog.on("sync", f_sgdma)
                    n_dsg += 16
                elif ob < 66:
                    hb = ob - 32
                    n_cons += 1
                    def f_hcp(e, b=b, pbanks=pbanks, mmv=mmv):
                        e.wait_ge(mmd, mmv)
                        last = None
                        for tgi, (t0, tw) in enumerate(BTGS):
                            last = e.tensor_copy(out=hbuf[:, b * T2B + t0:b * T2B + t0 + tw],
                                                 in_=pbanks[tgi][:, 0:tw])
                        last.then_inc(cons, 1)
                    prog.on("vector", f_hcp)
                    need_csl = n_csl
                    def f_conv(e, hb=hb, b=b, need_csl=need_csl):
                        e.wait_ge(csl, need_csl)
                        e.tensor_scalar(out=conv_acc[:, :], in0=hbuf[:, b * T2B + 0:b * T2B + SPC],
                                        scalar1=cw_sb[:, hb * 4:hb * 4 + 1],
                                        scalar2=cb_sb[:, hb:hb + 1], op0=ALU.mult, op1=ALU.add)
                        for k in range(1, 4):
                            e.scalar_tensor_tensor(out=conv_acc[:, :],
                                                   in0=hbuf[:, b * T2B + k:b * T2B + k + SPC],
                                                   scalar=cw_sb[:, hb * 4 + k:hb * 4 + k + 1],
                                                   in1=conv_acc[:, :], op0=ALU.mult, op1=ALU.add)
                        e.tensor_copy(out=conv_acc[0:1, 0:1], in_=conv_acc[0:1, 0:1]).then_inc(vA, 1)
                    prog.on("vector", f_conv)
                    n_vA += 1
                    cvv = n_vA
                    hu = 2 * hb + b
                    n_csl += 1
                    def f_hsilu(e, b=b, cvv=cvv, hu=hu):
                        e.wait_ge(vA, cvv)
                        e.wait_ge(dxb, 16 * max(0, hu - 1))
                        e.activation(out=hbf[b][:, :], in_=conv_acc[:, :], func=AF.Silu).then_inc(csl, 1)
                    prog.on("scalar", f_hsilu)
                    cslv = n_csl
                    if hb < 32:
                        tgt = o_xs[hb * 128:(hb + 1) * 128, b * SPC:(b + 1) * SPC]
                    else:
                        tgt = o_bc[(hb - 32) * 128:(hb - 31) * 128, b * SPC:(b + 1) * SPC]
                    def f_hdma(e, tgt=tgt, b=b, cslv=cslv):
                        e.wait_ge(csl, cslv)
                        e.dma_start(out=tgt, in_=hbf[b][:, :]).then_inc(dxb, 16)
                    prog.on("sync", f_hdma)
                    n_dxb += 16
                else:
                    n_cons += 1
                    def f_dtcp(e, b=b, pbanks=pbanks, mmv=mmv, rows=rows):
                        e.wait_ge(mmd, mmv)
                        last = None
                        for tgi, (t0, tw) in enumerate(BTGS):
                            last = e.tensor_copy(out=dtbuf[0:rows, b * T2B + t0:b * T2B + t0 + tw],
                                                 in_=pbanks[tgi][0:rows, 0:tw])
                        last.then_inc(cons, 1)
                    prog.on("vector", f_dtcp)
                    if b == 1:
                        need = n_cons
                        n_sA += 1
                        def f_dtsp(e, need=need):
                            e.wait_ge(cons, need)
                            e.activation(out=dtv[:, :], in_=dtbuf[:, :], func=AF.Exp,
                                         scale=1.0, bias=dtb_sb[:, :])
                            e.activation(out=dtv[:, :], in_=dtv[:, :], func=AF.Ln,
                                         scale=1.0, bias=cv_sb[0:NH, C_ONE:C_ONE + 1]).then_inc(sA, 1)
                        prog.on("scalar", f_dtsp)
                        sfin = n_sA
                        n_dst += 32
                        def f_dtdma(e, sfin=sfin):
                            e.wait_ge(sA, sfin)
                            for bb in range(2):
                                e.dma_start(out=o_dt[:, bb * SPC:(bb + 1) * SPC],
                                            in_=dtv[:, bb * T2B + HALO:bb * T2B + HALO + SPC]
                                            ).then_inc(dst, 16)
                        prog.on("sync", f_dtdma)

        def f_fin(e, a=n_dst, bb=n_dsg, c=n_dxb):
            e.wait_ge(dst, a)
            e.wait_ge(dsg, bb)
            e.wait_ge(dxb, c)
        prog.on("sync", f_fin)
        prog.emit(block)
    return nc



def phase2(nc, io):
    i_xs = io["d_xs"]; i_bc = io["d_bc"]; i_dt = io["d_dt"]; i_alog = io["alog"]
    o_xp = io["d_xp"]; o_sj = io["d_sj"]
    o_span = io["spn_loc"]; o_dspan = io["dsp_loc"]

    prog = Prog(); PHX[0] += 1
    with ExitStack() as st:
        _ctr = [0]
        def sb(shape, dt=F32):
            _ctr[0] += 1
            return st.enter_context(nc.sbuf_tensor("sb%d_%d" % (PHX[0], _ctr[0]), list(shape), dt))
        def ps(shape, dt=F32):
            _ctr[0] += 1
            return st.enter_context(nc.psum_tensor("ps%d_%d" % (PHX[0], _ctr[0]), list(shape), dt))
        def sem(name):
            return st.enter_context(nc.semaphore("s%d_%s" % (PHX[0], name)))

        dtv = sb((NH, TPC))
        alog = sb((NH, 1))
        acol = sb((NH, 1))
        dA = sb((NH, TPC))
        acum = sb((NH, TPC))
        dteh = sb((NH, CK))
        dteF = sb((NH, CK))
        dteT = sb((CK, NH))
        cdrow = sb((1, NH))
        cdB = sb((128, NH))
        onesf = sb((1, 128))
        dsum = sb((NH, BATCH))
        dspan = sb((NH, BATCH))
        identb = sb((128, 128), BF16)
        identf = sb((128, 128), F32)
        iota_p = sb((128, 1), mybir.dt.int32)
        iota_f = sb((128, 128), mybir.dt.int32)
        fidp = sb((128, 1))
        fidf = sb((128, 128))
        bc_sb = sb((128, TPC), BF16)
        xsch = [sb((128, IDIM), BF16) for _ in range(2)]
        xp_sb = [sb((128, IDIM), BF16) for _ in range(2)]
        xpp_sb = sb((128, IDIM), BF16)
        bt_sb = sb((128, 128), BF16)
        ssb = [sb((128, IDIM)) for _ in range(2)]
        tmp_sb = sb((128, 1024))
        SC = [sb((128, IDIM)) for _ in range(BATCH)]
        ptb = [ps((128, 128), BF16) for _ in range(2)]
        pcd = ps((128, 128))
        pst = [ps((128, 1024)) for _ in range(2)]

        ld = sem("ld"); gio = sem("gio")
        vv = sem("vv"); sv = sem("sv"); tv = sem("tv")
        mmv = sem("mmv"); cons = sem("cons"); dst = sem("dst"); dxp = sem("dxp")
        block = st.enter_context(nc.Block())

        n = dict(vv=0, sv=0, tv=0, mm=0, cons=0, dst=0, dxp=0, ld=0)

        def g_iota(e):
            e.iota(iota_p[:], pattern=[[0, 1]], base=0, channel_multiplier=1)
            e.iota(iota_f[:], pattern=[[1, 128]], base=0, channel_multiplier=0).then_inc(gio, 1)
        prog.on("gpsimd", g_iota)

        n["vv"] += 1
        def v_init(e):
            e.wait_ge(gio, 1)
            e.tensor_copy(out=fidp[:], in_=iota_p[:])
            e.tensor_copy(out=fidf[:], in_=iota_f[:])
            e.tensor_scalar(out=identb[:], in0=fidf[:], scalar1=fidp[:], scalar2=None,
                            op0=ALU.is_equal)
            e.tensor_scalar(out=identf[:], in0=fidf[:], scalar1=fidp[:], scalar2=None,
                            op0=ALU.is_equal)
            e.memset(onesf[:], 1.0)
            e.memset(SC[0][:, :], 0.0)
            e.memset(SC[1][:, :], 0.0).then_inc(vv, 1)
        prog.on("vector", v_init)
        init_v = n["vv"]

        def s_loads(e):
            e.dma_start(out=dtv[:, :], in_=i_dt).then_inc(ld, 16)
            e.dma_start(out=alog[:, :], in_=i_alog).then_inc(ld, 16)
            e.dma_start(out=bc_sb[:, :], in_=i_bc[0:128, :]).then_inc(ld, 16)
        prog.on("sync", s_loads)
        n["ld"] += 48

        n["sv"] += 1
        def s_A(e):
            e.wait_ge(ld, 48)
            e.activation(out=acol[:, :], in_=alog[:, :], func=AF.Exp, scale=1.0).then_inc(sv, 1)
        prog.on("scalar", s_A)

        n["vv"] += 3
        def v_dA(e, w0=n["vv"]):
            e.wait_ge(sv, 1)
            e.tensor_scalar(out=dA[:, :], in0=dtv[:, :], scalar1=acol[:, :], scalar2=-1.0,
                            op0=ALU.mult, op1=ALU.mult)
            for u in range(16):
                c0 = u * CK
                e.tensor_tensor_scan(out=acum[:, c0:c0 + CK], data0=dA[:, c0:c0 + CK],
                                     data1=dA[:, c0:c0 + CK], initial=0.0,
                                     op0=ALU.add, op1=ALU.bypass)
            e.reduce_sum(out=dsum[:, 0:1], in_=dA[:, 0:SPC], axis=AX.X).then_inc(vv, 1)
            e.reduce_sum(out=dsum[:, 1:2], in_=dA[:, SPC:TPC], axis=AX.X).then_inc(vv, 1)
            e.wait_ge(vv, w0 - 1)
            e.tensor_copy(out=dsum[0:1, 0:1], in_=dsum[0:1, 0:1]).then_inc(vv, 1)
        prog.on("vector", v_dA)
        acum_v = n["vv"]

        n["sv"] += 1
        def s_dspan(e, w=acum_v):
            e.wait_ge(vv, w)
            e.activation(out=dspan[:, :], in_=dsum[:, :], func=AF.Exp).then_inc(sv, 1)
        prog.on("scalar", s_dspan)
        n["dst"] += 16
        def f_dspan_dma(e, w=n["sv"]):
            e.wait_ge(sv, w)
            e.dma_start(out=o_dspan[:, 0:2], in_=dspan[:, :]).then_inc(dst, 16)
        prog.on("sync", f_dspan_dma)

        trdone = {}
        dxp_after = {}
        sj_after = {}
        sev_after = []
        pcd_free = [n["sv"]]

        for u in range(16):
            b, j = u // NCKB, u % NCKB
            c0 = b * SPC + j * CK
            buf = u % 2

            need_tv = trdone.get(u - 2, 0)
            n["ld"] += 16
            def f_xld(e, c0=c0, buf=buf, need_tv=need_tv):
                e.wait_ge(tv, need_tv)
                e.dma_start(out=xsch[buf][:, :].rearrange("p (pb s) -> p pb s", pb=32),
                            in_=i_xs.rearrange("(pb p) t -> p pb t", p=128)[:, :, c0:c0 + CK]
                            ).then_inc(ld, 16)
            prog.on("sync", f_xld)
            xldv = n["ld"]

            n["sv"] += 1
            def f_dteh(e, c0=c0, w=acum_v):
                e.wait_ge(vv, w)
                e.activation(out=dteh[:, :], in_=acum[:, c0:c0 + CK], func=AF.Exp,
                             scale=-1.0, bias=acum[:, c0 + CK - 1:c0 + CK]).then_inc(sv, 1)
            prog.on("scalar", f_dteh)
            dteh_v = n["sv"]

            n["vv"] += 1
            def f_dteF(e, c0=c0, w=dteh_v):
                e.wait_ge(sv, w)
                e.tensor_tensor(out=dteF[:, :], in0=dteh[:, :], in1=dtv[:, c0:c0 + CK],
                                op=ALU.mult).then_inc(vv, 1)
            prog.on("vector", f_dteF)
            dteF_v = n["vv"]

            n["tv"] += 1
            def f_dtetr(e, w=dteF_v, wfree=pcd_free[0]):
                e.wait_ge(vv, w)
                e.wait_ge(sv, wfree)
                e.transpose(pcd[0:128, 0:NH], dteF[:, :], identf[0:NH, 0:NH]).then_inc(tv, 1)
            prog.on("tensor", f_dtetr)
            dtetr_v = n["tv"]

            n["sv"] += 1
            def f_dteT_ev(e, w=dtetr_v):
                e.wait_ge(tv, w)
                e.copy(out=dteT[:, :], in_=pcd[0:128, 0:NH]).then_inc(sv, 1)
            prog.on("scalar", f_dteT_ev)
            dteT_v = n["sv"]
            pcd_free[0] = dteT_v

            need_dxp = dxp_after.get(u - 2, 0)
            for pairi in range(16):
                evg = n["vv"]
                n["tv"] += 2
                def f_tr(e, buf=buf, pairi=pairi, xldv=xldv, evg=evg, need_dxp=need_dxp):
                    e.wait_ge(ld, xldv)
                    e.wait_ge(vv, evg)
                    e.wait_ge(dxp, need_dxp)
                    last = None
                    for q in range(2):
                        pb = pairi * 2 + q
                        last = e.transpose(ptb[q][0:128, 0:128],
                                           xsch[buf][:, pb * 128:(pb + 1) * 128], identb[:])
                    last.then_inc(tv, 2)
                prog.on("tensor", f_tr)
                trv = n["tv"]
                n["vv"] += 1
                def f_ev(e, buf=buf, pairi=pairi, trv=trv):
                    e.wait_ge(tv, trv)
                    last = None
                    for q in range(2):
                        pb = pairi * 2 + q
                        last = e.tensor_copy(out=xp_sb[buf][:, pb * 128:(pb + 1) * 128],
                                             in_=ptb[q][0:128, 0:128])
                    last.then_inc(vv, 1)
                prog.on("vector", f_ev)
            trdone[u] = n["tv"]
            xptr_v = n["vv"]

            n["dxp"] += 16
            def f_xpdma(e, u=u, buf=buf, w=xptr_v):
                e.wait_ge(vv, w)
                e.dma_start(out=o_xp[u, :, :], in_=xp_sb[buf][:, :]).then_inc(dxp, 16)
            prog.on("sync", f_xpdma)
            dxp_after[u] = n["dxp"]

            need_mm = 4 * u
            n["vv"] += 1
            def f_xpp(e, buf=buf, w2=dteT_v, need_mm=need_mm):
                e.wait_ge(sv, w2)
                e.wait_ge(mmv, need_mm)
                e.tensor_tensor(out=xpp_sb[:, :].rearrange("s (h p) -> s h p", h=NH),
                                in0=xp_sb[buf][:, :].rearrange("s (h p) -> s h p", h=NH),
                                in1=dteT[:, :].unsqueeze(2).to_broadcast((128, NH, HPD)),
                                op=ALU.mult).then_inc(vv, 1)
            prog.on("vector", f_xpp)
            xpp_v = n["vv"]

            n["tv"] += 1
            def f_btr(e, c0=c0, evg=xptr_v):
                e.wait_ge(vv, evg)
                e.transpose(ptb[0][0:128, 0:128], bc_sb[:, c0:c0 + CK], identb[:]).then_inc(tv, 1)
            prog.on("tensor", f_btr)
            btr_v = n["tv"]
            n["vv"] += 1
            def f_btev(e, w=btr_v, need_mm=need_mm):
                e.wait_ge(tv, w)
                e.wait_ge(mmv, need_mm)
                e.tensor_copy(out=bt_sb[:, :], in_=ptb[0][0:128, 0:128]).then_inc(vv, 1)
            prog.on("vector", f_btev)
            bt_v = n["vv"]

            n["tv"] += 1
            def f_cdtr(e, c0=c0, wfree=pcd_free[0], wa=acum_v):
                e.wait_ge(vv, wa)
                e.wait_ge(sv, wfree)
                e.transpose(pcd[0:1, 0:NH], acum[:, c0 + CK - 1:c0 + CK],
                            identf[0:NH, 0:NH]).then_inc(tv, 1)
            prog.on("tensor", f_cdtr)
            cdtr_v = n["tv"]
            n["sv"] += 1
            def f_cdrow(e, w=cdtr_v):
                e.wait_ge(tv, w)
                e.copy(out=cdrow[:, :], in_=pcd[0:1, 0:NH]).then_inc(sv, 1)
            prog.on("scalar", f_cdrow)
            cdrow_v = n["sv"]
            pcd_free[0] = cdrow_v

            n["tv"] += 1
            def f_cdmm(e, w=cdrow_v, wfree=pcd_free[0]):
                e.wait_ge(sv, wfree)
                e.matmul(pcd[0:128, 0:NH], onesf[:, :], cdrow[:, :], start=True,
                         stop=True).then_inc(tv, 1)
            prog.on("tensor", f_cdmm)
            cdmm_v = n["tv"]
            n["sv"] += 1
            need_scans = n["cons"]
            def f_cdB(e, w=cdmm_v, need_scans=need_scans):
                e.wait_ge(tv, w)
                e.wait_ge(cons, need_scans)
                e.activation(out=cdB[:, :], in_=pcd[0:128, 0:NH], func=AF.Exp).then_inc(sv, 1)
            prog.on("scalar", f_cdB)
            cdB_v = n["sv"]
            pcd_free[0] = cdB_v

            need_sj = sj_after.get(u - 2, 0)
            for hg in range(4):
                g = u * 4 + hg
                pz = pst[g % 2]
                n["mm"] += 1
                def f_smm(e, hg=hg, pz=pz, w1=bt_v, w2=xpp_v, g=g,
                          need_cons=max(0, g - 1),
                          need_sev=(sev_after[g - 2] if g >= 2 else 0)):
                    e.wait_ge(vv, max(w1, w2))
                    e.wait_ge(cons, need_cons)
                    e.wait_ge(sv, need_sev)
                    last = None
                    for hh in range(16):
                        h = hg * 16 + hh
                        last = e.matmul(pz[0:128, hh * 64:(hh + 1) * 64], bt_sb[:, :],
                                        xpp_sb[:, h * 64:(h + 1) * 64], start=True, stop=True)
                    last.then_inc(mmv, 1)
                prog.on("tensor", f_smm)
                smm_v = n["mm"]

                n["sv"] += 1
                def f_sev(e, hg=hg, pz=pz, buf=buf, w=smm_v, need_sj=need_sj):
                    e.wait_ge(mmv, w)
                    e.wait_ge(dst, need_sj)
                    e.copy(out=ssb[buf][:, hg * 1024:(hg + 1) * 1024], in_=pz[:, :]).then_inc(sv, 1)
                prog.on("scalar", f_sev)
                sev_after.append(n["sv"])

                n["vv"] += 1
                n["cons"] += 1
                def f_scan(e, b=b, hg=hg, pz=pz, w=smm_v, wcd=cdB_v):
                    e.wait_ge(mmv, w)
                    e.wait_ge(sv, wcd)
                    e.tensor_tensor(out=tmp_sb[:, :].rearrange("n (h p) -> n h p", h=16),
                                    in0=SC[b][:, hg * 1024:(hg + 1) * 1024].rearrange(
                                        "n (h p) -> n h p", h=16),
                                    in1=cdB[:, hg * 16:(hg + 1) * 16].unsqueeze(2)
                                    .to_broadcast((128, 16, HPD)),
                                    op=ALU.mult).then_inc(vv, 1)
                    e.tensor_tensor(out=SC[b][:, hg * 1024:(hg + 1) * 1024], in0=tmp_sb[:, :],
                                    in1=pz[:, :], op=ALU.add).then_inc(cons, 1)
                prog.on("vector", f_scan)

            n["dst"] += 16
            def f_sjdma(e, u=u, buf=buf, w=n["sv"]):
                e.wait_ge(sv, w)
                e.dma_start(out=o_sj[u, :, :], in_=ssb[buf][:, :]).then_inc(dst, 16)
            prog.on("sync", f_sjdma)
            sj_after[u] = n["dst"]

        for b in range(BATCH):
            n["dst"] += 16
            def f_span(e, b=b, w=n["cons"]):
                e.wait_ge(cons, w)
                e.dma_start(out=o_span[b * 128:(b + 1) * 128, :], in_=SC[b][:, :]).then_inc(dst, 16)
            prog.on("sync", f_span)

        def f_fin(e, a=n["dst"], bx=n["dxp"]):
            e.wait_ge(dst, a)
            e.wait_ge(dxp, bx)
        prog.on("sync", f_fin)
        prog.emit(block)
    return nc


def phase3(nc, io):
    i_xnT = io["d_xnT"]; i_xnTlo = io["d_xnTlo"]
    i_wg = io["w_gate"]; i_wi0 = io["wi_0"]; i_wi1 = io["wi_1"]; i_wo = io["wo"]
    o_g = io["d_g"]; o_dbg = io["d_dbg"]

    prog = Prog(); PHX[0] += 1
    with ExitStack() as st:
        _ctr = [0]
        def sb(shape, dt=F32):
            _ctr[0] += 1
            return st.enter_context(nc.sbuf_tensor("sb%d_%d" % (PHX[0], _ctr[0]), list(shape), dt))
        def ps(shape, dt=F32):
            _ctr[0] += 1
            return st.enter_context(nc.psum_tensor("ps%d_%d" % (PHX[0], _ctr[0]), list(shape), dt))
        def sem(name):
            return st.enter_context(nc.semaphore("s%d_%s" % (PHX[0], name)))

        xnT = [sb((128, T2), BF16) for _ in range(16)]
        lo_sb = [sb((128, DM), BF16) for _ in range(2)]
        wgf = sb((128, 128))
        wghi = sb((128, 128), BF16)
        wglo = sb((128, 128), BF16)
        lg = sb((128, 8))
        m1 = sb((128, 1)); nm1 = sb((128, 1)); m2 = sb((128, 1))
        ge1 = sb((128, 8)); lmask = sb((128, 8)); usel = sb((128, 8))
        e2c = sb((128, 1)); denc = sb((128, 1)); recipc = sb((128, 1))
        wall = sb((128, 128))
        w0buf = [sb((128, 16 * 128), BF16) for _ in range(2)]
        w1buf = [sb((128, 16 * 128), BF16) for _ in range(2)]
        wobuf = [sb((128, 8 * 512), BF16) for _ in range(2)]
        sil = sb((128, 512), BF16)
        inter = sb((128, 8 * 1024), BF16)
        g_acc = sb((128, 8 * 2048))
        gst = [sb((128, DM), BF16) for _ in range(2)]
        plog = ps((128, 8))
        ph0 = ps((128, 512))
        ph1 = ps((128, 512))
        pout = [ps((128, 512)) for _ in range(2)]

        gv = sem("gv"); ld = sem("ld"); llo = sem("llo")
        vv = sem("vv"); sv = sem("sv"); mmv = sem("mmv")
        cons = sem("cons"); sps = sem("sps"); dgs = sem("dgs")
        lwi = sem("lwi"); lwo = sem("lwo")
        agS = sem("agS"); agD = sem("agD")
        block = st.enter_context(nc.Block())

        n = dict(gv=0, ld=0, llo=0, vv=0, sv=0, mm=0, cons=0, sps=0, dgs=0, lwi=0, lwo=0)

        def g_ag(e):
            e.collective_compute(
                "AllGather", ALU.bypass, replica_groups=RG8,
                ins=[io["spn_loc"].opt()], outs=[io["spn_all"].opt()]).then_inc(agS)
            e.collective_compute(
                "AllGather", ALU.bypass, replica_groups=RG8,
                ins=[io["dsp_loc"].opt()], outs=[io["dsp_all"].opt()]).then_inc(agD)
        prog.on("gpsimd", g_ag)


        def s_loads(e):
            for k in range(16):
                e.dma_start(out=xnT[k][:, :], in_=i_xnT[k * 128:(k + 1) * 128, :]).then_inc(ld, 16)
            e.dma_start(out=wgf[:, :].rearrange("d (k e8) -> d k e8", e8=8),
                        in_=i_wg.rearrange("(k d) e8 -> d k e8", d=128)).then_inc(ld, 16)
        prog.on("sync", s_loads)
        n["ld"] = 17 * 16

        n["vv"] += 1
        def v_wg(e):
            e.wait_ge(ld, 17 * 16)
            e.tensor_copy(out=wghi[:, :], in_=wgf[:, :])
            e.scalar_tensor_tensor(out=wglo[:, :], in0=wghi[:, :], scalar=-1.0, in1=wgf[:, :],
                                   op0=ALU.mult, op1=ALU.add).then_inc(vv, 1)
        prog.on("vector", v_wg)
        wg_v = n["vv"]

        def tcol_of(tti):
            b, tt = tti // 8, tti % 8
            return b * T2B + HALO + tt * 128

        for tti in range(16):
            tcol = tcol_of(tti)
            buf = tti % 2
            n["llo"] += 16
            def f_lolod(e, tcol=tcol, buf=buf, need_mm=max(0, n["mm"] - 1)):
                e.wait_ge(mmv, need_mm)
                e.dma_start(out=lo_sb[buf][:, :].rearrange("d (k t) -> d k t", k=16),
                            in_=i_xnTlo.rearrange("(k d) t -> d k t", d=128)[:, :, tcol:tcol + 128]
                            ).then_inc(llo, 16)
            prog.on("sync", f_lolod)
            lov = n["llo"]

            n["mm"] += 1
            def f_lgmm(e, tcol=tcol, buf=buf, lov=lov, wgv=wg_v, need_t2=n["vv"]):
                e.wait_ge(llo, lov)
                e.wait_ge(ld, 17 * 16)
                e.wait_ge(vv, need_t2)
                combos = []
                for k in range(16):
                    combos += [(xnT[k][:, tcol:tcol + 128], wghi[:, k * 8:(k + 1) * 8]),
                               (xnT[k][:, tcol:tcol + 128], wglo[:, k * 8:(k + 1) * 8]),
                               (lo_sb[buf][:, k * 128:(k + 1) * 128], wghi[:, k * 8:(k + 1) * 8]),
                               (lo_sb[buf][:, k * 128:(k + 1) * 128], wglo[:, k * 8:(k + 1) * 8])]
                last = None
                for ci, (stat, mov) in enumerate(combos):
                    last = e.matmul(plog[:, :], stat, mov, start=(ci == 0),
                                    stop=(ci == len(combos) - 1))
                last.then_inc(mmv, 1)
            prog.on("tensor", f_lgmm)
            lgv = n["mm"]

            base = n["vv"]
            n["vv"] += 6
            def f_top2a(e, lgv=lgv, base=base):
                e.wait_ge(mmv, lgv)
                e.tensor_copy(out=lg[:, :], in_=plog[:, :]).then_inc(vv, 1)
                e.wait_ge(vv, base + 1)
                e.reduce_max(out=m1[:, :], in_=lg[:, :], axis=AX.X).then_inc(vv, 1)
                e.wait_ge(vv, base + 2)
                e.tensor_scalar(out=ge1[:, :], in0=lg[:, :], scalar1=m1[:, :], scalar2=None,
                                op0=ALU.is_ge)
                e.tensor_scalar(out=nm1[:, :], in0=m1[:, :], scalar1=-1.0, scalar2=None,
                                op0=ALU.mult).then_inc(vv, 1)
                e.wait_ge(vv, base + 3)
                e.scalar_tensor_tensor(out=lmask[:, :], in0=ge1[:, :], scalar=-1e30, in1=lg[:, :],
                                       op0=ALU.mult, op1=ALU.add).then_inc(vv, 1)
                e.wait_ge(vv, base + 4)
                e.reduce_max(out=m2[:, :], in_=lmask[:, :], axis=AX.X).then_inc(vv, 1)
                e.wait_ge(vv, base + 5)
                e.tensor_copy(out=m2[0:1, 0:1], in_=m2[0:1, 0:1]).then_inc(vv, 1)
            prog.on("vector", f_top2a)
            t2a = n["vv"]

            n["sv"] += 1
            def f_top2s(e, w=t2a):
                e.wait_ge(vv, w)
                e.activation(out=usel[:, :], in_=lg[:, :], func=AF.Exp, scale=1.0, bias=nm1[:, :])
                e.activation(out=e2c[:, :], in_=m2[:, :], func=AF.Exp, scale=1.0,
                             bias=nm1[:, :]).then_inc(sv, 1)
            prog.on("scalar", f_top2s)
            t2s = n["sv"]

            base2 = n["vv"]
            n["vv"] += 5
            def f_top2b(e, tti=tti, w=t2s, base2=base2):
                e.wait_ge(sv, w)
                e.tensor_scalar(out=denc[:, :], in0=e2c[:, :], scalar1=1.0, scalar2=None,
                                op0=ALU.add).then_inc(vv, 1)
                e.wait_ge(vv, base2 + 1)
                e.reciprocal(out=recipc[:, :], in_=denc[:, :]).then_inc(vv, 1)
                e.wait_ge(vv, base2 + 2)
                e.tensor_scalar(out=lmask[:, :], in0=lg[:, :], scalar1=m2[:, :], scalar2=None,
                                op0=ALU.is_ge).then_inc(vv, 1)
                e.wait_ge(vv, base2 + 3)
                e.tensor_tensor(out=usel[:, :], in0=usel[:, :], in1=lmask[:, :],
                                op=ALU.mult).then_inc(vv, 1)
                e.wait_ge(vv, base2 + 4)
                e.tensor_scalar(out=wall[:, tti * 8:(tti + 1) * 8], in0=usel[:, :],
                                scalar1=recipc[:, :], scalar2=None, op0=ALU.mult).then_inc(vv, 1)
            prog.on("vector", f_top2b)
        top2_done_v = n["vv"]
        top2_done_mm = n["mm"]

        gdma_after_half = {}
        silu_after_half = {}
        sps_hist = []
        cons_hist = []
        ga_hist = []
        wo_used = []
        hm_after = {}
        out_done = {0: top2_done_mm}

        for half in range(2):
            for ei in range(8):
                for ib in range(8):
                    ibu = (half * 8 + ei) * 8 + ib
                    wb = ibu % 2
                    n["lwi"] += 32
                    def f_wil(e, ei=ei, ib=ib, wb=wb, need_mm=hm_after.get(ibu - 2, 0)):
                        e.wait_ge(mmv, need_mm)
                        e.dma_start(out=w0buf[wb][:, :].rearrange("d (k i) -> d k i", k=16),
                                    in_=i_wi0.rearrange("e8 (k d) i -> e8 d k i", d=128)[
                                        ei, :, :, ib * 128:(ib + 1) * 128]).then_inc(lwi, 16)
                        e.dma_start(out=w1buf[wb][:, :].rearrange("d (k i) -> d k i", k=16),
                                    in_=i_wi1.rearrange("e8 (k d) i -> e8 d k i", d=128)[
                                        ei, :, :, ib * 128:(ib + 1) * 128]).then_inc(lwi, 16)
                    prog.on("gpsimd", f_wil)
                    wiv = n["lwi"]

                    for tg in range(2):
                        tc0 = half * T2B * 0 + (half * T2B) + HALO + tg * 512
                        n["mm"] += 1
                        def f_h0(e, ib=ib, wb=wb, tc0=tc0, wiv=wiv,
                                 need_sps=(sps_hist[-1] if sps_hist else 0),
                                 need_cons=(cons_hist[-1] if cons_hist else 0),
                                 w2v=top2_done_v):
                            e.wait_ge(lwi, wiv)
                            e.wait_ge(vv, w2v)
                            e.wait_ge(sps, need_sps)
                            e.wait_ge(cons, need_cons)
                            last = None
                            for k in range(16):
                                last = e.matmul(ph0[:, :], w0buf[wb][:, k * 128:(k + 1) * 128],
                                                xnT[k][:, tc0:tc0 + 512], start=(k == 0), stop=(k == 15))
                            for k in range(16):
                                last = e.matmul(ph1[:, :], w1buf[wb][:, k * 128:(k + 1) * 128],
                                                xnT[k][:, tc0:tc0 + 512], start=(k == 0), stop=(k == 15))
                            last.then_inc(mmv, 1)
                        prog.on("tensor", f_h0)
                        hv = n["mm"]

                        n["sps"] += 1
                        def f_sil(e, w=hv):
                            e.wait_ge(mmv, w)
                            e.activation(out=sil[:, :], in_=ph0[:, :], func=AF.Silu).then_inc(sps, 1)
                        prog.on("scalar", f_sil)
                        sps_hist.append(n["sps"])

                        n["cons"] += 1
                        def f_int(e, ib=ib, tg=tg, w=hv, ws=n["sps"],
                                  need_out=out_done.get(half * 8 + ei - 1, 0) if True else 0):
                            e.wait_ge(mmv, max(w, need_out))
                            e.wait_ge(sps, ws)
                            e.tensor_tensor(out=inter[:, ib * 1024 + tg * 512: ib * 1024 + (tg + 1) * 512],
                                            in0=sil[:, :], in1=ph1[:, :], op=ALU.mult).then_inc(cons, 1)
                        prog.on("vector", f_int)
                        cons_hist.append(n["cons"])
                    hm_after[ibu] = n["mm"]

                inter_done = n["cons"]
                for dg in range(4):
                    dgu = ((half * 8 + ei) * 4 + dg)
                    wb = dgu % 2
                    n["lwo"] += 16
                    def f_wol(e, ei=ei, dg=dg, wb=wb,
                              need_mm=(wo_used[dgu - 2] if dgu >= 2 else 0)):
                        e.wait_ge(mmv, need_mm)
                        e.dma_start(out=wobuf[wb][:, :].rearrange("i (ib d) -> i ib d", ib=8),
                                    in_=i_wo.rearrange("e8 (ib i) d -> e8 i ib d", i=128)[
                                        ei, :, :, dg * 512:(dg + 1) * 512]).then_inc(lwo, 16)
                    prog.on("gpsimd", f_wol)
                    wov = n["lwo"]

                    for tt in range(8):
                        pz = pout[(dgu * 8 + tt) % 2]
                        n["mm"] += 1
                        def f_omm(e, tt=tt, wb=wb, pz=pz, wov=wov, wint=inter_done,
                                  need_ga=(ga_hist[-2] if len(ga_hist) >= 2 else 0)):
                            e.wait_ge(lwo, wov)
                            e.wait_ge(cons, wint)
                            e.wait_ge(vv, need_ga)
                            last = None
                            for ib in range(8):
                                last = e.matmul(pz[:, :], inter[:, ib * 1024 + tt * 128: ib * 1024 + tt * 128 + 128],
                                                wobuf[wb][:, ib * 512:(ib + 1) * 512],
                                                start=(ib == 0), stop=(ib == 7))
                            last.then_inc(mmv, 1)
                        prog.on("tensor", f_omm)
                        ov = n["mm"]

                        n["vv"] += 1
                        def f_ga(e, half=half, ei=ei, dg=dg, tt=tt, pz=pz, w=ov,
                                 need_dg=gdma_after_half.get(half - 1, 0)):
                            e.wait_ge(mmv, w)
                            e.wait_ge(dgs, need_dg)
                            tti = half * 8 + tt
                            wcol = wall[:, tti * 8 + ei: tti * 8 + ei + 1]
                            tgt = g_acc[:, tt * 2048 + dg * 512: tt * 2048 + (dg + 1) * 512]
                            if ei == 0:
                                e.tensor_scalar(out=tgt, in0=pz[:, :], scalar1=wcol, scalar2=None,
                                                op0=ALU.mult).then_inc(vv, 1)
                            else:
                                e.scalar_tensor_tensor(out=tgt, in0=pz[:, :], scalar=wcol, in1=tgt,
                                                       op0=ALU.mult, op1=ALU.add).then_inc(vv, 1)
                        prog.on("vector", f_ga)
                        ga_hist.append(n["vv"])
                    wo_used.append(n["mm"])
                out_done[half * 8 + ei] = n["mm"]

            ga_done = n["vv"]
            for tt in range(8):
                gb = tt % 2
                n["sv"] += 1
                def f_gsil(e, tt=tt, gb=gb, w=ga_done, need_dg=max(0, n["dgs"] - 16)):
                    e.wait_ge(vv, w)
                    e.wait_ge(dgs, need_dg)
                    e.activation(out=gst[gb][:, :], in_=g_acc[:, tt * 2048:(tt + 1) * 2048],
                                 func=AF.Silu).then_inc(sv, 1)
                prog.on("scalar", f_gsil)
                gsv = n["sv"]
                n["dgs"] += 16
                def f_gdma(e, half=half, tt=tt, gb=gb, w=gsv):
                    e.wait_ge(sv, w)
                    e.dma_start(out=o_g[(half * 8 + tt) * 128:(half * 8 + tt + 1) * 128, :],
                                in_=gst[gb][:, :]).then_inc(dgs, 16)
                prog.on("sync", f_gdma)
            gdma_after_half[half] = n["dgs"]

        def f_dbg(e, w=top2_done_v):
            e.wait_ge(vv, w)
            e.dma_start(out=o_dbg, in_=wall[:, :]).then_inc(dgs, 16)
        prog.on("sync", f_dbg)
        n["dgs"] += 16

        def f_fin(e, a=n["dgs"]):
            e.wait_ge(dgs, a)
        prog.on("sync", f_fin)

        def g_agw(e):
            e.wait_ge(agS, 1)
            e.wait_ge(agD, 1)
        prog.on("gpsimd", g_agw)
        prog.emit(block)
    return nc


def phase4a(nc, io):
    i_xp = io["d_xp"]; i_sj = io["d_sj"]; i_xs = io["d_xs"]; i_bc = io["d_bc"]
    i_sg = io["d_sg"]; i_dt = io["d_dt"]; i_alog = io["alog"]
    i_sall = io["spn_all"]; i_dall = io["dsp_all"]
    i_rm = io["rmask16"]; i_im = io["invmask16"]; i_drep = io["Drep"]
    i_nw = io["normw"]; i_mneg = io["masknegT"]; i_cv = io["cvec"]
    o_yn = io["d_yn"]; acsp = io["acsp_h"]

    prog = Prog(); PHX[0] += 1
    with ExitStack() as st:
        _ctr = [0]
        def sb(shape, dt=F32):
            _ctr[0] += 1
            return st.enter_context(nc.sbuf_tensor("sb%d_%d" % (PHX[0], _ctr[0]), list(shape), dt))
        def ps(shape, dt=F32):
            _ctr[0] += 1
            return st.enter_context(nc.psum_tensor("ps%d_%d" % (PHX[0], _ctr[0]), list(shape), dt))
        def sem(name):
            return st.enter_context(nc.semaphore("s%d_%s" % (PHX[0], name)))

        dtv = sb((NH, TPC))
        alog = sb((NH, 1))
        acum = sb((NH, TPC))
        rsplit = sb((NH, TPC))
        lndt = rsplit
        splA = sb((NH, TPC), BF16)
        splB = sb((NH, TPC), BF16)
        dall = sb((NH, 16)); rm16 = sb((NH, 16)); im16 = sb((NH, 16))
        dprm = sb((NH, 16)); qsuf = sb((NH, 16)); qm = sb((NH, 16))
        qrow = sb((1, NH))
        cdrow = sb((1, NH)); cdB = sb((128, NH))
        lndtT = sb((CK, NH))
        drep = sb((128, 32)); nw = sb((128, 32)); cv = sb((128, 8))
        mnegf = sb((128, 128)); mnegb = sb((128, 128), BF16)
        ones3 = sb((3, 128), BF16)
        onesf = sb((1, 128))
        onescol = sb((128, 1), BF16)
        identb = sb((128, 128), BF16)
        identf = sb((128, 128), F32)
        iota_p = sb((128, 1), mybir.dt.int32); iota_f = sb((128, 128), mybir.dt.int32)
        fidp = sb((128, 1)); fidf = sb((128, 128))
        b_sb = sb((128, TPC), BF16); c_sb = sb((128, TPC), BF16)
        SC = [sb((128, IDIM)) for _ in range(BATCH)]
        prevb = sb((128, IDIM), BF16)
        sld = sb((128, IDIM))
        tmp_sb = sb((128, IDIM))
        xpch = sb((128, IDIM), BF16)
        sjch = sb((128, IDIM))
        xsch = sb((128, IDIM), BF16)
        sgch = sb((128, IDIM), BF16)
        stat6 = sb((6, 16 * CK), BF16)
        mov6 = sb((6, 16 * CK), BF16)
        etop = sb((128, 512), BF16)
        ebot = sb((128, 512), BF16)
        mtop = sb((128, 512), BF16)
        mbot = sb((128, 512), BF16)
        cbt = sb((128, CK), BF16)
        ych = sb((128, IDIM))
        ysq = sb((128, IDIM), BF16)
        ynst = ysq
        stdrow = sb((1, CK)); rstdrow = sb((1, CK))
        pcd = ps((128, 128))
        pcb = ps((128, 128))
        pd = ps((128, 512))
        pr = ps((128, 512))
        pY = [ps((128, 512)) for _ in range(2)]
        prms = ps((1, CK))
        prstdB = ps((128, CK))

        ld = sem("ld"); gio = sem("gio")
        vv = sem("vv"); sv = sem("sv"); tv = sem("tv"); mmv = sem("mmv")
        dyn = sem("dyn"); fin = sem("fin")
        block = st.enter_context(nc.Block())

        n = dict(ld=0, vv=0, sv=0, tv=0, mm=0, dyn=0, fin=0)

        def vstep(e, instr):
            n["vv"] += 1
        def g_iota(e):
            e.iota(iota_p[:], pattern=[[0, 1]], base=0, channel_multiplier=1)
            e.iota(iota_f[:], pattern=[[1, 128]], base=0, channel_multiplier=0).then_inc(gio, 1)
        prog.on("gpsimd", g_iota)

        def s_loads(e):
            for t, s in [(dtv[:, :], i_dt), (alog[:, :], i_alog),
                         (rm16[:, :], i_rm), (im16[:, :], i_im), (drep[:, :], i_drep),
                         (nw[:, :], i_nw), (mnegf[:, :], i_mneg),
                         (b_sb[:, :], i_bc[0:128, :]), (c_sb[:, :], i_bc[128:256, :])]:
                e.dma_start(out=t, in_=s).then_inc(ld, 16)
            for j in range(NCORES):
                e.dma_start(out=dall[:, 2 * j:2 * j + 2],
                            in_=i_dall[j * NH:(j + 1) * NH, 0:2]).then_inc(ld, 16)
            e.dma_start(out=cv[:, :], in_=i_cv.to_broadcast((128, 8))).then_inc(ld, 16)
        prog.on("sync", s_loads)
        n["ld"] = 18 * 16
        ld0 = n["ld"]

        n["vv"] += 1
        def v_init(e):
            e.wait_ge(gio, 1)
            e.wait_ge(ld, ld0)
            e.tensor_copy(out=fidp[:], in_=iota_p[:])
            e.tensor_copy(out=fidf[:], in_=iota_f[:])
            e.tensor_scalar(out=identb[:], in0=fidf[:], scalar1=fidp[:], scalar2=None, op0=ALU.is_equal)
            e.tensor_scalar(out=identf[:], in0=fidf[:], scalar1=fidp[:], scalar2=None, op0=ALU.is_equal)
            e.memset(ones3[:, :], 1.0)
            e.memset(onesf[:, :], 1.0)
            e.memset(onescol[:, :], 1.0)
            e.memset(stat6[:, :], 1.0)
            e.memset(mov6[:, :], 1.0)
            e.tensor_copy(out=mnegb[:, :], in_=mnegf[:, :]).then_inc(vv, 1)
        prog.on("vector", v_init)
        init_v = n["vv"]

        n["sv"] += 1
        def s_A(e):
            e.wait_ge(ld, ld0)
            e.activation(out=alog[:, :], in_=alog[:, :], func=AF.Exp, scale=1.0).then_inc(sv, 1)
        prog.on("scalar", s_A)

        bse = n["vv"]
        n["vv"] += 2
        def v_acum(e, bse=bse):
            e.wait_ge(sv, 1)
            e.tensor_scalar(out=rsplit[:, :], in0=dtv[:, :], scalar1=alog[:, :], scalar2=-1.0,
                            op0=ALU.mult, op1=ALU.mult).then_inc(vv, 1)
            e.wait_ge(vv, bse + 1)
            for u in range(16):
                c0 = u * CK
                e.tensor_tensor_scan(out=acum[:, c0:c0 + CK], data0=rsplit[:, c0:c0 + CK],
                                     data1=rsplit[:, c0:c0 + CK], initial=0.0,
                                     op0=ALU.add, op1=ALU.bypass)
            e.tensor_copy(out=rsplit[:, :], in_=acum[:, :]).then_inc(vv, 1)
        prog.on("vector", v_acum)
        for q in range(3):
            bq = n["vv"]
            n["vv"] += 3
            need_ld = n["ld"]
            def v_split(e, bq=bq, need_ld=need_ld):
                e.wait_ge(vv, bq)
                e.wait_ge(ld, need_ld)
                e.tensor_copy(out=splA[:, :], in_=rsplit[:, :]).then_inc(vv, 1)
                e.wait_ge(vv, bq + 1)
                e.tensor_scalar(out=splB[:, :], in0=splA[:, :], scalar1=-1.0, scalar2=None,
                                op0=ALU.mult)
                e.scalar_tensor_tensor(out=rsplit[:, :], in0=splA[:, :], scalar=-1.0,
                                       in1=rsplit[:, :], op0=ALU.mult,
                                       op1=ALU.add).then_inc(vv, 1)
                e.wait_ge(vv, bq + 2)
                e.tensor_copy(out=splB[0:1, 0:1], in_=splB[0:1, 0:1]).then_inc(vv, 1)
            prog.on("vector", v_split)
            spq = n["vv"]
            n["ld"] += 32
            def f_spdma(e, q=q, w=spq):
                e.wait_ge(vv, w)
                e.dma_start(out=acsp[q, :, :], in_=splA[:, :]).then_inc(ld, 16)
                e.dma_start(out=acsp[3 + q, :, :], in_=splB[:, :]).then_inc(ld, 16)
            prog.on("sync", f_spdma)
        spl_v = n["vv"]
        acsp_ld = n["ld"]

        n["sv"] += 1
        def s_lndt(e, w=spl_v):
            e.wait_ge(vv, w)
            e.activation(out=lndt[:, :], in_=dtv[:, :], func=AF.Ln, scale=1.0).then_inc(sv, 1)
        prog.on("scalar", s_lndt)
        lndt_v = n["sv"]

        bse3 = n["vv"]
        n["vv"] += 3 + 7 * 2 + 1
        def v_relay(e, bse3=bse3):
            e.wait_ge(ld, ld0)
            e.wait_ge(vv, init_v)
            k = bse3
            e.tensor_tensor(out=dprm[:, :], in0=dall[:, :], in1=rm16[:, :], op=ALU.mult).then_inc(vv, 1); k += 1
            e.wait_ge(vv, k)
            e.tensor_tensor(out=dprm[:, :], in0=dprm[:, :], in1=im16[:, :], op=ALU.add).then_inc(vv, 1); k += 1
            e.wait_ge(vv, k)
            e.memset(qsuf[:, 14:16], 1.0).then_inc(vv, 1); k += 1
            for j in range(6, -1, -1):
                e.wait_ge(vv, k)
                e.tensor_tensor(out=qsuf[:, 2 * j:2 * j + 2], in0=qsuf[:, 2 * j + 2:2 * j + 4],
                                in1=dprm[:, 2 * j + 2:2 * j + 4], op=ALU.mult).then_inc(vv, 1); k += 1
                e.wait_ge(vv, k)
                e.tensor_copy(out=qsuf[0:1, 0:1], in_=qsuf[0:1, 0:1]).then_inc(vv, 1); k += 1
            e.wait_ge(vv, k)
            e.tensor_tensor(out=qm[:, :], in0=qsuf[:, :], in1=rm16[:, :], op=ALU.mult).then_inc(vv, 1)
        prog.on("vector", v_relay)
        qm_v = n["vv"]

        sall_done = {}
        pcd_free = [0]
        for b in range(BATCH):
            for j in range(NCORES):
                it = b * NCORES + j
                n["ld"] += 16
                def f_sld(e, b=b, j=j, need_vv=sall_done.get(it - 1, 0)):
                    e.wait_ge(vv, need_vv)
                    e.dma_start(out=sld[:, :],
                                in_=i_sall[j * 256 + b * 128:j * 256 + b * 128 + 128, :]
                                ).then_inc(ld, 16)
                prog.on("sync", f_sld)
                sldv = n["ld"]
                n["tv"] += 1
                def f_qtr(e, b=b, j=j, w=qm_v, wfree=pcd_free[0],
                          wterm=sall_done.get(it - 1, 0)):
                    e.wait_ge(vv, max(w, wterm))
                    e.wait_ge(sv, wfree)
                    e.transpose(pcd[0:1, 0:NH], qm[:, 2 * j + b:2 * j + b + 1],
                                identf[0:NH, 0:NH]).then_inc(tv, 1)
                prog.on("tensor", f_qtr)
                qtr_v = n["tv"]
                n["sv"] += 1
                def f_qrow(e, w=qtr_v):
                    e.wait_ge(tv, w)
                    e.copy(out=qrow[:, :], in_=pcd[0:1, 0:NH]).then_inc(sv, 1)
                prog.on("scalar", f_qrow)
                qrow_v = n["sv"]
                pcd_free[0] = qrow_v
                n["tv"] += 1
                def f_qmm(e, w=qrow_v):
                    e.wait_ge(sv, w)
                    e.matmul(pcd[0:128, 0:NH], onesf[:, :], qrow[:, :], start=True,
                             stop=True).then_inc(tv, 1)
                prog.on("tensor", f_qmm)
                qmm_v = n["tv"]
                n["vv"] += 2
                def f_term(e, b=b, j=j, w1=qmm_v, w2=sldv, k0=n["vv"] - 1):
                    e.wait_ge(tv, w1)
                    e.wait_ge(ld, w2)
                    e.tensor_tensor(out=tmp_sb[:, :].rearrange("x (h p) -> x h p", h=NH),
                                    in0=sld[:, :].rearrange("x (h p) -> x h p", h=NH),
                                    in1=pcd[0:128, 0:NH].unsqueeze(2).to_broadcast((128, NH, HPD)),
                                    op=ALU.mult).then_inc(vv, 1)
                    e.wait_ge(vv, k0)
                    if j == 0:
                        e.tensor_copy(out=SC[b][:, :], in_=tmp_sb[:, :]).then_inc(vv, 1)
                    else:
                        e.tensor_tensor(out=SC[b][:, :], in0=SC[b][:, :], in1=tmp_sb[:, :],
                                        op=ALU.add).then_inc(vv, 1)
                prog.on("vector", f_term)
                sall_done[it] = n["vv"]
                pcd_free[0] = qrow_v
        state_v = n["vv"]

        ych_done = {}
        yn_dma = {}
        for u in range(16):
            b, j = u // NCKB, u % NCKB
            c0 = b * SPC + j * CK
            prev_fin = ych_done.get(u - 1, state_v)

            lds = {}
            for nm, dst, srcap in [("xp", xpch, i_xp[u, :, :]),
                                   ("sj", sjch, i_sj[u, :, :])]:
                n["ld"] += 16
                def f_cl(e, dst=dst, srcap=srcap, w=prev_fin):
                    e.wait_ge(fin, 0)
                    e.wait_ge(vv, w)
                    e.dma_start(out=dst[:, :], in_=srcap).then_inc(ld, 16)
                prog.on("sync", f_cl)
                lds[nm] = n["ld"]
            for nm, dst, src in [("xs", xsch, i_xs), ("sg", sgch, i_sg)]:
                n["ld"] += 16
                def f_cl2(e, dst=dst, src=src, c0=c0, w=prev_fin):
                    e.wait_ge(vv, w)
                    e.dma_start(out=dst[:, :].rearrange("p (pb s) -> p pb s", pb=32),
                                in_=src.rearrange("(pb p) t -> p pb t", p=128)[:, :, c0:c0 + CK]
                                ).then_inc(ld, 16)
                prog.on("sync", f_cl2)
                lds[nm] = n["ld"]

            n["tv"] += 1
            def f_cdtr(e, c0=c0, w=spl_v, wfree=pcd_free[0], wp=prev_fin):
                e.wait_ge(vv, max(w, wp))
                e.wait_ge(sv, wfree)
                e.transpose(pcd[0:1, 0:NH], acum[:, c0 + CK - 1:c0 + CK],
                            identf[0:NH, 0:NH]).then_inc(tv, 1)
            prog.on("tensor", f_cdtr)
            cdtr_v = n["tv"]
            n["sv"] += 1
            def f_cdrow(e, w=cdtr_v):
                e.wait_ge(tv, w)
                e.copy(out=cdrow[:, :], in_=pcd[0:1, 0:NH]).then_inc(sv, 1)
            prog.on("scalar", f_cdrow)
            cdrow_v = n["sv"]
            n["tv"] += 1
            def f_cdmm(e, w=cdrow_v):
                e.wait_ge(sv, w)
                e.matmul(pcd[0:128, 0:NH], onesf[:, :], cdrow[:, :], start=True,
                         stop=True).then_inc(tv, 1)
            prog.on("tensor", f_cdmm)
            cdmm_v = n["tv"]
            n["sv"] += 1
            def f_cdB(e, w=cdmm_v):
                e.wait_ge(tv, w)
                e.activation(out=cdB[:, :], in_=pcd[0:128, 0:NH], func=AF.Exp).then_inc(sv, 1)
            prog.on("scalar", f_cdB)
            cdB_v = n["sv"]
            pcd_free[0] = cdB_v

            n["tv"] += 1
            def f_ldtr(e, c0=c0, w=lndt_v, wfree=pcd_free[0]):
                e.wait_ge(sv, max(w, wfree))
                e.transpose(pcd[0:128, 0:NH], lndt[:, c0:c0 + CK], identf[0:NH, 0:NH]).then_inc(tv, 1)
            prog.on("tensor", f_ldtr)
            ldtr_v = n["tv"]
            n["sv"] += 1
            def f_ldT(e, w=ldtr_v):
                e.wait_ge(tv, w)
                e.copy(out=lndtT[:, :], in_=pcd[0:128, 0:NH]).then_inc(sv, 1)
            prog.on("scalar", f_ldT)
            lndtT_v = n["sv"]
            pcd_free[0] = lndtT_v

            bse4 = n["vv"]
            n["vv"] += 3
            def f_scan(e, b=b, bse4=bse4, wcd=cdB_v, wsj=lds["sj"], wp=prev_fin):
                e.wait_ge(vv, wp)
                e.wait_ge(sv, wcd)
                e.wait_ge(ld, wsj)
                e.tensor_copy(out=prevb[:, :], in_=SC[b][:, :]).then_inc(vv, 1)
                e.wait_ge(vv, bse4 + 1)
                e.tensor_tensor(out=tmp_sb[:, :].rearrange("x (h p) -> x h p", h=NH),
                                in0=SC[b][:, :].rearrange("x (h p) -> x h p", h=NH),
                                in1=cdB[:, :].unsqueeze(2).to_broadcast((128, NH, HPD)),
                                op=ALU.mult).then_inc(vv, 1)
                e.wait_ge(vv, bse4 + 2)
                e.tensor_tensor(out=SC[b][:, :], in0=tmp_sb[:, :], in1=sjch[:, :],
                                op=ALU.add).then_inc(vv, 1)
            prog.on("vector", f_scan)
            scan_v = n["vv"]

            n["tv"] += 1
            def f_cbmm(e, c0=c0, w=init_v, wcons=ych_done.get(u - 1, 0)):
                e.wait_ge(vv, max(w, wcons))
                e.matmul(pcb[0:128, 0:CK], b_sb[:, c0:c0 + CK], c_sb[:, c0:c0 + CK],
                         start=True, stop=True).then_inc(tv, 1)
            prog.on("tensor", f_cbmm)
            cb_v = n["tv"]
            n["vv"] += 1
            def f_cbev(e, w=cb_v):
                e.wait_ge(tv, w)
                e.tensor_copy(out=cbt[:, :], in_=pcb[0:128, 0:CK]).then_inc(vv, 1)
            prog.on("vector", f_cbev)
            cbt_v = n["vv"]

            for hb in range(4):
                need_mm = n["mm"]
                n["ld"] += 32
                def f_smld(e, c0=c0, hb=hb, need_mm=need_mm, wsp=acsp_ld, wi=init_v):
                    e.wait_ge(ld, wsp)
                    e.wait_ge(vv, wi)
                    e.wait_ge(mmv, need_mm)
                    e.dma_start(out=stat6[3:6, :].rearrange("q (h s) -> q h s", h=16),
                                in_=acsp.ap().rearrange("q h t -> q h t")[3:6, hb * 16:(hb + 1) * 16,
                                                                          c0:c0 + CK]).then_inc(ld, 16)
                    e.dma_start(out=mov6[0:3, :].rearrange("q (h s) -> q h s", h=16),
                                in_=acsp.ap()[0:3, hb * 16:(hb + 1) * 16, c0:c0 + CK]).then_inc(ld, 16)
                prog.on("sync", f_smld)
                smld_v = n["ld"]


                for sub in range(4):
                    g4 = hb * 4 + sub
                    n["mm"] += 1
                    def f_dmm(e, sub=sub, w=smld_v, wE=n["sv"], winit=init_v):
                        e.wait_ge(ld, w)
                        e.wait_ge(vv, winit)
                        e.wait_ge(sv, wE)
                        last = None
                        for hh in range(4):
                            hcol = sub * 4 + hh
                            last = e.matmul(pd[:, hh * 128:(hh + 1) * 128],
                                            stat6[:, hcol * 128:(hcol + 1) * 128],
                                            mov6[:, hcol * 128:(hcol + 1) * 128],
                                            start=True, stop=False)
                            last = e.matmul(pd[:, hh * 128:(hh + 1) * 128], mnegb[:, :],
                                            identb[:, :], start=False, stop=True)
                            last = e.matmul(pr[:, hh * 128:(hh + 1) * 128], ones3[:, :],
                                            mov6[0:3, hcol * 128:(hcol + 1) * 128],
                                            start=True, stop=True)
                        last.then_inc(mmv, 1)
                    prog.on("tensor", f_dmm)
                    dmm_v = n["mm"]

                    n["sv"] += 1
                    def f_exps(e, hb=hb, sub=sub, w=dmm_v, wl=lndtT_v, wcons=n["vv"]):
                        e.wait_ge(mmv, w)
                        e.wait_ge(sv, wl)
                        e.wait_ge(vv, wcons)
                        for hh in range(4):
                            h = hb * 16 + sub * 4 + hh
                            e.activation(out=etop[:, hh * 128:(hh + 1) * 128],
                                         in_=pd[:, hh * 128:(hh + 1) * 128], func=AF.Exp,
                                         scale=1.0, bias=lndtT[:, h:h + 1])
                        e.activation(out=ebot[:, :], in_=pr[:, :], func=AF.Exp).then_inc(sv, 1)
                    prog.on("scalar", f_exps)
                    exps_v = n["sv"]

                    bse5 = n["vv"]
                    n["vv"] += 2
                    def f_mtt(e, w=exps_v, wcb=cbt_v, bse5=bse5, wY=n["mm"], c0=c0):
                        e.wait_ge(sv, w)
                        e.wait_ge(vv, wcb)
                        e.wait_ge(mmv, wY)
                        e.tensor_tensor(out=mtop[:, :].rearrange("s (q t) -> s q t", q=4),
                                        in0=etop[:, :].rearrange("s (q t) -> s q t", q=4),
                                        in1=cbt[:, :].unsqueeze(1).to_broadcast((128, 4, CK)),
                                        op=ALU.mult).then_inc(vv, 1)
                        e.wait_ge(vv, bse5 + 1)
                        e.tensor_tensor(out=mbot[:, :].rearrange("n (q t) -> n q t", q=4),
                                        in0=ebot[:, :].rearrange("n (q t) -> n q t", q=4),
                                        in1=c_sb[:, c0:c0 + CK].unsqueeze(1).to_broadcast((128, 4, CK)),
                                        op=ALU.mult).then_inc(vv, 1)
                    prog.on("vector", f_mtt)
                    mtt_v = n["vv"]

                    pz = pY[(g4 // 2) % 2]
                    n["mm"] += 1
                    def f_ymm(e, hb=hb, sub=sub, pz=pz, w=mtt_v, wx=lds["xp"], wsc=scan_v,
                              wev=n["vv"]):
                        e.wait_ge(vv, w)
                        e.wait_ge(ld, wx)
                        e.wait_ge(vv, wsc)
                        last = None
                        for hh in range(4):
                            h = hb * 16 + sub * 4 + hh
                            ph = h % 2
                            col = ((h % 8) // 2) * 128
                            tp = (0, 64) if ph else None
                            last = e.matmul(pz[ph * 64:ph * 64 + 64, col:col + CK],
                                            xpch[:, h * 64:(h + 1) * 64],
                                            mtop[:, hh * 128:(hh + 1) * 128],
                                            start=True, stop=False, tile_position=tp)
                            last = e.matmul(pz[ph * 64:ph * 64 + 64, col:col + CK],
                                            prevb[:, h * 64:(h + 1) * 64],
                                            mbot[:, hh * 128:(hh + 1) * 128],
                                            start=False, stop=True, tile_position=tp)
                        last.then_inc(mmv, 1)
                    prog.on("tensor", f_ymm)
                    ymm_v = n["mm"]

                    if sub % 2 == 1:
                        gg = hb * 2 + sub // 2
                        n["vv"] += 2
                        def f_yev(e, u=u, gg=gg, pz=pz, w=ymm_v, wxs=lds["xs"], k0=n["vv"] - 1):
                            e.wait_ge(mmv, w)
                            e.wait_ge(ld, wxs)
                            e.tensor_tensor(
                                out=tmp_sb[:, 0:512].rearrange("p (q s) -> p q s", q=4),
                                in0=xsch[:, gg * 512:(gg + 1) * 512].rearrange("p (q s) -> p q s", q=4),
                                in1=drep[:, gg * 4:(gg + 1) * 4].unsqueeze(2).to_broadcast((128, 4, CK)),
                                op=ALU.mult).then_inc(vv, 1)
                            e.wait_ge(vv, k0)
                            e.tensor_tensor(out=ych[:, gg * 512:(gg + 1) * 512],
                                            in0=tmp_sb[:, 0:512], in1=pz[:, :],
                                            op=ALU.add).then_inc(vv, 1)
                        prog.on("vector", f_yev)
            ych_v = n["vv"]

            bse6 = n["vv"]
            n["vv"] += 1
            def f_yg(e, w=lds["sg"], bse6=bse6):
                e.wait_ge(ld, w)
                e.tensor_tensor(out=ych[:, :], in0=ych[:, :], in1=sgch[:, :],
                                op=ALU.mult).then_inc(vv, 1)
            prog.on("vector", f_yg)
            yg_v = n["vv"]

            n["sv"] += 1
            def f_ysq(e, w=yg_v, need_dyn=yn_dma.get(u - 1, 0)):
                e.wait_ge(vv, w)
                e.wait_ge(dyn, need_dyn)
                e.activation(out=ysq[:, :], in_=ych[:, :], func=AF.Square).then_inc(sv, 1)
            prog.on("scalar", f_ysq)
            ysq_v = n["sv"]

            n["mm"] += 1
            def f_rmsmm(e, w=ysq_v, wfree=n["sv"]):
                e.wait_ge(sv, w)
                last = None
                for pb in range(32):
                    last = e.matmul(prms[0:1, 0:CK], onescol[:, :],
                                    ysq[:, pb * 128:(pb + 1) * 128],
                                    start=(pb == 0), stop=(pb == 31))
                last.then_inc(mmv, 1)
            prog.on("tensor", f_rmsmm)
            rmsmm_v = n["mm"]

            n["sv"] += 1
            def f_std(e, w=rmsmm_v):
                e.wait_ge(mmv, w)
                e.activation(out=stdrow[:, :], in_=prms[0:1, 0:CK], func=AF.Sqrt,
                             scale=1.0 / IDIM, bias=cv[0:1, 0:1]).then_inc(sv, 1)
            prog.on("scalar", f_std)
            std_v = n["sv"]

            n["vv"] += 1
            def f_rstd(e, w=std_v):
                e.wait_ge(sv, w)
                e.reciprocal(out=rstdrow[:, :], in_=stdrow[:, :]).then_inc(vv, 1)
            prog.on("vector", f_rstd)
            rstd_v = n["vv"]

            n["mm"] += 1
            def f_rbmm(e, w=rstd_v):
                e.wait_ge(vv, w)
                e.matmul(prstdB[0:128, 0:CK], onesf[:, :], rstdrow[:, :], start=True,
                         stop=True).then_inc(mmv, 1)
            prog.on("tensor", f_rbmm)
            rb_v = n["mm"]

            need_dyn = yn_dma.get(u - 1, 0)
            bse7 = n["vv"]
            n["vv"] += 32
            def f_yn(e, w=rb_v, bse7=bse7, need_dyn=need_dyn):
                e.wait_ge(mmv, w)
                e.wait_ge(dyn, need_dyn)
                for pb in range(32):
                    e.scalar_tensor_tensor(out=ynst[:, pb * 128:(pb + 1) * 128],
                                           in0=ych[:, pb * 128:(pb + 1) * 128],
                                           scalar=nw[:, pb:pb + 1],
                                           in1=prstdB[0:128, 0:CK],
                                           op0=ALU.mult, op1=ALU.mult).then_inc(vv, 1)
            prog.on("vector", f_yn)
            yn_v = n["vv"]
            ych_done[u] = yn_v

            n["dyn"] += 16
            def f_yndma(e, u=u, w=yn_v, c0=c0):
                e.wait_ge(vv, w)
                e.dma_start(out=o_yn.rearrange("(pb p) t -> p pb t", p=128)[:, :, c0:c0 + CK],
                            in_=ynst[:, :].rearrange("p (pb s) -> p pb s", pb=32)).then_inc(dyn, 16)
            prog.on("sync", f_yndma)
            yn_dma[u] = n["dyn"]

        def f_fin(e, a=n["dyn"]):
            e.wait_ge(dyn, a)
        prog.on("sync", f_fin)
        prog.emit(block)
    return nc


def phase4b(nc, io):
    i_yn = io["d_yn"]; i_wouT = io["w_outT"]; i_g = io["d_g"]
    i_xr = io["xh"]; i_cv = io["cvec"]
    o_x1 = io["d_x1"]; o_xn1T = io["d_xn1T"]

    prog = Prog(); PHX[0] += 1
    with ExitStack() as st:
        _ctr = [0]
        def sb(shape, dt=F32):
            _ctr[0] += 1
            return st.enter_context(nc.sbuf_tensor("sb%d_%d" % (PHX[0], _ctr[0]), list(shape), dt))
        def ps(shape, dt=F32):
            _ctr[0] += 1
            return st.enter_context(nc.psum_tensor("ps%d_%d" % (PHX[0], _ctr[0]), list(shape), dt))
        def sem(name):
            return st.enter_context(nc.semaphore("s%d_%s" % (PHX[0], name)))

        xn1T = [sb((128, TPC), BF16) for _ in range(16)]
        ynb = [sb((128, 32 * 512), BF16) for _ in range(2)]
        wst = [sb((128, 32 * 128), BF16) for _ in range(2)]
        s0blk = sb((128, 512), BF16)
        s0T = sb((128, 4 * 16 * 128), BF16)
        xt = sb((128, DM))
        gt = sb((128, DM), BF16)
        x1f = sb((128, DM))
        sqs = sb((128, DM))
        sxn1 = sb((128, DM), BF16)
        sumc, sqc, meanc, nvc, stdc, rstdc = (sb((128, 1)) for _ in range(6))
        cv = sb((128, 8))
        identb = sb((128, 128), BF16)
        iota_p = sb((128, 1), mybir.dt.int32); iota_f = sb((128, 128), mybir.dt.int32)
        fidp = sb((128, 1)); fidf = sb((128, 128))
        ps0 = [ps((128, 512)) for _ in range(2)]
        ptr = [ps((128, 128), BF16) for _ in range(2)]

        ld = sem("ld"); gio = sem("gio"); wld = sem("wld")
        vv = sem("vv"); sv = sem("sv"); tv = sem("tv"); mmv = sem("mmv")
        dst = sem("dst")
        block = st.enter_context(nc.Block())

        n = dict(ld=0, wld=0, vv=0, sv=0, tv=0, mm=0, dst=0)

        def g_iota(e):
            e.iota(iota_p[:], pattern=[[0, 1]], base=0, channel_multiplier=1)
            e.iota(iota_f[:], pattern=[[1, 128]], base=0, channel_multiplier=0).then_inc(gio, 1)
        prog.on("gpsimd", g_iota)

        def s_cv(e):
            e.dma_start(out=cv[:, :], in_=i_cv.to_broadcast((128, 8))).then_inc(ld, 16)
        prog.on("sync", s_cv)
        n["ld"] = 16

        n["vv"] += 1
        def v_init(e):
            e.wait_ge(gio, 1)
            e.tensor_copy(out=fidp[:], in_=iota_p[:])
            e.tensor_copy(out=fidf[:], in_=iota_f[:])
            e.tensor_scalar(out=identb[:], in0=fidf[:], scalar1=fidp[:], scalar2=None,
                            op0=ALU.is_equal).then_inc(vv, 1)
        prog.on("vector", v_init)
        init_v = n["vv"]

        mm_after_tg = {}
        ev_after = [0]
        for tg in range(4):
            tok0 = tg * 512
            ybuf = tg % 2
            n["ld"] += 16
            def f_ynld(e, tok0=tok0, ybuf=ybuf, need_mm=mm_after_tg.get(tg - 2, 0)):
                e.wait_ge(mmv, need_mm)
                e.dma_start(out=ynb[ybuf][:, :].rearrange("i (ib t) -> i ib t", ib=32),
                            in_=i_yn.rearrange("(ib i) t -> i ib t", i=128)[:, :, tok0:tok0 + 512]
                            ).then_inc(ld, 16)
            prog.on("sync", f_ynld)
            ynldv = n["ld"]

            for db in range(16):
                dbu = tg * 16 + db
                wbuf = dbu % 2
                n["wld"] += 16
                def f_wld(e, db=db, wbuf=wbuf, need_mm=mm_after_tg.get(tg - 1, 0) if db < 2 else 0,
                          nm=max(0, n["mm"] - 1)):
                    e.wait_ge(mmv, nm)
                    e.dma_start(out=wst[wbuf][:, :].rearrange("i (ib d) -> i ib d", ib=32),
                                in_=i_wouT.rearrange("(ib i) d -> i ib d", i=128)[
                                    :, :, db * 128:(db + 1) * 128]).then_inc(wld, 16)
                prog.on("gpsimd", f_wld)
                wldv = n["wld"]

                pz = ps0[dbu % 2]
                n["mm"] += 1
                def f_mm(e, ybuf=ybuf, wbuf=wbuf, pz=pz, w1=ynldv, w2=wldv,
                         need_ev=ev_after[0]):
                    e.wait_ge(ld, w1)
                    e.wait_ge(wld, w2)
                    e.wait_ge(vv, need_ev)
                    last = None
                    for ib in range(32):
                        last = e.matmul(pz[:, :], wst[wbuf][:, ib * 128:(ib + 1) * 128],
                                        ynb[ybuf][:, ib * 512:(ib + 1) * 512],
                                        start=(ib == 0), stop=(ib == 31))
                    last.then_inc(mmv, 1)
                prog.on("tensor", f_mm)
                mmvv = n["mm"]

                n["vv"] += 1
                def f_ev(e, pz=pz, w=mmvv, wtr=n["tv"]):
                    e.wait_ge(mmv, w)
                    e.wait_ge(tv, wtr)
                    e.tensor_copy(out=s0blk[:, :], in_=pz[:, :]).then_inc(vv, 1)
                prog.on("vector", f_ev)
                evv = n["vv"]
                ev_after[0] = evv

                for tsub in range(4):
                    pj = ptr[tsub % 2]
                    n["tv"] += 1
                    def f_tr(e, tsub=tsub, pj=pj, w=evv, wev=max(0, n["vv"] - (1 if tsub >= 2 else 0))):
                        e.wait_ge(vv, wev)
                        e.transpose(pj[0:128, 0:128], s0blk[:, tsub * 128:(tsub + 1) * 128],
                                    identb[:]).then_inc(tv, 1)
                    prog.on("tensor", f_tr)
                    trv = n["tv"]
                    n["vv"] += 1
                    def f_trev(e, tsub=tsub, db=db, pj=pj, w=trv):
                        e.wait_ge(tv, w)
                        e.tensor_copy(out=s0T[:, (tsub * 16 + db) * 128:(tsub * 16 + db + 1) * 128],
                                      in_=pj[0:128, 0:128]).then_inc(vv, 1)
                    prog.on("vector", f_trev)
            s0T_v = n["vv"]
            mm_after_tg[tg] = n["mm"]

            for tsub in range(4):
                t0 = tok0 + tsub * 128
                b, pos0 = t0 // SPC, t0 % SPC
                n["ld"] += 32
                def f_xg(e, b=b, pos0=pos0, t0=t0, w=max(s0T_v, n["vv"])):
                    e.wait_ge(vv, w)
                    e.dma_start(out=xt[:, :],
                                in_=i_xr[b, HALO + pos0:HALO + pos0 + 128, :]).then_inc(ld, 16)
                    e.dma_start(out=gt[:, :], in_=i_g[t0:t0 + 128, :]).then_inc(ld, 16)
                prog.on("sync", f_xg)
                xgv = n["ld"]

                bse = n["vv"]
                n["vv"] += 3
                def f_res(e, tsub=tsub, bse=bse, w=xgv, wd=n["dst"]):
                    e.wait_ge(ld, w)
                    e.wait_ge(dst, wd)
                    e.tensor_tensor(out=x1f[:, :].rearrange("t (db d) -> t db d", db=16),
                                    in0=gt[:, :].rearrange("t (db d) -> t db d", db=16),
                                    in1=s0T[:, :].rearrange("t (ts db d) -> t ts db d",
                                                            ts=4, db=16)[:, tsub, :, :],
                                    op=ALU.mult).then_inc(vv, 1)
                    e.wait_ge(vv, bse + 1)
                    e.tensor_tensor(out=x1f[:, :], in0=x1f[:, :], in1=xt[:, :],
                                    op=ALU.add).then_inc(vv, 1)
                    e.wait_ge(vv, bse + 2)
                    e.reduce_sum(out=sumc[:, :], in_=x1f[:, :], axis=AX.X).then_inc(vv, 1)
                prog.on("vector", f_res)
                res_v = n["vv"]

                n["dst"] += 16
                def f_x1dma(e, b=b, pos0=pos0, w=res_v):
                    e.wait_ge(vv, w)
                    e.dma_start(out=o_x1[b, pos0:pos0 + 128, :], in_=x1f[:, :]).then_inc(dst, 16)
                prog.on("sync", f_x1dma)
                x1d_v = n["dst"]

                n["sv"] += 1
                def f_sq(e, w=res_v):
                    e.wait_ge(vv, w)
                    e.activation(out=sqs[:, :], in_=x1f[:, :], func=AF.Square,
                                 accum_out=sqc[:, :]).then_inc(sv, 1)
                prog.on("scalar", f_sq)
                sqv = n["sv"]

                bse2 = n["vv"]
                n["vv"] += 2
                def f_stats(e, w=sqv, bse2=bse2):
                    e.wait_ge(sv, w)
                    e.tensor_scalar(out=meanc[:, :], in0=sumc[:, :], scalar1=1.0 / DM,
                                    scalar2=None, op0=ALU.mult)
                    e.tensor_scalar(out=nvc[:, :], in0=sqc[:, :], scalar1=1.0 / DM,
                                    scalar2=None, op0=ALU.mult).then_inc(vv, 1)
                    e.wait_ge(vv, bse2 + 1)
                    e.scalar_tensor_tensor(out=nvc[:, :], in0=meanc[:, :], scalar=meanc[:, :],
                                           in1=nvc[:, :], op0=ALU.mult,
                                           op1=ALU.subtract).then_inc(vv, 1)
                prog.on("vector", f_stats)
                st_v = n["vv"]

                n["sv"] += 1
                def f_std(e, w=st_v):
                    e.wait_ge(vv, w)
                    e.activation(out=stdc[:, :], in_=nvc[:, :], func=AF.Sqrt, scale=-1.0,
                                 bias=cv[:, 0:1]).then_inc(sv, 1)
                prog.on("scalar", f_std)
                stdv = n["sv"]

                bse3 = n["vv"]
                n["vv"] += 2
                def f_xn1(e, w=stdv, wd=x1d_v, bse3=bse3):
                    e.wait_ge(sv, w)
                    e.wait_ge(dst, wd)
                    e.reciprocal(out=rstdc[:, :], in_=stdc[:, :]).then_inc(vv, 1)
                    e.wait_ge(vv, bse3 + 1)
                    e.tensor_scalar(out=sxn1[:, :], in0=x1f[:, :], scalar1=meanc[:, :],
                                    scalar2=rstdc[:, :], op0=ALU.subtract,
                                    op1=ALU.mult).then_inc(vv, 1)
                prog.on("vector", f_xn1)
                xn1_v = n["vv"]

                for kg in range(8):
                    pj = ptr[kg % 2]
                    n["tv"] += 2
                    def f_tr2(e, kg=kg, pj=pj, w=xn1_v, wev=n["vv"]):
                        e.wait_ge(vv, wev)
                        last = None
                        for q in range(2):
                            last = e.transpose(ptr[q][0:128, 0:128],
                                               sxn1[:, (kg * 2 + q) * 128:(kg * 2 + q + 1) * 128],
                                               identb[:])
                        last.then_inc(tv, 2)
                    prog.on("tensor", f_tr2)
                    trv2 = n["tv"]
                    n["vv"] += 1
                    def f_tre2(e, kg=kg, t0=t0, w=trv2):
                        e.wait_ge(tv, w)
                        last = None
                        for q in range(2):
                            last = e.tensor_copy(out=xn1T[kg * 2 + q][:, t0:t0 + 128],
                                                 in_=ptr[q][0:128, 0:128])
                        last.then_inc(vv, 1)
                    prog.on("vector", f_tre2)

        all_v = n["vv"]
        for k in range(16):
            n["dst"] += 16
            def f_spill(e, k=k, w=all_v):
                e.wait_ge(vv, w)
                e.dma_start(out=o_xn1T[k * 128:(k + 1) * 128, :], in_=xn1T[k][:, :]).then_inc(dst, 16)
            prog.on("sync", f_spill)

        def f_fin(e, a=n["dst"]):
            e.wait_ge(dst, a)
        prog.on("sync", f_fin)
        prog.emit(block)
    return nc


def phase5a(nc, io):
    i_xn1T = io["d_xn1T"]; i_wqkvT = io["wqkvT"]; i_fw = io["ffnW"]
    i_sc = io["fscale"]; i_cv = io["cvec"]
    o_qkv = io["d_qkv"]; o_g1 = io["d_g1"]; ofd = io["ofd_h"]

    prog = Prog(); PHX[0] += 1
    with ExitStack() as st:
        _ctr = [0]
        def sb(shape, dt=F32):
            _ctr[0] += 1
            return st.enter_context(nc.sbuf_tensor("sb%d_%d" % (PHX[0], _ctr[0]), list(shape), dt))
        def ps(shape, dt=F32):
            _ctr[0] += 1
            return st.enter_context(nc.psum_tensor("ps%d_%d" % (PHX[0], _ctr[0]), list(shape), dt))
        def sem(name):
            return st.enter_context(nc.semaphore("s%d_%s" % (PHX[0], name)))

        xn1T = [sb((128, TPC), BF16) for _ in range(16)]
        wqb = [sb((128, 16 * 128), BF16) for _ in range(2)]
        qst = sb((128, TPC), BF16)
        ffnb = [sb((128, 16 * 512), BF16) for _ in range(2)]
        scaleB = sb((128, DM))
        srow = sb((1, DM))
        ofst = sb((128, 512), BF16)
        ofrow = sb((128, DM), BF16)
        sqp = sb((128, 64))
        ss = sb((128, 1)); vn = sb((128, 1)); th = sb((128, 1)); dn = sb((128, 1))
        rc = sb((128, 1)); f1 = sb((128, 1)); onm = sb((128, 1)); rc2 = sb((128, 1))
        ff = sb((128, 1))
        g1pre = sb((128, DM))
        g1st = sb((128, DM), BF16)
        cv = sb((128, 8))
        onesf = sb((1, 128))
        psq = [ps((128, 512)) for _ in range(2)]
        psf = [ps((128, 512)) for _ in range(2)]
        psb = ps((128, 512))

        ld = sem("ld"); wld = sem("wld")
        vv = sem("vv"); sv = sem("sv"); mmv = sem("mmv"); dst = sem("dst")
        block = st.enter_context(nc.Block())

        n = dict(ld=0, wld=0, vv=0, sv=0, mm=0, dst=0)

        def s_loads(e):
            for k in range(16):
                e.dma_start(out=xn1T[k][:, :], in_=i_xn1T[k * 128:(k + 1) * 128, :]).then_inc(ld, 16)
            e.dma_start(out=srow[:, :], in_=i_sc).then_inc(ld, 16)
            e.dma_start(out=cv[:, :], in_=i_cv.to_broadcast((128, 8))).then_inc(ld, 16)
        prog.on("sync", s_loads)
        n["ld"] = 18 * 16
        ld0 = n["ld"]

        n["vv"] += 1
        def v_init(e):
            e.memset(onesf[:, :], 1.0).then_inc(vv, 1)
        prog.on("vector", v_init)
        init_v = n["vv"]

        ev_hist = {}
        for ob in range(24):
            wbuf = ob % 2
            n["wld"] += 16
            def f_wq(e, ob=ob, wbuf=wbuf, need=max(0, n["mm"] - 1)):
                e.wait_ge(mmv, need)
                e.dma_start(out=wqb[wbuf][:, :].rearrange("d (k o) -> d k o", k=16),
                            in_=i_wqkvT.rearrange("(k d) o -> d k o", d=128)[
                                :, :, ob * 128:(ob + 1) * 128]).then_inc(wld, 16)
            prog.on("gpsimd", f_wq)
            wqv = n["wld"]
            for tg in range(4):
                pz = psq[(ob * 4 + tg) % 2]
                n["mm"] += 1
                def f_qmm(e, wbuf=wbuf, tg=tg, pz=pz, w=wqv,
                          need_ev=ev_hist.get(ob * 4 + tg - 2, 0)):
                    e.wait_ge(wld, w)
                    e.wait_ge(ld, ld0)
                    e.wait_ge(vv, need_ev)
                    last = None
                    for k in range(16):
                        last = e.matmul(pz[:, :], wqb[wbuf][:, k * 128:(k + 1) * 128],
                                        xn1T[k][:, tg * 512:(tg + 1) * 512],
                                        start=(k == 0), stop=(k == 15))
                    last.then_inc(mmv, 1)
                prog.on("tensor", f_qmm)
                qmmv = n["mm"]
                n["vv"] += 1
                def f_qev(e, tg=tg, pz=pz, w=qmmv, wd=n["dst"]):
                    e.wait_ge(mmv, w)
                    e.wait_ge(dst, wd)
                    e.tensor_copy(out=qst[:, tg * 512:(tg + 1) * 512], in_=pz[:, :]).then_inc(vv, 1)
                prog.on("vector", f_qev)
                ev_hist[ob * 4 + tg] = n["vv"]
            n["dst"] += 16
            def f_qdma(e, ob=ob, w=n["vv"]):
                e.wait_ge(vv, w)
                e.dma_start(out=o_qkv[ob * 128:(ob + 1) * 128, :], in_=qst[:, :]).then_inc(dst, 16)
            prog.on("sync", f_qdma)
        qkv_done_mm = n["mm"]

        n["mm"] += 1
        def f_scmm(e, w=init_v):
            e.wait_ge(vv, w)
            e.wait_ge(ld, ld0)
            last = None
            for q in range(4):
                last = e.matmul(psb[:, :] if False else psb[:, 0:512], onesf[:, :],
                                srow[:, q * 512:(q + 1) * 512], start=True, stop=True)
            last.then_inc(mmv, 1)
        prog.on("tensor", None)
        prog.items["tensor"].pop()
        n["mm"] -= 1
        for q in range(4):
            n["mm"] += 1
            def f_scmm(e, q=q, w=init_v, wev=n["vv"]):
                e.wait_ge(vv, max(w, wev))
                e.wait_ge(ld, ld0)
                e.matmul(psb[:, 0:512], onesf[:, :], srow[:, q * 512:(q + 1) * 512],
                         start=True, stop=True).then_inc(mmv, 1)
            prog.on("tensor", f_scmm)
            scv = n["mm"]
            n["vv"] += 1
            def f_scev(e, q=q, w=scv):
                e.wait_ge(mmv, w)
                e.tensor_copy(out=scaleB[:, q * 512:(q + 1) * 512], in_=psb[:, 0:512]).then_inc(vv, 1)
            prog.on("vector", f_scev)
        scB_v = n["vv"]

        of_hist = {}
        for kg in range(4):
            fbuf = kg % 2
            n["wld"] += 16
            def f_fl(e, kg=kg, fbuf=fbuf, need=of_hist.get(kg - 2, qkv_done_mm)):
                e.wait_ge(mmv, need)
                e.dma_start(out=ffnb[fbuf][:, :].rearrange("nd (k c) -> nd k c", k=16),
                            in_=i_fw.rearrange("(k nd) c -> nd k c", nd=128)[
                                :, :, kg * 512:(kg + 1) * 512]).then_inc(wld, 16)
            prog.on("gpsimd", f_fl)
            flv = n["wld"]
            for tt in range(16):
                pz = psf[(kg * 16 + tt) % 2]
                n["mm"] += 1
                def f_fmm(e, fbuf=fbuf, tt=tt, pz=pz, w=flv, need_ev=max(0, n["vv"] - 1)):
                    e.wait_ge(wld, w)
                    e.wait_ge(vv, need_ev)
                    last = None
                    for k in range(16):
                        last = e.matmul(pz[:, :], xn1T[k][:, tt * 128:(tt + 1) * 128],
                                        ffnb[fbuf][:, k * 512:(k + 1) * 512],
                                        start=(k == 0), stop=(k == 15))
                    last.then_inc(mmv, 1)
                prog.on("tensor", f_fmm)
                fmmv = n["mm"]
                bsf = n["vv"]
                n["vv"] += 1
                def f_oev(e, kg=kg, tt=tt, pz=pz, w=fmmv, wsc=scB_v, wd=n["dst"]):
                    e.wait_ge(mmv, w)
                    e.wait_ge(vv, wsc)
                    e.wait_ge(dst, wd)
                    e.tensor_tensor(out=ofst[:, :], in0=pz[:, :],
                                    in1=scaleB[:, kg * 512:(kg + 1) * 512],
                                    op=ALU.mult).then_inc(vv, 1)
                prog.on("vector", f_oev)
                oevv = n["vv"]
                n["sv"] += 1
                def f_osq(e, kg=kg, tt=tt, w=oevv):
                    e.wait_ge(vv, w)
                    e.activation(out=g1pre[:, 0:512], in_=ofst[:, :], func=AF.Square,
                                 accum_out=sqp[:, tt * 4 + kg:tt * 4 + kg + 1]).then_inc(sv, 1)
                prog.on("scalar", f_osq)
                osqv = n["sv"]
                n["dst"] += 16
                def f_odma(e, kg=kg, tt=tt, w=oevv, ws=osqv):
                    e.wait_ge(vv, w)
                    e.wait_ge(sv, ws)
                    e.dma_start(out=ofd[tt * 128:(tt + 1) * 128, kg * 512:(kg + 1) * 512],
                                in_=ofst[:, :]).then_inc(dst, 16)
                prog.on("sync", f_odma)
            of_hist[kg] = n["mm"]
        ffn_dst = n["dst"]
        sq_done = n["sv"]

        g1_hist = {}
        for tt in range(16):
            n["ld"] += 16
            def f_ofl(e, tt=tt, w=ffn_dst, need_v=g1_hist.get(tt - 1, 0)):
                e.wait_ge(dst, w)
                e.wait_ge(vv, need_v)
                e.dma_start(out=ofrow[:, :], in_=ofd[tt * 128:(tt + 1) * 128, :]).then_inc(ld, 16)
            prog.on("sync", f_ofl)
            oflv = n["ld"]

            bsn = n["vv"]
            n["vv"] += 3
            def f_n1(e, tt=tt, bsn=bsn, ws=sq_done):
                e.wait_ge(sv, ws)
                e.tensor_tensor(out=ss[:, :], in0=sqp[:, tt * 4:tt * 4 + 1],
                                in1=sqp[:, tt * 4 + 1:tt * 4 + 2], op=ALU.add).then_inc(vv, 1)
                e.wait_ge(vv, bsn + 1)
                e.tensor_tensor(out=ss[:, :], in0=ss[:, :], in1=sqp[:, tt * 4 + 2:tt * 4 + 3],
                                op=ALU.add).then_inc(vv, 1)
                e.wait_ge(vv, bsn + 2)
                e.tensor_tensor(out=ss[:, :], in0=ss[:, :], in1=sqp[:, tt * 4 + 3:tt * 4 + 4],
                                op=ALU.add).then_inc(vv, 1)
            prog.on("vector", f_n1)
            n1v = n["vv"]

            n["sv"] += 1
            def f_vn(e, w=n1v):
                e.wait_ge(vv, w)
                e.activation(out=vn[:, :], in_=ss[:, :], func=AF.Sqrt, scale=1.0).then_inc(sv, 1)
            prog.on("scalar", f_vn)
            vnv = n["sv"]

            bsn2 = n["vv"]
            n["vv"] += 2
            def f_n2(e, bsn2=bsn2, w=vnv):
                e.wait_ge(sv, w)
                e.tensor_scalar(out=vn[:, :], in0=vn[:, :], scalar1=1e-7, scalar2=None,
                                op0=ALU.max).then_inc(vv, 1)
                e.wait_ge(vv, bsn2 + 1)
                e.tensor_scalar(out=dn[:, :], in0=vn[:, :], scalar1=1e-7, scalar2=None,
                                op0=ALU.add).then_inc(vv, 1)
            prog.on("vector", f_n2)
            n2v = n["vv"]

            n["sv"] += 1
            def f_th(e, w=n2v):
                e.wait_ge(vv, w)
                e.activation(out=th[:, :], in_=vn[:, :], func=AF.Tanh).then_inc(sv, 1)
            prog.on("scalar", f_th)
            thv = n["sv"]

            bsn3 = n["vv"]
            n["vv"] += 6
            def f_n3(e, bsn3=bsn3, w=thv):
                e.wait_ge(sv, w)
                e.reciprocal(out=rc[:, :], in_=dn[:, :]).then_inc(vv, 1)
                e.wait_ge(vv, bsn3 + 1)
                e.tensor_tensor(out=f1[:, :], in0=th[:, :], in1=rc[:, :], op=ALU.mult).then_inc(vv, 1)
                e.wait_ge(vv, bsn3 + 2)
                e.scalar_tensor_tensor(out=onm[:, :], in0=f1[:, :], scalar=1.0, in1=vn[:, :],
                                       op0=ALU.mult, op1=ALU.mult).then_inc(vv, 1)
                e.wait_ge(vv, bsn3 + 3)
                e.tensor_scalar(out=onm[:, :], in0=onm[:, :], scalar1=1e-7, scalar2=None,
                                op0=ALU.max).then_inc(vv, 1)
                e.wait_ge(vv, bsn3 + 4)
                e.reciprocal(out=rc2[:, :], in_=onm[:, :]).then_inc(vv, 1)
                e.wait_ge(vv, bsn3 + 5)
                e.tensor_scalar(out=rc2[:, :], in0=rc2[:, :], scalar1=0.99, scalar2=1.0,
                                op0=ALU.mult, op1=ALU.min).then_inc(vv, 1)
            prog.on("vector", f_n3)
            n3v = n["vv"]

            bsn4 = n["vv"]
            n["vv"] += 2
            def f_g1p(e, tt=tt, bsn4=bsn4, w=n3v, wl=oflv):
                e.wait_ge(vv, w)
                e.wait_ge(ld, wl)
                e.tensor_tensor(out=ff[:, :], in0=f1[:, :], in1=rc2[:, :], op=ALU.mult).then_inc(vv, 1)
                e.wait_ge(vv, bsn4 + 1)
                e.tensor_scalar(out=g1pre[:, :], in0=ofrow[:, :], scalar1=ff[:, :], scalar2=None,
                                op0=ALU.mult).then_inc(vv, 1)
            prog.on("vector", f_g1p)
            g1pv = n["vv"]

            n["sv"] += 1
            def f_g1s(e, w=g1pv, wd=n["dst"]):
                e.wait_ge(vv, w)
                e.wait_ge(dst, wd)
                e.activation(out=g1st[:, :], in_=g1pre[:, :], func=AF.Silu).then_inc(sv, 1)
            prog.on("scalar", f_g1s)
            g1sv = n["sv"]
            n["dst"] += 16
            def f_g1dma(e, tt=tt, w=g1sv):
                e.wait_ge(sv, w)
                e.dma_start(out=o_g1[tt * 128:(tt + 1) * 128, :], in_=g1st[:, :]).then_inc(dst, 16)
            prog.on("sync", f_g1dma)
            g1_hist[tt] = n["vv"]

        def f_fin(e, a=n["dst"]):
            e.wait_ge(dst, a)
        prog.on("sync", f_fin)
        prog.emit(block)
    return nc


def phase5b(nc, io):
    i_qkv = io["d_qkv"]; i_g1 = io["d_g1"]; i_x1 = io["d_x1"]
    i_woT = io["wo1T"]; i_sl = io["slrow"]
    o_out = io["outp"]

    prog = Prog(); PHX[0] += 1
    with ExitStack() as st:
        _ctr = [0]
        def sb(shape, dt=F32):
            _ctr[0] += 1
            return st.enter_context(nc.sbuf_tensor("sb%d_%d" % (PHX[0], _ctr[0]), list(shape), dt))
        def ps(shape, dt=F32):
            _ctr[0] += 1
            return st.enter_context(nc.psum_tensor("ps%d_%d" % (PHX[0], _ctr[0]), list(shape), dt))
        def sem(name):
            return st.enter_context(nc.semaphore("s%d_%s" % (PHX[0], name)))

        q1 = [sb((128, SPC), BF16) for _ in range(16)]
        kv = [sb((128, TPC), BF16) for _ in range(8)]
        o1f = [sb((128, SPC), BF16) for _ in range(16)]
        tmp0 = sb((128, SPC), BF16)
        tmp1 = sb((128, SPC), BF16)
        p10 = sb((1, SPC))
        dvt = sb((128, SPC), BF16)
        slrow = sb((1, 16))
        onesf = sb((1, 128))
        onescol = sb((128, 1), BF16)
        negcol = sb((128, 1), BF16)
        wst = [sb((128, 16 * 128), BF16) for _ in range(2)]
        s0blk = sb((128, 512), BF16)
        s1T = sb((128, 4 * 16 * 128), BF16)
        xt = sb((128, DM))
        gt = sb((128, DM), BF16)
        x2f = sb((128, DM))
        identb = sb((128, 128), BF16)
        iota_p = sb((128, 1), mybir.dt.int32); iota_f = sb((128, 128), mybir.dt.int32)
        fidp = sb((128, 1)); fidf = sb((128, 128))
        pdif = ps((1, 512))
        pb10 = ps((128, 512))
        pso = [ps((128, 512)) for _ in range(2)]
        ptr = [ps((128, 128), BF16) for _ in range(2)]

        ld = sem("ld"); wld = sem("wld"); gio = sem("gio")
        vv = sem("vv"); sv = sem("sv"); tv = sem("tv"); mmv = sem("mmv"); dst = sem("dst")
        block = st.enter_context(nc.Block())

        n = dict(ld=0, wld=0, vv=0, sv=0, tv=0, mm=0, dst=0)

        def g_iota(e):
            e.iota(iota_p[:], pattern=[[0, 1]], base=0, channel_multiplier=1)
            e.iota(iota_f[:], pattern=[[1, 128]], base=0, channel_multiplier=0).then_inc(gio, 1)
        prog.on("gpsimd", g_iota)

        def s_loads(e):
            for h in range(16):
                e.dma_start(out=q1[h][:, :], in_=i_qkv[h * 128:(h + 1) * 128, SPC:TPC]).then_inc(ld, 16)
            for z in range(8):
                e.dma_start(out=kv[z][:, :], in_=i_qkv[2048 + z * 128:2048 + (z + 1) * 128, :]).then_inc(ld, 16)
            e.dma_start(out=slrow[:, :], in_=i_sl).then_inc(ld, 16)
        prog.on("sync", s_loads)
        n["ld"] = 25 * 16
        ld0 = n["ld"]

        n["vv"] += 1
        def v_init(e):
            e.wait_ge(gio, 1)
            e.tensor_copy(out=fidp[:], in_=iota_p[:])
            e.tensor_copy(out=fidf[:], in_=iota_f[:])
            e.tensor_scalar(out=identb[:], in0=fidf[:], scalar1=fidp[:], scalar2=None,
                            op0=ALU.is_equal)
            e.memset(onesf[:, :], 1.0)
            e.memset(onescol[:, :], 1.0)
            e.memset(negcol[:, :], -1.0).then_inc(vv, 1)
        prog.on("vector", v_init)
        init_v = n["vv"]

        ATT_SCALE = float(1.0 / np.sqrt(128.0))
        for h in range(16):
            g = h // 4
            k0 = kv[2 * g]; v0 = kv[2 * g + 1]
            bsa = n["vv"]
            n["vv"] += 2
            def f_tt(e, h=h, k0=k0, v0=v0, bsa=bsa, w=init_v, wmm=n["mm"]):
                e.wait_ge(ld, ld0)
                e.wait_ge(vv, w)
                e.wait_ge(mmv, wmm)
                e.tensor_tensor(out=tmp0[:, :], in0=q1[h][:, :], in1=k0[:, 0:SPC],
                                op=ALU.mult).then_inc(vv, 1)
                e.wait_ge(vv, bsa + 1)
                e.tensor_tensor(out=tmp1[:, :], in0=q1[h][:, :], in1=k0[:, SPC:TPC],
                                op=ALU.mult).then_inc(vv, 1)
            prog.on("vector", f_tt)
            ttv = n["vv"]

            for half in range(2):
                n["mm"] += 1
                def f_dmm(e, half=half, w=ttv, ws=n["sv"]):
                    e.wait_ge(vv, w)
                    e.wait_ge(sv, ws)
                    e.matmul(pdif[0:1, 0:512], onescol[:, :], tmp0[:, half * 512:(half + 1) * 512],
                             start=True, stop=False)
                    e.matmul(pdif[0:1, 0:512], negcol[:, :], tmp1[:, half * 512:(half + 1) * 512],
                             start=False, stop=True).then_inc(mmv, 1)
                prog.on("tensor", f_dmm)
                dmmv = n["mm"]
                n["sv"] += 1
                def f_sig(e, h=h, half=half, w=dmmv):
                    e.wait_ge(mmv, w)
                    e.activation(out=p10[0:1, half * 512:(half + 1) * 512], in_=pdif[0:1, 0:512],
                                 func=AF.Sigmoid, scale=ATT_SCALE,
                                 bias=slrow[0:1, h:h + 1]).then_inc(sv, 1)
                prog.on("scalar", f_sig)
            sigv = n["sv"]

            bsb = n["vv"]
            n["vv"] += 1
            def f_dv(e, v0=v0, bsb=bsb):
                e.tensor_tensor(out=dvt[:, :], in0=v0[:, 0:SPC], in1=v0[:, SPC:TPC],
                                op=ALU.subtract).then_inc(vv, 1)
            prog.on("vector", f_dv)
            dvv = n["vv"]

            for half in range(2):
                n["mm"] += 1
                def f_bmm(e, half=half, w=sigv, wev=n["vv"]):
                    e.wait_ge(sv, w)
                    e.wait_ge(vv, wev)
                    e.matmul(pb10[:, 0:512], onesf[:, :], p10[0:1, half * 512:(half + 1) * 512],
                             start=True, stop=True).then_inc(mmv, 1)
                prog.on("tensor", f_bmm)
                bmmv = n["mm"]
                bsc = n["vv"]
                n["vv"] += 2
                def f_o1(e, h=h, half=half, v0=v0, w=bmmv, wd=dvv, bsc=bsc):
                    e.wait_ge(mmv, w)
                    e.wait_ge(vv, wd)
                    e.tensor_tensor(out=o1f[h][:, half * 512:(half + 1) * 512],
                                    in0=dvt[:, half * 512:(half + 1) * 512], in1=pb10[:, 0:512],
                                    op=ALU.mult).then_inc(vv, 1)
                    e.wait_ge(vv, bsc + 1)
                    e.tensor_tensor(out=o1f[h][:, half * 512:(half + 1) * 512],
                                    in0=o1f[h][:, half * 512:(half + 1) * 512],
                                    in1=v0[:, SPC + half * 512:SPC + (half + 1) * 512],
                                    op=ALU.add).then_inc(vv, 1)
                prog.on("vector", f_o1)
        att_v = n["vv"]
        att_mm = n["mm"]

        ev_after = [0]
        for tg in range(4):
            tok0 = tg * 512
            b = tok0 // SPC
            for db in range(16):
                dbu = tg * 16 + db
                wbuf = dbu % 2
                n["wld"] += 16
                def f_wl(e, db=db, wbuf=wbuf, nm=max(0, n["mm"] - att_mm - 1) + att_mm if n["mm"] > att_mm else att_mm):
                    e.wait_ge(mmv, nm)
                    e.dma_start(out=wst[wbuf][:, :].rearrange("o (k d) -> o k d", k=16),
                                in_=i_woT.rearrange("(k o) d -> o k d", o=128)[
                                    :, :, db * 128:(db + 1) * 128]).then_inc(wld, 16)
                prog.on("gpsimd", f_wl)
                wlv = n["wld"]

                pz = pso[dbu % 2]
                n["mm"] += 1
                def f_mm(e, tg=tg, db=db, wbuf=wbuf, pz=pz, w=wlv, wa=att_v,
                         need_ev=ev_after[0]):
                    e.wait_ge(wld, w)
                    e.wait_ge(vv, max(wa, need_ev))
                    last = None
                    for ob in range(16):
                        gg = ob // 4
                        if tg < 2:
                            mov = kv[2 * gg + 1][:, tg * 512:(tg + 1) * 512]
                        else:
                            mov = o1f[ob][:, (tg - 2) * 512:(tg - 1) * 512]
                        last = e.matmul(pz[:, :], wst[wbuf][:, ob * 128:(ob + 1) * 128], mov,
                                        start=(ob == 0), stop=(ob == 15))
                    last.then_inc(mmv, 1)
                prog.on("tensor", f_mm)
                mmvv = n["mm"]

                n["vv"] += 1
                def f_ev(e, pz=pz, w=mmvv, wtr=n["tv"]):
                    e.wait_ge(mmv, w)
                    e.wait_ge(tv, wtr)
                    e.tensor_copy(out=s0blk[:, :], in_=pz[:, :]).then_inc(vv, 1)
                prog.on("vector", f_ev)
                evv = n["vv"]
                ev_after[0] = evv

                for tsub in range(4):
                    pj = ptr[tsub % 2]
                    n["tv"] += 1
                    def f_tr(e, tsub=tsub, pj=pj, w=evv, wev=n["vv"]):
                        e.wait_ge(vv, wev)
                        e.transpose(pj[0:128, 0:128], s0blk[:, tsub * 128:(tsub + 1) * 128],
                                    identb[:]).then_inc(tv, 1)
                    prog.on("tensor", f_tr)
                    trv = n["tv"]
                    n["vv"] += 1
                    def f_trev(e, tsub=tsub, db=db, pj=pj, w=trv):
                        e.wait_ge(tv, w)
                        e.tensor_copy(out=s1T[:, (tsub * 16 + db) * 128:(tsub * 16 + db + 1) * 128],
                                      in_=pj[0:128, 0:128]).then_inc(vv, 1)
                    prog.on("vector", f_trev)
            s1T_v = n["vv"]

            for tsub in range(4):
                t0 = tok0 + tsub * 128
                b2, pos0 = t0 // SPC, t0 % SPC
                n["ld"] += 32
                def f_xg(e, b2=b2, pos0=pos0, t0=t0, w=n["vv"], wd=n["dst"]):
                    e.wait_ge(vv, w)
                    e.wait_ge(dst, wd)
                    e.dma_start(out=xt[:, :], in_=i_x1[b2, pos0:pos0 + 128, :]).then_inc(ld, 16)
                    e.dma_start(out=gt[:, :], in_=i_g1[t0:t0 + 128, :]).then_inc(ld, 16)
                prog.on("sync", f_xg)
                xgv = n["ld"]

                bsr = n["vv"]
                n["vv"] += 2
                def f_res(e, tsub=tsub, bsr=bsr, w=xgv, wd=n["dst"]):
                    e.wait_ge(ld, w)
                    e.wait_ge(dst, wd)
                    e.tensor_tensor(out=x2f[:, :].rearrange("t (db d) -> t db d", db=16),
                                    in0=gt[:, :].rearrange("t (db d) -> t db d", db=16),
                                    in1=s1T[:, :].rearrange("t (ts db d) -> t ts db d",
                                                            ts=4, db=16)[:, tsub, :, :],
                                    op=ALU.mult).then_inc(vv, 1)
                    e.wait_ge(vv, bsr + 1)
                    e.tensor_tensor(out=x2f[:, :], in0=x2f[:, :], in1=xt[:, :],
                                    op=ALU.add).then_inc(vv, 1)
                prog.on("vector", f_res)
                resv = n["vv"]

                n["dst"] += 16
                def f_odma(e, b2=b2, pos0=pos0, w=resv):
                    e.wait_ge(vv, w)
                    e.dma_start(out=o_out[b2, pos0:pos0 + 128, :], in_=x2f[:, :]).then_inc(dst, 16)
                prog.on("sync", f_odma)

        def f_fin(e, a=n["dst"]):
            e.wait_ge(dst, a)
        prog.on("sync", f_fin)
        prog.emit(block)
    return nc


def build_mega():
    nc = bass.Bass("TRN2", target_bir_lowering=False)
    io = {}

    def EI(nm, shp, dt):
        return nc.dram_tensor(nm, shp, dt, kind="ExternalInput").ap()

    io["xh"] = EI("xh", [BATCH, T2B, DM], F32)
    io["w_inT"] = EI("w_inT", [DM, PROJ], F32)
    io["conv_wr"] = EI("conv_wr", [128, 34 * 4], F32)
    io["conv_br"] = EI("conv_br", [128, 34], F32)
    io["dt_bias"] = EI("dt_bias", [NH, 1], F32)
    io["cvec"] = EI("cvec", [1, 8], F32)
    io["alog"] = EI("alog", [NH, 1], F32)
    io["w_gate"] = EI("w_gate", [DM, 8], F32)
    io["wi_0"] = EI("wi_0", [8, DM, 1024], F32)
    io["wi_1"] = EI("wi_1", [8, DM, 1024], F32)
    io["wo"] = EI("wo", [8, 1024, DM], F32)
    io["rmask16"] = EI("rmask16", [NH, 16], F32)
    io["invmask16"] = EI("invmask16", [NH, 16], F32)
    io["Drep"] = EI("Drep", [128, 32], F32)
    io["normw"] = EI("normw", [128, 32], F32)
    io["masknegT"] = EI("masknegT", [128, 128], F32)
    io["w_outT"] = EI("w_outT", [IDIM, DM], F32)
    io["wqkvT"] = EI("wqkvT", [DM, 3072], F32)
    io["ffnW"] = EI("ffnW", [DM, DM], F32)
    io["fscale"] = EI("fscale", [1, DM], F32)
    io["wo1T"] = EI("wo1T", [DM, DM], F32)
    io["slrow"] = EI("slrow", [1, 16], F32)

    io["outp"] = nc.dram_tensor("outp", [BATCH, SPC, DM], F32, kind="ExternalOutput").ap()

    def IN(nm, shp, dt, **kw):
        return nc.dram_tensor(nm, shp, dt, kind="Internal", **kw)

    io["d_xnT"] = IN("d_xnT", [DM, T2], BF16).ap()
    io["d_xnTlo"] = IN("d_xnTlo", [DM, T2], BF16).ap()
    io["d_sg"] = IN("d_sg", [IDIM, TPC], BF16).ap()
    io["d_xs"] = IN("d_xs", [IDIM, TPC], BF16).ap()
    io["d_bc"] = IN("d_bc", [2 * NSD, TPC], BF16).ap()
    io["d_dt"] = IN("d_dt", [NH, TPC], F32).ap()
    io["d_xp"] = IN("d_xp", [16, 128, IDIM], BF16).ap()
    io["d_sj"] = IN("d_sj", [16, 128, IDIM], F32).ap()
    io["spn_loc"] = IN("spn_loc", [BATCH * 128, IDIM], F32).ap()
    io["spn_all"] = IN("spn_all", [NCORES * BATCH * 128, IDIM], F32).ap()
    io["dsp_loc"] = IN("dsp_loc", [NH, 8], F32).ap()
    io["dsp_all"] = IN("dsp_all", [NCORES * NH, 8], F32).ap()
    io["d_g"] = IN("d_g", [TPC, DM], BF16).ap()
    io["d_dbg"] = IN("d_dbg", [128, 128], F32).ap()
    io["acsp_h"] = IN("acsp", [6, NH, TPC], BF16)
    io["d_yn"] = IN("d_yn", [IDIM, TPC], BF16).ap()
    io["d_x1"] = IN("d_x1", [BATCH, SPC, DM], F32).ap()
    io["d_xn1T"] = IN("d_xn1T", [DM, TPC], BF16).ap()
    io["d_qkv"] = IN("d_qkv", [3072, TPC], BF16).ap()
    io["d_g1"] = IN("d_g1", [TPC, DM], BF16).ap()
    io["ofd_h"] = IN("ofd", [TPC, DM], BF16)

    nc.all_engine_barrier()
    phase1(nc, io)
    nc.reset()
    phase2(nc, io)
    nc.reset()
    phase3(nc, io)
    nc.reset()
    phase4a(nc, io)
    nc.reset()
    phase4b(nc, io)
    nc.reset()
    phase5a(nc, io)
    nc.reset()
    phase5b(nc, io)
    return nc


def _shard_x(x):
    out = []
    xz = np.concatenate([np.zeros((BATCH, HALO, DM), np.float32), np.ascontiguousarray(x)], axis=1)
    for c in range(NCORES):
        out.append(np.ascontiguousarray(xz[:, c * SPC: c * SPC + T2B, :]))
    return out


_MEGA = {}


def kernel(**inputs):
    if "nc" not in _MEGA:
        _MEGA["nc"] = build_mega()
    nc = _MEGA["nc"]
    w_inT = np.ascontiguousarray(inputs["L0_attn_W_in"].T)
    conv_w = np.ascontiguousarray(
        inputs["L0_attn_conv_w"].reshape(34, 128, 4).transpose(1, 0, 2).reshape(128, 136))
    conv_b = np.ascontiguousarray(inputs["L0_attn_conv_b"].reshape(34, 128).T)
    dt_bias = np.ascontiguousarray(inputs["L0_attn_dt_bias"]).reshape(NH, 1)
    xs_sh = _shard_x(inputs["L0_attn_x"])
    cvec = np.array([[EPS_LN, 1.0, 0, 0, 0, 0, 0, 0]], np.float32)
    alog = np.ascontiguousarray(inputs["L0_attn_A_log"]).reshape(NH, 1)
    D = np.asarray(inputs["L0_attn_D"], np.float32)
    Drep = np.ascontiguousarray(np.concatenate(
        [np.tile(D[0::2], (64, 1)), np.tile(D[1::2], (64, 1))], axis=0))
    normw = np.ascontiguousarray(np.asarray(inputs["L0_attn_norm_w"], np.float32)
                                 .reshape(32, 128).T)
    mneg = np.where(np.arange(128)[None, :] >= np.arange(128)[:, None], 0.0,
                    -1e30).astype(np.float32).T.copy()
    w_outT = np.ascontiguousarray(inputs["L0_attn_W_out"].T)
    wqkvT = np.ascontiguousarray(inputs["L1_attn_wqkv"].T)
    wo1T = np.ascontiguousarray(inputs["L1_attn_wo"].T)
    ffnW = np.ascontiguousarray(inputs["L1_ffn_weight"])
    fscale = np.ascontiguousarray(inputs["L1_ffn_scale"]).reshape(1, DM)
    slopes = (2.0 ** (-0.5)) ** np.arange(1, 17)
    slrow = np.ascontiguousarray((-slopes).reshape(1, 16)).astype(np.float32)
    w_gate = np.ascontiguousarray(inputs["L0_ffn_w_gate"])
    wi_0 = np.ascontiguousarray(inputs["L0_ffn_wi_0"])
    wi_1 = np.ascontiguousarray(inputs["L0_ffn_wi_1"])
    wo = np.ascontiguousarray(inputs["L0_ffn_wo"])
    in_maps = []
    for c in range(NCORES):
        rm = np.zeros((NH, 16), np.float32)
        rm[:, 0:2 * c] = 1.0
        in_maps.append({
            "xh": xs_sh[c], "w_inT": w_inT, "conv_wr": conv_w, "conv_br": conv_b,
            "dt_bias": dt_bias, "cvec": cvec, "alog": alog,
            "w_gate": w_gate, "wi_0": wi_0, "wi_1": wi_1, "wo": wo,
            "rmask16": rm, "invmask16": np.ascontiguousarray(1.0 - rm),
            "Drep": Drep, "normw": normw, "masknegT": mneg,
            "w_outT": w_outT, "wqkvT": wqkvT, "ffnW": ffnW, "fscale": fscale,
            "wo1T": wo1T, "slrow": slrow})
    res = run_bass_kernel_spmd(nc, in_maps, core_ids=list(range(NCORES)))
    out = np.concatenate([np.asarray(res.results[c]["outp"]) for c in range(NCORES)], axis=1)
    return np.ascontiguousarray(out.astype(np.float32))


# version 07
# speedup vs baseline: 1.2361x; 1.0681x over previous
import numpy as np
import ml_dtypes
from contextlib import ExitStack
import concourse.bass as bass
import concourse.mybir as mybir
from concourse.bass_utils import run_bass_kernel_spmd

F32 = mybir.dt.float32
BF16 = mybir.dt.bfloat16
AF = mybir.ActivationFunctionType
ALU = mybir.AluOpType
AX = mybir.AxisListType
BF = ml_dtypes.bfloat16

NCORES = 8
BATCH = 2
SEQ = 8192
DM = 2048
SPC = SEQ // NCORES
HALO = 3
T2B = SPC + HALO
T2 = 2 * T2B
TPC = 2 * SPC
PROJ = 8512
IDIM = 4096
NH = 64
HPD = 64
NSD = 128
CONV_DIM = 4352
CK = 128
NCKB = SPC // CK
EPS_LN = 1e-5
EPS_RMS = 1e-5

BTGS = [(0, 512), (512, 512), (1024, 3)]
BTGS = [(0, 512), (512, 512), (1024, 3)]

RG8 = [[0, 1, 2, 3, 4, 5, 6, 7]]
PHX = [0]


class Prog:
    ENGS = ("sync", "vector", "scalar", "tensor", "gpsimd")

    def __init__(self):
        self.items = {e: [] for e in self.ENGS}

    def on(self, eng, fn):
        self.items[eng].append(fn)

    def emit(self, block):
        def mk(name):
            def body(eng):
                for fn in self.items[name]:
                    fn(eng)
            return body
        block.sync(mk("sync"))
        if self.items["vector"]:
            block.vector(mk("vector"))
        if self.items["scalar"]:
            block.scalar(mk("scalar"))
        if self.items["tensor"]:
            block.tensor(mk("tensor"))
        if self.items["gpsimd"]:
            block.gpsimd(mk("gpsimd"))


def phase1(nc, io):
    xh = io["xh"]; w_inT = io["w_inT"]; conv_w = io["conv_wr"]; conv_b = io["conv_br"]
    dt_bias = io["dt_bias"]; cvec = io["cvec"]
    o_xnT = io["d_xnT"]; o_xnTlo = io["d_xnTlo"]; o_sg = io["d_sg"]
    o_xs = io["d_xs"]; o_bc = io["d_bc"]; o_dt = io["d_dt"]

    prog = Prog(); PHX[0] += 1
    with ExitStack() as st:
        _ctr = [0]
        def sb(shape, dt=F32):
            _ctr[0] += 1
            return st.enter_context(nc.sbuf_tensor("sb%d_%d" % (PHX[0], _ctr[0]), list(shape), dt))
        def ps(shape, dt=F32):
            _ctr[0] += 1
            return st.enter_context(nc.psum_tensor("ps%d_%d" % (PHX[0], _ctr[0]), list(shape), dt))
        def sem(name):
            return st.enter_context(nc.semaphore("s%d_%s" % (PHX[0], name)))

        xnT = [sb((128, T2), BF16) for _ in range(16)]
        xnTlo = [sb((128, T2), BF16) for _ in range(16)]
        sx = [sb((128, DM)) for _ in range(2)]
        sxn = [sb((128, DM), BF16) for _ in range(2)]
        sxnlo = [sb((128, DM), BF16) for _ in range(2)]
        sxf = sb((128, DM))
        sqscr = None
        sumc, sqc, meanc, nvc, stdc, rstdc = (sb((128, 1)) for _ in range(6))
        identb = sb((128, 128), BF16)
        iota_p = sb((128, 1), mybir.dt.int32)
        iota_f = sb((128, 128), mybir.dt.int32)
        fidp = sb((128, 1))
        fidf = sb((128, 128))
        wbf = [sb((128, 16 * 128), BF16) for _ in range(2)]
        hbuf = sb((128, T2))
        conv_acc = sb((128, SPC))
        hbf = [sb((128, SPC), BF16) for _ in range(2)]
        sgst = [sb((128, SPC), BF16) for _ in range(2)]
        cw_sb = sb((128, 34 * 4))
        cb_sb = sb((128, 34))
        dtb_sb = sb((NH, 1))
        cv_sb = sb((128, 8))
        dtbuf = sb((NH, T2))
        dtv = dtbuf
        ptr = [ps((128, 128), BF16) for _ in range(2)]
        pproj = [ps((128, 512)) for _ in range(6)]

        ld = sem("ld"); gio = sem("gio"); idn = sem("idn")
        vA = sem("vA"); sA = sem("sA"); xnv = sem("xnv")
        trs = sem("trs"); evs = sem("evs")
        wld = sem("wld"); mmd = sem("mmd"); cons = sem("cons")
        csl = sem("csl"); dst = sem("dst"); dsg = sem("dsg"); dxb = sem("dxb")
        block = st.enter_context(nc.Block())

        def g_iota(e):
            e.iota(iota_p[:], pattern=[[0, 1]], base=0, channel_multiplier=1)
            e.iota(iota_f[:], pattern=[[1, 128]], base=0, channel_multiplier=0).then_inc(gio, 1)
        prog.on("gpsimd", g_iota)

        def v_ident(e):
            e.wait_ge(gio, 1)
            e.tensor_copy(out=fidp[:], in_=iota_p[:])
            e.tensor_copy(out=fidf[:], in_=iota_f[:])
            e.tensor_scalar(out=identb[:], in0=fidf[:], scalar1=fidp[:], scalar2=None,
                            op0=ALU.is_equal).then_inc(idn, 1)
        prog.on("vector", v_ident)

        def s_loads(e):
            e.dma_start(out=cw_sb[:, 0:136], in_=conv_w).then_inc(ld, 16)
            e.dma_start(out=cb_sb[:, 0:34], in_=conv_b).then_inc(ld, 16)
            e.dma_start(out=dtb_sb[:, :], in_=dt_bias).then_inc(ld, 16)
            e.dma_start(out=cv_sb[:, :], in_=cvec.to_broadcast((128, 8))).then_inc(ld, 16)
        prog.on("sync", s_loads)
        n_ld = 64
        C_EPSLN, C_ONE = 0, 1

        ttiles = []
        for b in range(BATCH):
            for i in range(9):
                ttiles.append((b, i * 128, 128 if i < 8 else T2B - 8 * 128))

        n_xn = 0
        n_tr = 0
        n_ev = 0
        n_vA = 0
        n_sA = 0
        gidx = 0

        for it, (b, r0, rows) in enumerate(ttiles):
            buf = it % 2
            need_xn = max(0, it - 1)
            def f_load(e, b=b, r0=r0, rows=rows, buf=buf, need_xn=need_xn, ldv=n_ld + 16):
                e.wait_ge(xnv, need_xn)
                e.dma_start(out=sx[buf][0:rows, :], in_=xh[b, r0:r0 + rows, :]).then_inc(ld, 16)
            prog.on("sync", f_load)
            n_ld += 16
            ldv = n_ld

            n_sA += 1
            def f_sq(e, rows=rows, buf=buf, ldv=ldv):
                e.wait_ge(ld, ldv)
                e.activation(out=hbuf[0:rows, 0:DM], in_=sx[buf][0:rows, :], func=AF.Square,
                             accum_out=sqc[0:rows, :]).then_inc(sA, 1)
            prog.on("scalar", f_sq)
            sAv = n_sA

            n_vA += 2
            def f_stats(e, rows=rows, buf=buf, ldv=ldv, sAv=sAv, w0=n_vA - 1):
                e.wait_ge(ld, ldv)
                e.reduce_sum(out=sumc[0:rows, :], in_=sx[buf][0:rows, :], axis=AX.X).then_inc(vA, 1)
                e.wait_ge(vA, w0)
                e.tensor_scalar(out=meanc[0:rows, :], in0=sumc[0:rows, :], scalar1=1.0 / DM,
                                scalar2=None, op0=ALU.mult)
                e.wait_ge(sA, sAv)
                e.tensor_scalar(out=nvc[0:rows, :], in0=sqc[0:rows, :], scalar1=1.0 / DM,
                                scalar2=None, op0=ALU.mult)
                e.scalar_tensor_tensor(out=nvc[0:rows, :], in0=meanc[0:rows, :],
                                       scalar=meanc[0:rows, :], in1=nvc[0:rows, :],
                                       op0=ALU.mult, op1=ALU.subtract).then_inc(vA, 1)
            prog.on("vector", f_stats)
            vAv = n_vA

            n_sA += 1
            def f_std(e, rows=rows, vAv=vAv):
                e.wait_ge(vA, vAv)
                e.activation(out=stdc[0:rows, :], in_=nvc[0:rows, :], func=AF.Sqrt,
                             scale=-1.0, bias=cv_sb[0:rows, C_EPSLN:C_EPSLN + 1]).then_inc(sA, 1)
            prog.on("scalar", f_std)
            sAv2 = n_sA

            need_tr = 32 * max(0, it - 1)
            n_xn += 1
            def f_xn(e, rows=rows, buf=buf, sAv2=sAv2, need_tr=need_tr):
                e.wait_ge(sA, sAv2)
                e.wait_ge(trs, need_tr)
                e.reciprocal(out=rstdc[0:rows, :], in_=stdc[0:rows, :])
                e.tensor_scalar(out=sxf[0:rows, :], in0=sx[buf][0:rows, :],
                                scalar1=meanc[0:rows, :], scalar2=rstdc[0:rows, :],
                                op0=ALU.subtract, op1=ALU.mult)
                e.tensor_copy(out=sxn[buf][0:rows, :], in_=sxf[0:rows, :])
                e.scalar_tensor_tensor(out=sxnlo[buf][0:rows, :], in0=sxn[buf][0:rows, :],
                                       scalar=-1.0, in1=sxf[0:rows, :],
                                       op0=ALU.mult, op1=ALU.add).then_inc(xnv, 1)
            prog.on("vector", f_xn)
            xnvv = n_xn

            tcol = b * T2B + r0
            for kg in range(16):
                lo = kg >= 8
                src_t = (sxnlo if lo else sxn)
                dst_t = (xnTlo if lo else xnT)
                kb = kg % 8
                need_ev = 2 * gidx
                n_tr += 2
                def f_tr(e, rows=rows, buf=buf, kb=kb, src_t=src_t, xnvv=xnvv, need_ev=need_ev):
                    e.wait_ge(xnv, xnvv)
                    e.wait_ge(idn, 1)
                    e.wait_ge(evs, need_ev)
                    last = None
                    for j in range(2):
                        last = e.transpose(ptr[j][0:128, 0:rows],
                                           src_t[buf][0:rows, (kb * 2 + j) * 128:(kb * 2 + j + 1) * 128],
                                           identb[0:rows, 0:rows])
                    last.then_inc(trs, 2)
                prog.on("tensor", f_tr)
                trv = n_tr
                n_ev += 2
                def f_ev(e, rows=rows, kb=kb, dst_t=dst_t, tcol=tcol, trv=trv):
                    e.wait_ge(trs, trv)
                    last = None
                    for j in range(2):
                        last = e.copy(out=dst_t[kb * 2 + j][:, tcol:tcol + rows],
                                      in_=ptr[j][0:128, 0:rows])
                    last.then_inc(evs, 2)
                prog.on("scalar", f_ev)
                gidx += 1

        all_ev = n_ev
        n_dst = 0
        for k in range(16):
            n_dst += 32
            def f_spill(e, k=k, all_ev=all_ev):
                e.wait_ge(evs, all_ev)
                e.dma_start(out=o_xnT[k * 128:(k + 1) * 128, :], in_=xnT[k][:, :]).then_inc(dst, 16)
                e.dma_start(out=o_xnTlo[k * 128:(k + 1) * 128, :], in_=xnTlo[k][:, :]).then_inc(dst, 16)
            prog.on("sync", f_spill)

        n_wld = 0
        n_mm = 0
        n_cons = 0
        n_csl = 0
        n_dsg = 0
        n_dxb = 0

        for ob in range(67):
            rows = 64 if ob == 66 else 128
            buf = ob % 2
            need_mm = 2 * max(0, ob - 1)
            n_wld += 16
            def f_wdma(e, ob=ob, rows=rows, buf=buf, need_mm=need_mm):
                e.wait_ge(mmd, need_mm)
                e.dma_start(out=wbf[buf][:, 0:16 * rows].rearrange("d (k o) -> d k o", o=rows),
                            in_=w_inT.rearrange("(k d) o -> d k o", d=128)[:, :, ob * 128:ob * 128 + rows]
                            ).then_inc(wld, 16)
            prog.on("gpsimd", f_wdma)
            wldv = n_wld

            for b in range(BATCH):
                u = 2 * ob + b
                pset = u % 2
                pbanks = pproj[pset * 3: pset * 3 + 3]
                need_cons = max(0, u - 1)
                n_mm += 1
                def f_mm(e, ob=ob, rows=rows, buf=buf, b=b, pbanks=pbanks, wldv=wldv,
                         need_cons=need_cons, all_ev=all_ev):
                    e.wait_ge(wld, wldv)
                    e.wait_ge(cons, need_cons)
                    e.wait_ge(evs, all_ev)
                    last = None
                    for tgi, (t0, tw) in enumerate(BTGS):
                        for k in range(16):
                            last = e.matmul(pbanks[tgi][0:rows, 0:tw],
                                            wbf[buf][:, k * rows:(k + 1) * rows],
                                            xnT[k][:, b * T2B + t0: b * T2B + t0 + tw],
                                            start=(k == 0), stop=(k == 15))
                    last.then_inc(mmd, 1)
                prog.on("tensor", f_mm)
                mmv = n_mm

                if ob < 32:
                    gu = 2 * ob + b
                    n_cons += 1
                    def f_gate(e, b=b, pbanks=pbanks, mmv=mmv, gu=gu):
                        e.wait_ge(mmd, mmv)
                        e.wait_ge(dsg, 16 * max(0, gu - 1))
                        e.activation(out=sgst[b][:, 0:509], in_=pbanks[0][:, 3:512], func=AF.Silu)
                        e.activation(out=sgst[b][:, 509:1021], in_=pbanks[1][:, 0:512], func=AF.Silu)
                        e.activation(out=sgst[b][:, 1021:1024], in_=pbanks[2][:, 0:3],
                                     func=AF.Silu).then_inc(cons, 1)
                    prog.on("scalar", f_gate)
                    def f_sgdma(e, ob=ob, b=b, need=n_cons):
                        e.wait_ge(cons, need)
                        e.dma_start(out=o_sg[ob * 128:(ob + 1) * 128, b * SPC:(b + 1) * SPC],
                                    in_=sgst[b][:, :]).then_inc(dsg, 16)
                    prog.on("sync", f_sgdma)
                    n_dsg += 16
                elif ob < 66:
                    hb = ob - 32
                    n_cons += 1
                    def f_hcp(e, b=b, pbanks=pbanks, mmv=mmv):
                        e.wait_ge(mmd, mmv)
                        last = None
                        for tgi, (t0, tw) in enumerate(BTGS):
                            last = e.tensor_copy(out=hbuf[:, b * T2B + t0:b * T2B + t0 + tw],
                                                 in_=pbanks[tgi][:, 0:tw])
                        last.then_inc(cons, 1)
                    prog.on("vector", f_hcp)
                    need_csl = n_csl
                    def f_conv(e, hb=hb, b=b, need_csl=need_csl):
                        e.wait_ge(csl, need_csl)
                        e.tensor_scalar(out=conv_acc[:, :], in0=hbuf[:, b * T2B + 0:b * T2B + SPC],
                                        scalar1=cw_sb[:, hb * 4:hb * 4 + 1],
                                        scalar2=cb_sb[:, hb:hb + 1], op0=ALU.mult, op1=ALU.add)
                        for k in range(1, 4):
                            e.scalar_tensor_tensor(out=conv_acc[:, :],
                                                   in0=hbuf[:, b * T2B + k:b * T2B + k + SPC],
                                                   scalar=cw_sb[:, hb * 4 + k:hb * 4 + k + 1],
                                                   in1=conv_acc[:, :], op0=ALU.mult, op1=ALU.add)
                        e.tensor_copy(out=conv_acc[0:1, 0:1], in_=conv_acc[0:1, 0:1]).then_inc(vA, 1)
                    prog.on("vector", f_conv)
                    n_vA += 1
                    cvv = n_vA
                    hu = 2 * hb + b
                    n_csl += 1
                    def f_hsilu(e, b=b, cvv=cvv, hu=hu):
                        e.wait_ge(vA, cvv)
                        e.wait_ge(dxb, 16 * max(0, hu - 1))
                        e.activation(out=hbf[b][:, :], in_=conv_acc[:, :], func=AF.Silu).then_inc(csl, 1)
                    prog.on("scalar", f_hsilu)
                    cslv = n_csl
                    if hb < 32:
                        tgt = o_xs[hb * 128:(hb + 1) * 128, b * SPC:(b + 1) * SPC]
                    else:
                        tgt = o_bc[(hb - 32) * 128:(hb - 31) * 128, b * SPC:(b + 1) * SPC]
                    def f_hdma(e, tgt=tgt, b=b, cslv=cslv):
                        e.wait_ge(csl, cslv)
                        e.dma_start(out=tgt, in_=hbf[b][:, :]).then_inc(dxb, 16)
                    prog.on("sync", f_hdma)
                    n_dxb += 16
                else:
                    n_cons += 1
                    def f_dtcp(e, b=b, pbanks=pbanks, mmv=mmv, rows=rows):
                        e.wait_ge(mmd, mmv)
                        last = None
                        for tgi, (t0, tw) in enumerate(BTGS):
                            last = e.tensor_copy(out=dtbuf[0:rows, b * T2B + t0:b * T2B + t0 + tw],
                                                 in_=pbanks[tgi][0:rows, 0:tw])
                        last.then_inc(cons, 1)
                    prog.on("vector", f_dtcp)
                    if b == 1:
                        need = n_cons
                        n_sA += 1
                        def f_dtsp(e, need=need):
                            e.wait_ge(cons, need)
                            e.activation(out=dtv[:, :], in_=dtbuf[:, :], func=AF.Exp,
                                         scale=1.0, bias=dtb_sb[:, :])
                            e.activation(out=dtv[:, :], in_=dtv[:, :], func=AF.Ln,
                                         scale=1.0, bias=cv_sb[0:NH, C_ONE:C_ONE + 1]).then_inc(sA, 1)
                        prog.on("scalar", f_dtsp)
                        sfin = n_sA
                        n_dst += 32
                        def f_dtdma(e, sfin=sfin):
                            e.wait_ge(sA, sfin)
                            for bb in range(2):
                                e.dma_start(out=o_dt[:, bb * SPC:(bb + 1) * SPC],
                                            in_=dtv[:, bb * T2B + HALO:bb * T2B + HALO + SPC]
                                            ).then_inc(dst, 16)
                        prog.on("sync", f_dtdma)

        def f_fin(e, a=n_dst, bb=n_dsg, c=n_dxb):
            e.wait_ge(dst, a)
            e.wait_ge(dsg, bb)
            e.wait_ge(dxb, c)
        prog.on("sync", f_fin)
        prog.emit(block)
    return nc



def phase2(nc, io):
    i_xs = io["d_xs"]; i_bc = io["d_bc"]; i_dt = io["d_dt"]; i_alog = io["alog"]
    o_xp = io["d_xp"]; o_sj = io["d_sj"]
    o_span = io["spn_loc"]; o_dspan = io["dsp_loc"]

    prog = Prog(); PHX[0] += 1
    with ExitStack() as st:
        _ctr = [0]
        def sb(shape, dt=F32):
            _ctr[0] += 1
            return st.enter_context(nc.sbuf_tensor("sb%d_%d" % (PHX[0], _ctr[0]), list(shape), dt))
        def ps(shape, dt=F32):
            _ctr[0] += 1
            return st.enter_context(nc.psum_tensor("ps%d_%d" % (PHX[0], _ctr[0]), list(shape), dt))
        def sem(name):
            return st.enter_context(nc.semaphore("s%d_%s" % (PHX[0], name)))

        dtv = sb((NH, TPC))
        alog = sb((NH, 1))
        acol = sb((NH, 1))
        dA = sb((NH, TPC))
        acum = sb((NH, TPC))
        dteh = sb((NH, CK))
        dteF = sb((NH, CK))
        dteT = sb((CK, NH))
        cdrow = sb((1, NH))
        cdB = sb((128, NH))
        onesf = sb((1, 128))
        dsum = sb((NH, BATCH))
        dspan = sb((NH, BATCH))
        identb = sb((128, 128), BF16)
        identf = sb((128, 128), F32)
        iota_p = sb((128, 1), mybir.dt.int32)
        iota_f = sb((128, 128), mybir.dt.int32)
        fidp = sb((128, 1))
        fidf = sb((128, 128))
        bc_sb = sb((128, TPC), BF16)
        xsch = [sb((128, IDIM), BF16) for _ in range(2)]
        xp_sb = [sb((128, IDIM), BF16) for _ in range(2)]
        xpp_sb = sb((128, IDIM), BF16)
        bt_sb = sb((128, 128), BF16)
        ssb = [sb((128, IDIM)) for _ in range(2)]
        tmp_sb = sb((128, 1024))
        SC = [sb((128, IDIM)) for _ in range(BATCH)]
        ptb = [ps((128, 128), BF16) for _ in range(2)]
        pcd = ps((128, 128))
        pst = [ps((128, 1024)) for _ in range(2)]

        ld = sem("ld"); gio = sem("gio")
        vv = sem("vv"); sv = sem("sv"); tv = sem("tv")
        mmv = sem("mmv"); cons = sem("cons"); dst = sem("dst"); dxp = sem("dxp")
        block = st.enter_context(nc.Block())

        n = dict(vv=0, sv=0, tv=0, mm=0, cons=0, dst=0, dxp=0, ld=0)

        def g_iota(e):
            e.iota(iota_p[:], pattern=[[0, 1]], base=0, channel_multiplier=1)
            e.iota(iota_f[:], pattern=[[1, 128]], base=0, channel_multiplier=0).then_inc(gio, 1)
        prog.on("gpsimd", g_iota)

        n["vv"] += 1
        def v_init(e):
            e.wait_ge(gio, 1)
            e.tensor_copy(out=fidp[:], in_=iota_p[:])
            e.tensor_copy(out=fidf[:], in_=iota_f[:])
            e.tensor_scalar(out=identb[:], in0=fidf[:], scalar1=fidp[:], scalar2=None,
                            op0=ALU.is_equal)
            e.tensor_scalar(out=identf[:], in0=fidf[:], scalar1=fidp[:], scalar2=None,
                            op0=ALU.is_equal)
            e.memset(onesf[:], 1.0)
            e.memset(SC[0][:, :], 0.0)
            e.memset(SC[1][:, :], 0.0).then_inc(vv, 1)
        prog.on("vector", v_init)
        init_v = n["vv"]

        def s_loads(e):
            e.dma_start(out=dtv[:, :], in_=i_dt).then_inc(ld, 16)
            e.dma_start(out=alog[:, :], in_=i_alog).then_inc(ld, 16)
            e.dma_start(out=bc_sb[:, :], in_=i_bc[0:128, :]).then_inc(ld, 16)
        prog.on("sync", s_loads)
        n["ld"] += 48

        n["sv"] += 1
        def s_A(e):
            e.wait_ge(ld, 48)
            e.activation(out=acol[:, :], in_=alog[:, :], func=AF.Exp, scale=1.0).then_inc(sv, 1)
        prog.on("scalar", s_A)

        n["vv"] += 3
        def v_dA(e, w0=n["vv"]):
            e.wait_ge(sv, 1)
            e.tensor_scalar(out=dA[:, :], in0=dtv[:, :], scalar1=acol[:, :], scalar2=-1.0,
                            op0=ALU.mult, op1=ALU.mult)
            for u in range(16):
                c0 = u * CK
                e.tensor_tensor_scan(out=acum[:, c0:c0 + CK], data0=dA[:, c0:c0 + CK],
                                     data1=dA[:, c0:c0 + CK], initial=0.0,
                                     op0=ALU.add, op1=ALU.bypass)
            e.reduce_sum(out=dsum[:, 0:1], in_=dA[:, 0:SPC], axis=AX.X).then_inc(vv, 1)
            e.reduce_sum(out=dsum[:, 1:2], in_=dA[:, SPC:TPC], axis=AX.X).then_inc(vv, 1)
            e.wait_ge(vv, w0 - 1)
            e.tensor_copy(out=dsum[0:1, 0:1], in_=dsum[0:1, 0:1]).then_inc(vv, 1)
        prog.on("vector", v_dA)
        acum_v = n["vv"]

        n["sv"] += 1
        def s_dspan(e, w=acum_v):
            e.wait_ge(vv, w)
            e.activation(out=dspan[:, :], in_=dsum[:, :], func=AF.Exp).then_inc(sv, 1)
        prog.on("scalar", s_dspan)
        n["dst"] += 16
        def f_dspan_dma(e, w=n["sv"]):
            e.wait_ge(sv, w)
            e.dma_start(out=o_dspan[:, 0:2], in_=dspan[:, :]).then_inc(dst, 16)
        prog.on("sync", f_dspan_dma)

        trdone = {}
        dxp_after = {}
        sj_after = {}
        sev_after = []
        pcd_free = [n["sv"]]

        for u in range(16):
            b, j = u // NCKB, u % NCKB
            c0 = b * SPC + j * CK
            buf = u % 2

            need_tv = trdone.get(u - 2, 0)
            n["ld"] += 16
            def f_xld(e, c0=c0, buf=buf, need_tv=need_tv):
                e.wait_ge(tv, need_tv)
                e.dma_start(out=xsch[buf][:, :].rearrange("p (pb s) -> p pb s", pb=32),
                            in_=i_xs.rearrange("(pb p) t -> p pb t", p=128)[:, :, c0:c0 + CK]
                            ).then_inc(ld, 16)
            prog.on("sync", f_xld)
            xldv = n["ld"]

            n["sv"] += 1
            def f_dteh(e, c0=c0, w=acum_v):
                e.wait_ge(vv, w)
                e.activation(out=dteh[:, :], in_=acum[:, c0:c0 + CK], func=AF.Exp,
                             scale=-1.0, bias=acum[:, c0 + CK - 1:c0 + CK]).then_inc(sv, 1)
            prog.on("scalar", f_dteh)
            dteh_v = n["sv"]

            n["vv"] += 1
            def f_dteF(e, c0=c0, w=dteh_v):
                e.wait_ge(sv, w)
                e.tensor_tensor(out=dteF[:, :], in0=dteh[:, :], in1=dtv[:, c0:c0 + CK],
                                op=ALU.mult).then_inc(vv, 1)
            prog.on("vector", f_dteF)
            dteF_v = n["vv"]

            n["tv"] += 1
            def f_dtetr(e, w=dteF_v, wfree=pcd_free[0]):
                e.wait_ge(vv, w)
                e.wait_ge(sv, wfree)
                e.transpose(pcd[0:128, 0:NH], dteF[:, :], identf[0:NH, 0:NH]).then_inc(tv, 1)
            prog.on("tensor", f_dtetr)
            dtetr_v = n["tv"]

            n["sv"] += 1
            def f_dteT_ev(e, w=dtetr_v):
                e.wait_ge(tv, w)
                e.copy(out=dteT[:, :], in_=pcd[0:128, 0:NH]).then_inc(sv, 1)
            prog.on("scalar", f_dteT_ev)
            dteT_v = n["sv"]
            pcd_free[0] = dteT_v

            need_dxp = dxp_after.get(u - 2, 0)
            for pairi in range(16):
                evg = n["vv"]
                n["tv"] += 2
                def f_tr(e, buf=buf, pairi=pairi, xldv=xldv, evg=evg, need_dxp=need_dxp):
                    e.wait_ge(ld, xldv)
                    e.wait_ge(vv, evg)
                    e.wait_ge(dxp, need_dxp)
                    last = None
                    for q in range(2):
                        pb = pairi * 2 + q
                        last = e.transpose(ptb[q][0:128, 0:128],
                                           xsch[buf][:, pb * 128:(pb + 1) * 128], identb[:])
                    last.then_inc(tv, 2)
                prog.on("tensor", f_tr)
                trv = n["tv"]
                n["vv"] += 1
                def f_ev(e, buf=buf, pairi=pairi, trv=trv):
                    e.wait_ge(tv, trv)
                    last = None
                    for q in range(2):
                        pb = pairi * 2 + q
                        last = e.tensor_copy(out=xp_sb[buf][:, pb * 128:(pb + 1) * 128],
                                             in_=ptb[q][0:128, 0:128])
                    last.then_inc(vv, 1)
                prog.on("vector", f_ev)
            trdone[u] = n["tv"]
            xptr_v = n["vv"]

            n["dxp"] += 16
            def f_xpdma(e, u=u, buf=buf, w=xptr_v):
                e.wait_ge(vv, w)
                e.dma_start(out=o_xp[u, :, :], in_=xp_sb[buf][:, :]).then_inc(dxp, 16)
            prog.on("sync", f_xpdma)
            dxp_after[u] = n["dxp"]

            need_mm = 4 * u
            n["vv"] += 1
            def f_xpp(e, buf=buf, w2=dteT_v, need_mm=need_mm):
                e.wait_ge(sv, w2)
                e.wait_ge(mmv, need_mm)
                e.tensor_tensor(out=xpp_sb[:, :].rearrange("s (h p) -> s h p", h=NH),
                                in0=xp_sb[buf][:, :].rearrange("s (h p) -> s h p", h=NH),
                                in1=dteT[:, :].unsqueeze(2).to_broadcast((128, NH, HPD)),
                                op=ALU.mult).then_inc(vv, 1)
            prog.on("vector", f_xpp)
            xpp_v = n["vv"]

            n["tv"] += 1
            def f_btr(e, c0=c0, evg=xptr_v):
                e.wait_ge(vv, evg)
                e.transpose(ptb[0][0:128, 0:128], bc_sb[:, c0:c0 + CK], identb[:]).then_inc(tv, 1)
            prog.on("tensor", f_btr)
            btr_v = n["tv"]
            n["vv"] += 1
            def f_btev(e, w=btr_v, need_mm=need_mm):
                e.wait_ge(tv, w)
                e.wait_ge(mmv, need_mm)
                e.tensor_copy(out=bt_sb[:, :], in_=ptb[0][0:128, 0:128]).then_inc(vv, 1)
            prog.on("vector", f_btev)
            bt_v = n["vv"]

            n["tv"] += 1
            def f_cdtr(e, c0=c0, wfree=pcd_free[0], wa=acum_v):
                e.wait_ge(vv, wa)
                e.wait_ge(sv, wfree)
                e.transpose(pcd[0:1, 0:NH], acum[:, c0 + CK - 1:c0 + CK],
                            identf[0:NH, 0:NH]).then_inc(tv, 1)
            prog.on("tensor", f_cdtr)
            cdtr_v = n["tv"]
            n["sv"] += 1
            def f_cdrow(e, w=cdtr_v):
                e.wait_ge(tv, w)
                e.copy(out=cdrow[:, :], in_=pcd[0:1, 0:NH]).then_inc(sv, 1)
            prog.on("scalar", f_cdrow)
            cdrow_v = n["sv"]
            pcd_free[0] = cdrow_v

            n["tv"] += 1
            def f_cdmm(e, w=cdrow_v, wfree=pcd_free[0]):
                e.wait_ge(sv, wfree)
                e.matmul(pcd[0:128, 0:NH], onesf[:, :], cdrow[:, :], start=True,
                         stop=True).then_inc(tv, 1)
            prog.on("tensor", f_cdmm)
            cdmm_v = n["tv"]
            n["sv"] += 1
            need_scans = n["cons"]
            def f_cdB(e, w=cdmm_v, need_scans=need_scans):
                e.wait_ge(tv, w)
                e.wait_ge(cons, need_scans)
                e.activation(out=cdB[:, :], in_=pcd[0:128, 0:NH], func=AF.Exp).then_inc(sv, 1)
            prog.on("scalar", f_cdB)
            cdB_v = n["sv"]
            pcd_free[0] = cdB_v

            need_sj = sj_after.get(u - 2, 0)
            for hg in range(4):
                g = u * 4 + hg
                pz = pst[g % 2]
                n["mm"] += 1
                def f_smm(e, hg=hg, pz=pz, w1=bt_v, w2=xpp_v, g=g,
                          need_cons=max(0, g - 1),
                          need_sev=(sev_after[g - 2] if g >= 2 else 0)):
                    e.wait_ge(vv, max(w1, w2))
                    e.wait_ge(cons, need_cons)
                    e.wait_ge(sv, need_sev)
                    last = None
                    for hh in range(16):
                        h = hg * 16 + hh
                        last = e.matmul(pz[0:128, hh * 64:(hh + 1) * 64], bt_sb[:, :],
                                        xpp_sb[:, h * 64:(h + 1) * 64], start=True, stop=True)
                    last.then_inc(mmv, 1)
                prog.on("tensor", f_smm)
                smm_v = n["mm"]

                n["sv"] += 1
                def f_sev(e, hg=hg, pz=pz, buf=buf, w=smm_v, need_sj=need_sj):
                    e.wait_ge(mmv, w)
                    e.wait_ge(dst, need_sj)
                    e.copy(out=ssb[buf][:, hg * 1024:(hg + 1) * 1024], in_=pz[:, :]).then_inc(sv, 1)
                prog.on("scalar", f_sev)
                sev_after.append(n["sv"])

                n["vv"] += 1
                n["cons"] += 1
                def f_scan(e, b=b, hg=hg, pz=pz, w=smm_v, wcd=cdB_v):
                    e.wait_ge(mmv, w)
                    e.wait_ge(sv, wcd)
                    e.tensor_tensor(out=tmp_sb[:, :].rearrange("n (h p) -> n h p", h=16),
                                    in0=SC[b][:, hg * 1024:(hg + 1) * 1024].rearrange(
                                        "n (h p) -> n h p", h=16),
                                    in1=cdB[:, hg * 16:(hg + 1) * 16].unsqueeze(2)
                                    .to_broadcast((128, 16, HPD)),
                                    op=ALU.mult).then_inc(vv, 1)
                    e.tensor_tensor(out=SC[b][:, hg * 1024:(hg + 1) * 1024], in0=tmp_sb[:, :],
                                    in1=pz[:, :], op=ALU.add).then_inc(cons, 1)
                prog.on("vector", f_scan)

            n["dst"] += 16
            def f_sjdma(e, u=u, buf=buf, w=n["sv"]):
                e.wait_ge(sv, w)
                e.dma_start(out=o_sj[u, :, :], in_=ssb[buf][:, :]).then_inc(dst, 16)
            prog.on("sync", f_sjdma)
            sj_after[u] = n["dst"]

        for b in range(BATCH):
            n["dst"] += 16
            def f_span(e, b=b, w=n["cons"]):
                e.wait_ge(cons, w)
                e.dma_start(out=o_span[b * 128:(b + 1) * 128, :], in_=SC[b][:, :]).then_inc(dst, 16)
            prog.on("sync", f_span)

        def f_fin(e, a=n["dst"], bx=n["dxp"]):
            e.wait_ge(dst, a)
            e.wait_ge(dxp, bx)
        prog.on("sync", f_fin)
        prog.emit(block)
    return nc


def phase3(nc, io):
    i_xnT = io["d_xnT"]; i_xnTlo = io["d_xnTlo"]
    i_wg = io["w_gate"]; i_wi0 = io["wi_0"]; i_wi1 = io["wi_1"]; i_wo = io["wo"]
    o_g = io["d_g"]; o_dbg = io["d_dbg"]

    prog = Prog(); PHX[0] += 1
    with ExitStack() as st:
        _ctr = [0]
        def sb(shape, dt=F32):
            _ctr[0] += 1
            return st.enter_context(nc.sbuf_tensor("sb%d_%d" % (PHX[0], _ctr[0]), list(shape), dt))
        def ps(shape, dt=F32):
            _ctr[0] += 1
            return st.enter_context(nc.psum_tensor("ps%d_%d" % (PHX[0], _ctr[0]), list(shape), dt))
        def sem(name):
            return st.enter_context(nc.semaphore("s%d_%s" % (PHX[0], name)))

        xnT = [sb((128, T2), BF16) for _ in range(16)]
        lo_sb = [sb((128, DM), BF16) for _ in range(2)]
        wgf = sb((128, 128))
        wghi = sb((128, 128), BF16)
        wglo = sb((128, 128), BF16)
        lg = sb((128, 8))
        m1 = sb((128, 1)); nm1 = sb((128, 1)); m2 = sb((128, 1))
        ge1 = sb((128, 8)); lmask = sb((128, 8)); usel = sb((128, 8))
        e2c = sb((128, 1)); denc = sb((128, 1)); recipc = sb((128, 1))
        wall = sb((128, 128))
        w0buf = [sb((128, 16 * 128), BF16) for _ in range(2)]
        w1buf = [sb((128, 16 * 128), BF16) for _ in range(2)]
        wobuf = [sb((128, 8 * 512), BF16) for _ in range(2)]
        sil = sb((128, 512), BF16)
        inter = sb((128, 8 * 1024), BF16)
        g_acc = sb((128, 8 * 2048))
        gst = [sb((128, DM), BF16) for _ in range(2)]
        plog = ps((128, 8))
        ph0 = ps((128, 512))
        ph1 = ps((128, 512))
        pout = [ps((128, 512)) for _ in range(2)]

        gv = sem("gv"); ld = sem("ld"); llo = sem("llo")
        vv = sem("vv"); sv = sem("sv"); mmv = sem("mmv")
        cons = sem("cons"); sps = sem("sps"); dgs = sem("dgs")
        lwi = sem("lwi"); lwo = sem("lwo")
        agS = sem("agS"); agD = sem("agD")
        block = st.enter_context(nc.Block())

        n = dict(gv=0, ld=0, llo=0, vv=0, sv=0, mm=0, cons=0, sps=0, dgs=0, lwi=0, lwo=0)

        def g_ag(e):
            e.collective_compute(
                "AllGather", ALU.bypass, replica_groups=RG8,
                ins=[io["spn_loc"].opt()], outs=[io["spn_all"].opt()]).then_inc(agS)
            e.collective_compute(
                "AllGather", ALU.bypass, replica_groups=RG8,
                ins=[io["dsp_loc"].opt()], outs=[io["dsp_all"].opt()]).then_inc(agD)
        prog.on("gpsimd", g_ag)


        def s_loads(e):
            for k in range(16):
                e.dma_start(out=xnT[k][:, :], in_=i_xnT[k * 128:(k + 1) * 128, :]).then_inc(ld, 16)
            e.dma_start(out=wgf[:, :].rearrange("d (k e8) -> d k e8", e8=8),
                        in_=i_wg.rearrange("(k d) e8 -> d k e8", d=128)).then_inc(ld, 16)
        prog.on("sync", s_loads)
        n["ld"] = 17 * 16

        n["vv"] += 1
        def v_wg(e):
            e.wait_ge(ld, 17 * 16)
            e.tensor_copy(out=wghi[:, :], in_=wgf[:, :])
            e.scalar_tensor_tensor(out=wglo[:, :], in0=wghi[:, :], scalar=-1.0, in1=wgf[:, :],
                                   op0=ALU.mult, op1=ALU.add).then_inc(vv, 1)
        prog.on("vector", v_wg)
        wg_v = n["vv"]

        def tcol_of(tti):
            b, tt = tti // 8, tti % 8
            return b * T2B + HALO + tt * 128

        for tti in range(16):
            tcol = tcol_of(tti)
            buf = tti % 2
            n["llo"] += 16
            def f_lolod(e, tcol=tcol, buf=buf, need_mm=max(0, n["mm"] - 1)):
                e.wait_ge(mmv, need_mm)
                e.dma_start(out=lo_sb[buf][:, :].rearrange("d (k t) -> d k t", k=16),
                            in_=i_xnTlo.rearrange("(k d) t -> d k t", d=128)[:, :, tcol:tcol + 128]
                            ).then_inc(llo, 16)
            prog.on("sync", f_lolod)
            lov = n["llo"]

            n["mm"] += 1
            def f_lgmm(e, tcol=tcol, buf=buf, lov=lov, wgv=wg_v, need_t2=n["vv"]):
                e.wait_ge(llo, lov)
                e.wait_ge(ld, 17 * 16)
                e.wait_ge(vv, need_t2)
                combos = []
                for k in range(16):
                    combos += [(xnT[k][:, tcol:tcol + 128], wghi[:, k * 8:(k + 1) * 8]),
                               (xnT[k][:, tcol:tcol + 128], wglo[:, k * 8:(k + 1) * 8]),
                               (lo_sb[buf][:, k * 128:(k + 1) * 128], wghi[:, k * 8:(k + 1) * 8]),
                               (lo_sb[buf][:, k * 128:(k + 1) * 128], wglo[:, k * 8:(k + 1) * 8])]
                last = None
                for ci, (stat, mov) in enumerate(combos):
                    last = e.matmul(plog[:, :], stat, mov, start=(ci == 0),
                                    stop=(ci == len(combos) - 1))
                last.then_inc(mmv, 1)
            prog.on("tensor", f_lgmm)
            lgv = n["mm"]

            base = n["vv"]
            n["vv"] += 6
            def f_top2a(e, lgv=lgv, base=base):
                e.wait_ge(mmv, lgv)
                e.tensor_copy(out=lg[:, :], in_=plog[:, :]).then_inc(vv, 1)
                e.wait_ge(vv, base + 1)
                e.reduce_max(out=m1[:, :], in_=lg[:, :], axis=AX.X).then_inc(vv, 1)
                e.wait_ge(vv, base + 2)
                e.tensor_scalar(out=ge1[:, :], in0=lg[:, :], scalar1=m1[:, :], scalar2=None,
                                op0=ALU.is_ge)
                e.tensor_scalar(out=nm1[:, :], in0=m1[:, :], scalar1=-1.0, scalar2=None,
                                op0=ALU.mult).then_inc(vv, 1)
                e.wait_ge(vv, base + 3)
                e.scalar_tensor_tensor(out=lmask[:, :], in0=ge1[:, :], scalar=-1e30, in1=lg[:, :],
                                       op0=ALU.mult, op1=ALU.add).then_inc(vv, 1)
                e.wait_ge(vv, base + 4)
                e.reduce_max(out=m2[:, :], in_=lmask[:, :], axis=AX.X).then_inc(vv, 1)
                e.wait_ge(vv, base + 5)
                e.tensor_copy(out=m2[0:1, 0:1], in_=m2[0:1, 0:1]).then_inc(vv, 1)
            prog.on("vector", f_top2a)
            t2a = n["vv"]

            n["sv"] += 1
            def f_top2s(e, w=t2a):
                e.wait_ge(vv, w)
                e.activation(out=usel[:, :], in_=lg[:, :], func=AF.Exp, scale=1.0, bias=nm1[:, :])
                e.activation(out=e2c[:, :], in_=m2[:, :], func=AF.Exp, scale=1.0,
                             bias=nm1[:, :]).then_inc(sv, 1)
            prog.on("scalar", f_top2s)
            t2s = n["sv"]

            base2 = n["vv"]
            n["vv"] += 5
            def f_top2b(e, tti=tti, w=t2s, base2=base2):
                e.wait_ge(sv, w)
                e.tensor_scalar(out=denc[:, :], in0=e2c[:, :], scalar1=1.0, scalar2=None,
                                op0=ALU.add).then_inc(vv, 1)
                e.wait_ge(vv, base2 + 1)
                e.reciprocal(out=recipc[:, :], in_=denc[:, :]).then_inc(vv, 1)
                e.wait_ge(vv, base2 + 2)
                e.tensor_scalar(out=lmask[:, :], in0=lg[:, :], scalar1=m2[:, :], scalar2=None,
                                op0=ALU.is_ge).then_inc(vv, 1)
                e.wait_ge(vv, base2 + 3)
                e.tensor_tensor(out=usel[:, :], in0=usel[:, :], in1=lmask[:, :],
                                op=ALU.mult).then_inc(vv, 1)
                e.wait_ge(vv, base2 + 4)
                e.tensor_scalar(out=wall[:, tti * 8:(tti + 1) * 8], in0=usel[:, :],
                                scalar1=recipc[:, :], scalar2=None, op0=ALU.mult).then_inc(vv, 1)
            prog.on("vector", f_top2b)
        top2_done_v = n["vv"]
        top2_done_mm = n["mm"]

        gdma_after_half = {}
        silu_after_half = {}
        sps_hist = []
        cons_hist = []
        ga_hist = []
        wo_used = []
        hm_after = {}
        out_done = {0: top2_done_mm}

        for half in range(2):
            for ei in range(8):
                for ib in range(8):
                    ibu = (half * 8 + ei) * 8 + ib
                    wb = ibu % 2
                    n["lwi"] += 32
                    def f_wil(e, ei=ei, ib=ib, wb=wb, need_mm=hm_after.get(ibu - 2, 0)):
                        e.wait_ge(mmv, need_mm)
                        e.dma_start(out=w0buf[wb][:, :].rearrange("d (k i) -> d k i", k=16),
                                    in_=i_wi0.rearrange("e8 (k d) i -> e8 d k i", d=128)[
                                        ei, :, :, ib * 128:(ib + 1) * 128]).then_inc(lwi, 16)
                        e.dma_start(out=w1buf[wb][:, :].rearrange("d (k i) -> d k i", k=16),
                                    in_=i_wi1.rearrange("e8 (k d) i -> e8 d k i", d=128)[
                                        ei, :, :, ib * 128:(ib + 1) * 128]).then_inc(lwi, 16)
                    prog.on("gpsimd", f_wil)
                    wiv = n["lwi"]

                    for tg in range(2):
                        tc0 = half * T2B * 0 + (half * T2B) + HALO + tg * 512
                        n["mm"] += 1
                        def f_h0(e, ib=ib, wb=wb, tc0=tc0, wiv=wiv,
                                 need_sps=(sps_hist[-1] if sps_hist else 0),
                                 need_cons=(cons_hist[-1] if cons_hist else 0),
                                 w2v=top2_done_v):
                            e.wait_ge(lwi, wiv)
                            e.wait_ge(vv, w2v)
                            e.wait_ge(sps, need_sps)
                            e.wait_ge(cons, need_cons)
                            last = None
                            for k in range(16):
                                last = e.matmul(ph0[:, :], w0buf[wb][:, k * 128:(k + 1) * 128],
                                                xnT[k][:, tc0:tc0 + 512], start=(k == 0), stop=(k == 15))
                            for k in range(16):
                                last = e.matmul(ph1[:, :], w1buf[wb][:, k * 128:(k + 1) * 128],
                                                xnT[k][:, tc0:tc0 + 512], start=(k == 0), stop=(k == 15))
                            last.then_inc(mmv, 1)
                        prog.on("tensor", f_h0)
                        hv = n["mm"]

                        n["sps"] += 1
                        def f_sil(e, w=hv):
                            e.wait_ge(mmv, w)
                            e.activation(out=sil[:, :], in_=ph0[:, :], func=AF.Silu).then_inc(sps, 1)
                        prog.on("scalar", f_sil)
                        sps_hist.append(n["sps"])

                        n["cons"] += 1
                        def f_int(e, ib=ib, tg=tg, w=hv, ws=n["sps"],
                                  need_out=out_done.get(half * 8 + ei - 1, 0) if True else 0):
                            e.wait_ge(mmv, max(w, need_out))
                            e.wait_ge(sps, ws)
                            e.tensor_tensor(out=inter[:, ib * 1024 + tg * 512: ib * 1024 + (tg + 1) * 512],
                                            in0=sil[:, :], in1=ph1[:, :], op=ALU.mult).then_inc(cons, 1)
                        prog.on("vector", f_int)
                        cons_hist.append(n["cons"])
                    hm_after[ibu] = n["mm"]

                inter_done = n["cons"]
                for dg in range(4):
                    dgu = ((half * 8 + ei) * 4 + dg)
                    wb = dgu % 2
                    n["lwo"] += 16
                    def f_wol(e, ei=ei, dg=dg, wb=wb,
                              need_mm=(wo_used[dgu - 2] if dgu >= 2 else 0)):
                        e.wait_ge(mmv, need_mm)
                        e.dma_start(out=wobuf[wb][:, :].rearrange("i (ib d) -> i ib d", ib=8),
                                    in_=i_wo.rearrange("e8 (ib i) d -> e8 i ib d", i=128)[
                                        ei, :, :, dg * 512:(dg + 1) * 512]).then_inc(lwo, 16)
                    prog.on("gpsimd", f_wol)
                    wov = n["lwo"]

                    for tt in range(8):
                        pz = pout[(dgu * 8 + tt) % 2]
                        n["mm"] += 1
                        def f_omm(e, tt=tt, wb=wb, pz=pz, wov=wov, wint=inter_done,
                                  need_ga=(ga_hist[-2] if len(ga_hist) >= 2 else 0)):
                            e.wait_ge(lwo, wov)
                            e.wait_ge(cons, wint)
                            e.wait_ge(vv, need_ga)
                            last = None
                            for ib in range(8):
                                last = e.matmul(pz[:, :], inter[:, ib * 1024 + tt * 128: ib * 1024 + tt * 128 + 128],
                                                wobuf[wb][:, ib * 512:(ib + 1) * 512],
                                                start=(ib == 0), stop=(ib == 7))
                            last.then_inc(mmv, 1)
                        prog.on("tensor", f_omm)
                        ov = n["mm"]

                        n["vv"] += 1
                        def f_ga(e, half=half, ei=ei, dg=dg, tt=tt, pz=pz, w=ov,
                                 need_dg=gdma_after_half.get(half - 1, 0)):
                            e.wait_ge(mmv, w)
                            e.wait_ge(dgs, need_dg)
                            tti = half * 8 + tt
                            wcol = wall[:, tti * 8 + ei: tti * 8 + ei + 1]
                            tgt = g_acc[:, tt * 2048 + dg * 512: tt * 2048 + (dg + 1) * 512]
                            if ei == 0:
                                e.tensor_scalar(out=tgt, in0=pz[:, :], scalar1=wcol, scalar2=None,
                                                op0=ALU.mult).then_inc(vv, 1)
                            else:
                                e.scalar_tensor_tensor(out=tgt, in0=pz[:, :], scalar=wcol, in1=tgt,
                                                       op0=ALU.mult, op1=ALU.add).then_inc(vv, 1)
                        prog.on("vector", f_ga)
                        ga_hist.append(n["vv"])
                    wo_used.append(n["mm"])
                out_done[half * 8 + ei] = n["mm"]

            ga_done = n["vv"]
            for tt in range(8):
                gb = tt % 2
                n["sv"] += 1
                def f_gsil(e, tt=tt, gb=gb, w=ga_done, need_dg=max(0, n["dgs"] - 16)):
                    e.wait_ge(vv, w)
                    e.wait_ge(dgs, need_dg)
                    e.activation(out=gst[gb][:, :], in_=g_acc[:, tt * 2048:(tt + 1) * 2048],
                                 func=AF.Silu).then_inc(sv, 1)
                prog.on("scalar", f_gsil)
                gsv = n["sv"]
                n["dgs"] += 16
                def f_gdma(e, half=half, tt=tt, gb=gb, w=gsv):
                    e.wait_ge(sv, w)
                    e.dma_start(out=o_g[(half * 8 + tt) * 128:(half * 8 + tt + 1) * 128, :],
                                in_=gst[gb][:, :]).then_inc(dgs, 16)
                prog.on("sync", f_gdma)
            gdma_after_half[half] = n["dgs"]

        def f_dbg(e, w=top2_done_v):
            e.wait_ge(vv, w)
            e.dma_start(out=o_dbg, in_=wall[:, :]).then_inc(dgs, 16)
        prog.on("sync", f_dbg)
        n["dgs"] += 16

        def f_fin(e, a=n["dgs"]):
            e.wait_ge(dgs, a)
        prog.on("sync", f_fin)

        def g_agw(e):
            e.wait_ge(agS, 1)
            e.wait_ge(agD, 1)
        prog.on("gpsimd", g_agw)
        prog.emit(block)
    return nc


def phase4a(nc, io):
    i_xp = io["d_xp"]; i_sj = io["d_sj"]; i_xs = io["d_xs"]; i_bc = io["d_bc"]
    i_sg = io["d_sg"]; i_dt = io["d_dt"]; i_alog = io["alog"]
    i_sall = io["spn_all"]; i_dall = io["dsp_all"]
    i_rm = io["rmask16"]; i_im = io["invmask16"]; i_drep = io["Drep"]
    i_nw = io["normw"]; i_mneg = io["masknegT"]; i_cv = io["cvec"]
    o_yn = io["d_yn"]; acsp = io["acsp_h"]

    prog = Prog(); PHX[0] += 1
    with ExitStack() as st:
        _ctr = [0]
        def sb(shape, dt=F32):
            _ctr[0] += 1
            return st.enter_context(nc.sbuf_tensor("sb%d_%d" % (PHX[0], _ctr[0]), list(shape), dt))
        def ps(shape, dt=F32):
            _ctr[0] += 1
            return st.enter_context(nc.psum_tensor("ps%d_%d" % (PHX[0], _ctr[0]), list(shape), dt))
        def sem(name):
            return st.enter_context(nc.semaphore("s%d_%s" % (PHX[0], name)))

        dtv = sb((NH, TPC))
        alog = sb((NH, 1))
        acum = sb((NH, TPC))
        rsplit = sb((NH, TPC))
        lndt = rsplit
        splA = sb((NH, TPC), BF16)
        splB = sb((NH, TPC), BF16)
        dall = sb((NH, 16)); rm16 = sb((NH, 16)); im16 = sb((NH, 16))
        dprm = sb((NH, 16)); qsuf = sb((NH, 16)); qm = sb((NH, 16))
        qrow = sb((1, NH))
        cdrow = sb((1, NH)); cdB = sb((128, NH))
        lndtT = sb((CK, NH))
        drep = sb((128, 32)); nw = sb((128, 32)); cv = sb((128, 8))
        mnegf = sb((128, 128)); mnegb = sb((128, 128), BF16)
        ones3 = sb((3, 128), BF16)
        onesf = sb((1, 128))
        onescol = sb((128, 1), BF16)
        identb = sb((128, 128), BF16)
        identf = sb((128, 128), F32)
        iota_p = sb((128, 1), mybir.dt.int32); iota_f = sb((128, 128), mybir.dt.int32)
        fidp = sb((128, 1)); fidf = sb((128, 128))
        b_sb = sb((128, TPC), BF16); c_sb = sb((128, TPC), BF16)
        SC = [sb((128, IDIM)) for _ in range(BATCH)]
        prevb = [sb((128, IDIM), BF16) for _ in range(2)]
        tmp_sb = sb((128, IDIM))
        xpch = [sb((128, IDIM), BF16) for _ in range(2)]
        sjch = sb((128, IDIM))
        sld = sjch
        xsch = sb((128, IDIM), BF16)
        sgch = sb((128, IDIM), BF16)
        stat6 = [sb((6, 16 * CK), BF16) for _ in range(2)]
        mov6 = [sb((6, 16 * CK), BF16) for _ in range(2)]
        etop = [sb((128, 512), BF16) for _ in range(2)]
        ebot = [sb((128, 512), BF16) for _ in range(2)]
        mtop = [sb((128, 512), BF16) for _ in range(2)]
        mbot = [sb((128, 512), BF16) for _ in range(2)]
        cbt = sb((128, CK), BF16)
        ych = [sb((128, IDIM), BF16) for _ in range(2)]
        ysq = sb((128, IDIM), BF16)
        ynst = ysq
        stdrow = sb((1, CK)); rstdrow = sb((1, CK))
        pmix = ps((128, 512))
        pcd = pmix[:, 0:128]
        pcb = pmix[:, 128:256]
        pfin = ps((128, 512))
        prms = pfin[0:1, 0:128]
        prstdB = pfin[:, 128:256]
        pd = [ps((128, 512)) for _ in range(2)]
        pr = [ps((128, 512)) for _ in range(2)]
        pY = [ps((128, 512)) for _ in range(2)]

        ld = sem("ld"); gio = sem("gio")
        vv = sem("vv"); sv = sem("sv"); tv = sem("tv"); mmv = sem("mmv")
        dyn = sem("dyn"); fin = sem("fin")
        block = st.enter_context(nc.Block())

        n = dict(ld=0, vv=0, sv=0, tv=0, mm=0, dyn=0, fin=0)

        def vstep(e, instr):
            n["vv"] += 1
        def g_iota(e):
            e.iota(iota_p[:], pattern=[[0, 1]], base=0, channel_multiplier=1)
            e.iota(iota_f[:], pattern=[[1, 128]], base=0, channel_multiplier=0).then_inc(gio, 1)
        prog.on("gpsimd", g_iota)

        def s_loads(e):
            for t, s in [(dtv[:, :], i_dt), (alog[:, :], i_alog),
                         (rm16[:, :], i_rm), (im16[:, :], i_im), (drep[:, :], i_drep),
                         (nw[:, :], i_nw), (mnegf[:, :], i_mneg),
                         (b_sb[:, :], i_bc[0:128, :]), (c_sb[:, :], i_bc[128:256, :])]:
                e.dma_start(out=t, in_=s).then_inc(ld, 16)
            for j in range(NCORES):
                e.dma_start(out=dall[:, 2 * j:2 * j + 2],
                            in_=i_dall[j * NH:(j + 1) * NH, 0:2]).then_inc(ld, 16)
            e.dma_start(out=cv[:, :], in_=i_cv.to_broadcast((128, 8))).then_inc(ld, 16)
        prog.on("sync", s_loads)
        n["ld"] = 18 * 16
        ld0 = n["ld"]

        n["vv"] += 1
        def v_init(e):
            e.wait_ge(gio, 1)
            e.wait_ge(ld, ld0)
            e.tensor_copy(out=fidp[:], in_=iota_p[:])
            e.tensor_copy(out=fidf[:], in_=iota_f[:])
            e.tensor_scalar(out=identb[:], in0=fidf[:], scalar1=fidp[:], scalar2=None, op0=ALU.is_equal)
            e.tensor_scalar(out=identf[:], in0=fidf[:], scalar1=fidp[:], scalar2=None, op0=ALU.is_equal)
            e.memset(ones3[:, :], 1.0)
            e.memset(onesf[:, :], 1.0)
            e.memset(onescol[:, :], 1.0)
            for q2 in range(2):
                e.memset(stat6[q2][:, :], 1.0)
                e.memset(mov6[q2][:, :], 1.0)
            e.tensor_copy(out=mnegb[:, :], in_=mnegf[:, :]).then_inc(vv, 1)
        prog.on("vector", v_init)
        init_v = n["vv"]

        n["sv"] += 1
        def s_A(e):
            e.wait_ge(ld, ld0)
            e.activation(out=alog[:, :], in_=alog[:, :], func=AF.Exp, scale=1.0).then_inc(sv, 1)
        prog.on("scalar", s_A)

        bse = n["vv"]
        n["vv"] += 2
        def v_acum(e, bse=bse):
            e.wait_ge(sv, 1)
            e.tensor_scalar(out=rsplit[:, :], in0=dtv[:, :], scalar1=alog[:, :], scalar2=-1.0,
                            op0=ALU.mult, op1=ALU.mult).then_inc(vv, 1)
            e.wait_ge(vv, bse + 1)
            for u in range(16):
                c0 = u * CK
                e.tensor_tensor_scan(out=acum[:, c0:c0 + CK], data0=rsplit[:, c0:c0 + CK],
                                     data1=rsplit[:, c0:c0 + CK], initial=0.0,
                                     op0=ALU.add, op1=ALU.bypass)
            e.tensor_copy(out=rsplit[:, :], in_=acum[:, :]).then_inc(vv, 1)
        prog.on("vector", v_acum)
        for q in range(3):
            bq = n["vv"]
            n["vv"] += 3
            need_ld = n["ld"]
            def v_split(e, bq=bq, need_ld=need_ld):
                e.wait_ge(vv, bq)
                e.wait_ge(ld, need_ld)
                e.tensor_copy(out=splA[:, :], in_=rsplit[:, :]).then_inc(vv, 1)
                e.wait_ge(vv, bq + 1)
                e.tensor_scalar(out=splB[:, :], in0=splA[:, :], scalar1=-1.0, scalar2=None,
                                op0=ALU.mult)
                e.scalar_tensor_tensor(out=rsplit[:, :], in0=splA[:, :], scalar=-1.0,
                                       in1=rsplit[:, :], op0=ALU.mult,
                                       op1=ALU.add).then_inc(vv, 1)
                e.wait_ge(vv, bq + 2)
                e.tensor_copy(out=splB[0:1, 0:1], in_=splB[0:1, 0:1]).then_inc(vv, 1)
            prog.on("vector", v_split)
            spq = n["vv"]
            n["ld"] += 32
            def f_spdma(e, q=q, w=spq):
                e.wait_ge(vv, w)
                e.dma_start(out=acsp[q, :, :], in_=splA[:, :]).then_inc(ld, 16)
                e.dma_start(out=acsp[3 + q, :, :], in_=splB[:, :]).then_inc(ld, 16)
            prog.on("sync", f_spdma)
        spl_v = n["vv"]
        acsp_ld = n["ld"]

        n["sv"] += 1
        def s_lndt(e, w=spl_v):
            e.wait_ge(vv, w)
            e.activation(out=lndt[:, :], in_=dtv[:, :], func=AF.Ln, scale=1.0).then_inc(sv, 1)
        prog.on("scalar", s_lndt)
        lndt_v = n["sv"]

        bse3 = n["vv"]
        n["vv"] += 3 + 7 * 2 + 1
        def v_relay(e, bse3=bse3):
            e.wait_ge(ld, ld0)
            e.wait_ge(vv, init_v)
            k = bse3
            e.tensor_tensor(out=dprm[:, :], in0=dall[:, :], in1=rm16[:, :], op=ALU.mult).then_inc(vv, 1); k += 1
            e.wait_ge(vv, k)
            e.tensor_tensor(out=dprm[:, :], in0=dprm[:, :], in1=im16[:, :], op=ALU.add).then_inc(vv, 1); k += 1
            e.wait_ge(vv, k)
            e.memset(qsuf[:, 14:16], 1.0).then_inc(vv, 1); k += 1
            for j in range(6, -1, -1):
                e.wait_ge(vv, k)
                e.tensor_tensor(out=qsuf[:, 2 * j:2 * j + 2], in0=qsuf[:, 2 * j + 2:2 * j + 4],
                                in1=dprm[:, 2 * j + 2:2 * j + 4], op=ALU.mult).then_inc(vv, 1); k += 1
                e.wait_ge(vv, k)
                e.tensor_copy(out=qsuf[0:1, 0:1], in_=qsuf[0:1, 0:1]).then_inc(vv, 1); k += 1
            e.wait_ge(vv, k)
            e.tensor_tensor(out=qm[:, :], in0=qsuf[:, :], in1=rm16[:, :], op=ALU.mult).then_inc(vv, 1)
        prog.on("vector", v_relay)
        qm_v = n["vv"]

        sall_done = {}
        pcd_free = [0]
        for b in range(BATCH):
            for j in range(NCORES):
                it = b * NCORES + j
                n["ld"] += 16
                def f_sld(e, b=b, j=j, need_vv=sall_done.get(it - 1, 0)):
                    e.wait_ge(vv, need_vv)
                    e.dma_start(out=sld[:, :],
                                in_=i_sall[j * 256 + b * 128:j * 256 + b * 128 + 128, :]
                                ).then_inc(ld, 16)
                prog.on("sync", f_sld)
                sldv = n["ld"]
                n["tv"] += 1
                def f_qtr(e, b=b, j=j, w=qm_v, wfree=pcd_free[0],
                          wterm=sall_done.get(it - 1, 0)):
                    e.wait_ge(vv, max(w, wterm))
                    e.wait_ge(sv, wfree)
                    e.transpose(pcd[0:1, 0:NH], qm[:, 2 * j + b:2 * j + b + 1],
                                identf[0:NH, 0:NH]).then_inc(tv, 1)
                prog.on("tensor", f_qtr)
                qtr_v = n["tv"]
                n["sv"] += 1
                def f_qrow(e, w=qtr_v):
                    e.wait_ge(tv, w)
                    e.copy(out=qrow[:, :], in_=pcd[0:1, 0:NH]).then_inc(sv, 1)
                prog.on("scalar", f_qrow)
                qrow_v = n["sv"]
                pcd_free[0] = qrow_v
                n["tv"] += 1
                def f_qmm(e, w=qrow_v):
                    e.wait_ge(sv, w)
                    e.matmul(pcd[0:128, 0:NH], onesf[:, :], qrow[:, :], start=True,
                             stop=True).then_inc(tv, 1)
                prog.on("tensor", f_qmm)
                qmm_v = n["tv"]
                n["vv"] += 2
                def f_term(e, b=b, j=j, w1=qmm_v, w2=sldv, k0=n["vv"] - 1):
                    e.wait_ge(tv, w1)
                    e.wait_ge(ld, w2)
                    e.tensor_tensor(out=tmp_sb[:, :].rearrange("x (h p) -> x h p", h=NH),
                                    in0=sld[:, :].rearrange("x (h p) -> x h p", h=NH),
                                    in1=pcd[0:128, 0:NH].unsqueeze(2).to_broadcast((128, NH, HPD)),
                                    op=ALU.mult).then_inc(vv, 1)
                    e.wait_ge(vv, k0)
                    if j == 0:
                        e.tensor_copy(out=SC[b][:, :], in_=tmp_sb[:, :]).then_inc(vv, 1)
                    else:
                        e.tensor_tensor(out=SC[b][:, :], in0=SC[b][:, :], in1=tmp_sb[:, :],
                                        op=ALU.add).then_inc(vv, 1)
                prog.on("vector", f_term)
                sall_done[it] = n["vv"]
                pcd_free[0] = qrow_v
        state_v = n["vv"]

        scan_done = {}
        ychv_done = {}
        yg_done = {}
        ysq_done = {}
        ymm_done = {}
        yn_dma = {}
        cbt_hist = {}
        fin_done = {}
        exps_hist = {}
        mtt_hist = {}
        ymm_hist = {}
        dmm_after_hb = {}
        pend = [None]

        def reg_finalize(uctx):
            u, b, c0, lds_sg = uctx

            def r_yg():
                n["vv"] += 1
                def f_yg(e, w=lds_sg, up=u % 2):
                    e.wait_ge(ld, w)
                    e.tensor_tensor(out=ych[up][:, :], in0=ych[up][:, :], in1=sgch[:, :],
                                    op=ALU.mult).then_inc(vv, 1)
                prog.on("vector", f_yg)
                yg_done[u] = n["vv"]

            def r_ysq():
                n["sv"] += 1
                def f_ysq(e, w=yg_done[u], need_dyn=yn_dma.get(u - 1, 0), up=u % 2):
                    e.wait_ge(vv, w)
                    e.wait_ge(dyn, need_dyn)
                    e.activation(out=ysq[:, :], in_=ych[up][:, :], func=AF.Square).then_inc(sv, 1)
                prog.on("scalar", f_ysq)
                ysq_done[u] = n["sv"]

            def r_rmsmm():
                n["mm"] += 1
                def f_rmsmm(e, w=ysq_done[u]):
                    e.wait_ge(sv, w)
                    last = None
                    for pb in range(32):
                        last = e.matmul(prms[0:1, 0:CK], onescol[:, :],
                                        ysq[:, pb * 128:(pb + 1) * 128],
                                        start=(pb == 0), stop=(pb == 31))
                    last.then_inc(mmv, 1)
                prog.on("tensor", f_rmsmm)
                uctx_state["rmsmm_v"] = n["mm"]

            def r_std():
                n["sv"] += 1
                def f_std(e, w=uctx_state["rmsmm_v"]):
                    e.wait_ge(mmv, w)
                    e.activation(out=stdrow[:, :], in_=prms[0:1, 0:CK], func=AF.Sqrt,
                                 scale=1.0 / IDIM, bias=cv[0:1, 0:1]).then_inc(sv, 1)
                prog.on("scalar", f_std)
                uctx_state["std_v"] = n["sv"]

            def r_rstd():
                n["vv"] += 1
                def f_rstd(e, w=uctx_state["std_v"]):
                    e.wait_ge(sv, w)
                    e.reciprocal(out=rstdrow[:, :], in_=stdrow[:, :]).then_inc(vv, 1)
                prog.on("vector", f_rstd)
                uctx_state["rstd_v"] = n["vv"]

            def r_rbmm():
                n["mm"] += 1
                def f_rbmm(e, w=uctx_state["rstd_v"]):
                    e.wait_ge(vv, w)
                    e.matmul(prstdB[0:128, 0:CK], onesf[:, :], rstdrow[:, :], start=True,
                             stop=True).then_inc(mmv, 1)
                prog.on("tensor", f_rbmm)
                uctx_state["rb_v"] = n["mm"]

            def r_yn():
                n["vv"] += 32
                def f_yn(e, w=uctx_state["rb_v"], need_dyn=yn_dma.get(u - 1, 0), up=u % 2):
                    e.wait_ge(mmv, w)
                    e.wait_ge(dyn, need_dyn)
                    for pb in range(32):
                        e.scalar_tensor_tensor(out=ynst[:, pb * 128:(pb + 1) * 128],
                                               in0=ych[up][:, pb * 128:(pb + 1) * 128],
                                               scalar=nw[:, pb:pb + 1],
                                               in1=prstdB[0:128, 0:CK],
                                               op0=ALU.mult, op1=ALU.mult).then_inc(vv, 1)
                prog.on("vector", f_yn)
                uctx_state["yn_v"] = n["vv"]

            def r_yndma():
                n["dyn"] += 16
                def f_yndma(e, w=uctx_state["yn_v"], c0=c0):
                    e.wait_ge(vv, w)
                    e.dma_start(out=o_yn.rearrange("(pb p) t -> p pb t", p=128)[:, :, c0:c0 + CK],
                                in_=ynst[:, :].rearrange("p (pb s) -> p pb s", pb=32)).then_inc(dyn, 16)
                prog.on("sync", f_yndma)
                yn_dma[u] = n["dyn"]

            uctx_state = {}
            return [r_yg, r_ysq, r_rmsmm, r_std, r_rstd, r_rbmm, r_yn, r_yndma]

        for u in range(16):
            b, j = u // NCKB, u % NCKB
            c0 = b * SPC + j * CK
            up = u % 2

            prev_fin = fin_done.get(u - 1, state_v)
            lds = {}
            n["ld"] += 16
            def f_clxp(e, u=u, up=up, w=prev_fin):
                e.wait_ge(vv, w)
                e.dma_start(out=xpch[up][:, :], in_=i_xp[u, :, :]).then_inc(ld, 16)
            prog.on("sync", f_clxp)
            lds["xp"] = n["ld"]
            n["ld"] += 16
            def f_clsj(e, u=u, w=prev_fin):
                e.wait_ge(vv, w)
                e.dma_start(out=sjch[:, :], in_=i_sj[u, :, :]).then_inc(ld, 16)
            prog.on("sync", f_clsj)
            lds["sj"] = n["ld"]
            n["ld"] += 16
            def f_clxs(e, c0=c0, w=prev_fin):
                e.wait_ge(vv, w)
                e.dma_start(out=xsch[:, :].rearrange("p (pb s) -> p pb s", pb=32),
                            in_=i_xs.rearrange("(pb p) t -> p pb t", p=128)[:, :, c0:c0 + CK]
                            ).then_inc(ld, 16)
            prog.on("sync", f_clxs)
            lds["xs"] = n["ld"]
            n["ld"] += 16
            def f_clsg(e, c0=c0, w=prev_fin):
                e.wait_ge(vv, w)
                e.dma_start(out=sgch[:, :].rearrange("p (pb s) -> p pb s", pb=32),
                            in_=i_sg.rearrange("(pb p) t -> p pb t", p=128)[:, :, c0:c0 + CK]
                            ).then_inc(ld, 16)
            prog.on("sync", f_clsg)
            lds["sg"] = n["ld"]

            n["tv"] += 1
            def f_cdtr(e, c0=c0, w=spl_v, wfree=pcd_free[0]):
                e.wait_ge(vv, w)
                e.wait_ge(sv, wfree)
                e.transpose(pcd[0:1, 0:NH], acum[:, c0 + CK - 1:c0 + CK],
                            identf[0:NH, 0:NH]).then_inc(tv, 1)
            prog.on("tensor", f_cdtr)
            cdtr_v = n["tv"]
            n["sv"] += 1
            def f_cdrow(e, w=cdtr_v):
                e.wait_ge(tv, w)
                e.copy(out=cdrow[:, :], in_=pcd[0:1, 0:NH]).then_inc(sv, 1)
            prog.on("scalar", f_cdrow)
            cdrow_v = n["sv"]
            n["tv"] += 1
            def f_cdmm(e, w=cdrow_v):
                e.wait_ge(sv, w)
                e.matmul(pcd[0:128, 0:NH], onesf[:, :], cdrow[:, :], start=True,
                         stop=True).then_inc(tv, 1)
            prog.on("tensor", f_cdmm)
            cdmm_v = n["tv"]
            n["sv"] += 1
            def f_cdB(e, w=cdmm_v, wsc=scan_done.get(u - 1, 0)):
                e.wait_ge(tv, w)
                e.wait_ge(vv, wsc)
                e.activation(out=cdB[:, :], in_=pcd[0:128, 0:NH], func=AF.Exp).then_inc(sv, 1)
            prog.on("scalar", f_cdB)
            cdB_v = n["sv"]
            pcd_free[0] = cdB_v

            n["tv"] += 1
            def f_ldtr(e, c0=c0, w=lndt_v, wfree=pcd_free[0]):
                e.wait_ge(sv, max(w, wfree))
                e.transpose(pcd[0:128, 0:NH], lndt[:, c0:c0 + CK], identf[0:NH, 0:NH]).then_inc(tv, 1)
            prog.on("tensor", f_ldtr)
            ldtr_v = n["tv"]
            n["sv"] += 1
            def f_ldT(e, w=ldtr_v):
                e.wait_ge(tv, w)
                e.copy(out=lndtT[:, :], in_=pcd[0:128, 0:NH]).then_inc(sv, 1)
            prog.on("scalar", f_ldT)
            lndtT_v = n["sv"]
            pcd_free[0] = lndtT_v

            bse4 = n["vv"]
            n["vv"] += 3
            def f_scan(e, b=b, up=up, bse4=bse4, wcd=cdB_v, wsj=lds["sj"],
                       wpb=ymm_done.get(u - 2, 0)):
                e.wait_ge(mmv, wpb)
                e.wait_ge(sv, wcd)
                e.wait_ge(ld, wsj)
                e.tensor_copy(out=prevb[up][:, :], in_=SC[b][:, :]).then_inc(vv, 1)
                e.wait_ge(vv, bse4 + 1)
                e.tensor_tensor(out=tmp_sb[:, :].rearrange("x (h p) -> x h p", h=NH),
                                in0=SC[b][:, :].rearrange("x (h p) -> x h p", h=NH),
                                in1=cdB[:, :].unsqueeze(2).to_broadcast((128, NH, HPD)),
                                op=ALU.mult).then_inc(vv, 1)
                e.wait_ge(vv, bse4 + 2)
                e.tensor_tensor(out=SC[b][:, :], in0=tmp_sb[:, :], in1=sjch[:, :],
                                op=ALU.add).then_inc(vv, 1)
            prog.on("vector", f_scan)
            scan_v = n["vv"]
            scan_done[u] = scan_v

            n["tv"] += 1
            def f_cbmm(e, c0=c0, w=init_v, wcons=cbt_hist.get(u - 1, 0)):
                e.wait_ge(vv, max(w, wcons))
                e.matmul(pcb[0:128, 0:CK], b_sb[:, c0:c0 + CK], c_sb[:, c0:c0 + CK],
                         start=True, stop=True).then_inc(tv, 1)
            prog.on("tensor", f_cbmm)
            cb_v = n["tv"]
            n["vv"] += 1
            def f_cbev(e, w=cb_v):
                e.wait_ge(tv, w)
                e.tensor_copy(out=cbt[:, :], in_=pcb[0:128, 0:CK]).then_inc(vv, 1)
            prog.on("vector", f_cbev)
            cbt_v = n["vv"]
            cbt_hist[u] = cbt_v

            sub_state = {}
            for s in range(17):
                if s < 16:
                    hb = s // 4
                    gglob = u * 4 + hb
                    if s % 4 == 0:
                        n["ld"] += 32
                        def f_smld(e, c0=c0, hb=hb, sp2=gglob % 2,
                                   need_mm=n["mm"],
                                   wsp=acsp_ld, wi=init_v):
                            e.wait_ge(ld, wsp)
                            e.wait_ge(vv, wi)
                            e.wait_ge(mmv, need_mm)
                            e.dma_start(out=stat6[sp2][3:6, :].rearrange("q (h s) -> q h s", h=16),
                                        in_=acsp.ap()[3:6, hb * 16:(hb + 1) * 16,
                                                      c0:c0 + CK]).then_inc(ld, 16)
                            e.dma_start(out=mov6[sp2][0:3, :].rearrange("q (h s) -> q h s", h=16),
                                        in_=acsp.ap()[0:3, hb * 16:(hb + 1) * 16, c0:c0 + CK]
                                        ).then_inc(ld, 16)
                        prog.on("sync", f_smld)
                        sub_state["smld_%d" % hb] = n["ld"]
                    sub = s % 4
                    sglob = u * 16 + s
                    sp = sglob % 2
                    n["mm"] += 1
                    def f_dmm(e, sub=sub, sp=sp, sp2=gglob % 2,
                              w=sub_state["smld_%d" % hb],
                              wE=exps_hist.get(sglob - 2, 0), winit=init_v):
                        e.wait_ge(ld, w)
                        e.wait_ge(vv, winit)
                        e.wait_ge(sv, wE)
                        last = None
                        for hh in range(4):
                            hcol = sub * 4 + hh
                            last = e.matmul(pd[sp][:, hh * 128:(hh + 1) * 128],
                                            stat6[sp2][:, hcol * 128:(hcol + 1) * 128],
                                            mov6[sp2][:, hcol * 128:(hcol + 1) * 128],
                                            start=True, stop=False)
                            last = e.matmul(pd[sp][:, hh * 128:(hh + 1) * 128], mnegb[:, :],
                                            identb[:, :], start=False, stop=True)
                            last = e.matmul(pr[sp][:, hh * 128:(hh + 1) * 128], ones3[:, :],
                                            mov6[sp2][0:3, hcol * 128:(hcol + 1) * 128],
                                            start=True, stop=True)
                        last.then_inc(mmv, 1)
                    prog.on("tensor", f_dmm)
                    sub_state["dmm_%d" % s] = n["mm"]
                    if s % 4 == 3:
                        dmm_after_hb[gglob] = n["mm"]

                if s >= 1:
                    so = s - 1
                    hbo = so // 4
                    subo = so % 4
                    sglo = u * 16 + so
                    spo = sglo % 2
                    sp2o = (u * 4 + hbo) % 2
                    n["sv"] += 1
                    def f_exps(e, hbo=hbo, subo=subo, spo=spo, spo2=sp2o,
                               w=sub_state["dmm_%d" % so], wl=lndtT_v,
                               wcons=mtt_hist.get(sglo - 2, 0)):
                        e.wait_ge(mmv, w)
                        e.wait_ge(sv, wl)
                        e.wait_ge(vv, wcons)
                        for hh in range(4):
                            h = hbo * 16 + subo * 4 + hh
                            e.activation(out=etop[spo][:, hh * 128:(hh + 1) * 128],
                                         in_=pd[spo][:, hh * 128:(hh + 1) * 128], func=AF.Exp,
                                         scale=1.0, bias=lndtT[:, h:h + 1])
                        e.activation(out=ebot[spo][:, :], in_=pr[spo][:, :],
                                     func=AF.Exp).then_inc(sv, 1)
                    prog.on("scalar", f_exps)
                    exps_hist[sglo] = n["sv"]

                    bse5 = n["vv"]
                    n["vv"] += 2
                    def f_mtt(e, spo=spo, w=exps_hist[sglo], bse5=bse5, wcb=cbt_v,
                              wY=ymm_hist.get(sglo - 2, 0), c0=c0):
                        e.wait_ge(sv, w)
                        e.wait_ge(vv, wcb)
                        e.wait_ge(mmv, wY)
                        e.tensor_tensor(out=mtop[spo][:, :].rearrange("s (q t) -> s q t", q=4),
                                        in0=etop[spo][:, :].rearrange("s (q t) -> s q t", q=4),
                                        in1=cbt[:, :].unsqueeze(1).to_broadcast((128, 4, CK)),
                                        op=ALU.mult).then_inc(vv, 1)
                        e.wait_ge(vv, bse5 + 1)
                        e.tensor_tensor(out=mbot[spo][:, :].rearrange("n (q t) -> n q t", q=4),
                                        in0=ebot[spo][:, :].rearrange("n (q t) -> n q t", q=4),
                                        in1=c_sb[:, c0:c0 + CK].unsqueeze(1).to_broadcast((128, 4, CK)),
                                        op=ALU.mult).then_inc(vv, 1)
                    prog.on("vector", f_mtt)
                    mtt_hist[sglo] = n["vv"]

                    g4 = hbo * 4 + subo
                    pz = pY[(g4 // 2) % 2]
                    n["mm"] += 1
                    def f_ymm(e, hbo=hbo, subo=subo, spo=spo, up=up, pz=pz,
                              w=mtt_hist[sglo], wx=lds["xp"], wsc=scan_v):
                        e.wait_ge(vv, w)
                        e.wait_ge(ld, wx)
                        e.wait_ge(vv, wsc)
                        last = None
                        for hh in range(4):
                            h = hbo * 16 + subo * 4 + hh
                            ph = h % 2
                            col = ((h % 8) // 2) * 128
                            tp = (0, 64) if ph else None
                            last = e.matmul(pz[ph * 64:ph * 64 + 64, col:col + CK],
                                            xpch[up][:, h * 64:(h + 1) * 64],
                                            mtop[spo][:, hh * 128:(hh + 1) * 128],
                                            start=True, stop=False, tile_position=tp)
                            last = e.matmul(pz[ph * 64:ph * 64 + 64, col:col + CK],
                                            prevb[up][:, h * 64:(h + 1) * 64],
                                            mbot[spo][:, hh * 128:(hh + 1) * 128],
                                            start=False, stop=True, tile_position=tp)
                        last.then_inc(mmv, 1)
                    prog.on("tensor", f_ymm)
                    ymm_hist[sglo] = n["mm"]

                    if subo % 2 == 1:
                        gg = hbo * 2 + subo // 2
                        n["vv"] += 2
                        def f_yev(e, up=up, gg=gg, pz=pz, w=ymm_hist[sglo],
                                  wxs=lds["xs"], wsq=ysq_done.get(u - 2, 0),
                                  k0=n["vv"] - 1):
                            e.wait_ge(mmv, w)
                            e.wait_ge(ld, wxs)
                            e.wait_ge(sv, wsq)
                            e.tensor_tensor(
                                out=tmp_sb[:, 0:512].rearrange("p (q s) -> p q s", q=4),
                                in0=xsch[:, gg * 512:(gg + 1) * 512].rearrange("p (q s) -> p q s", q=4),
                                in1=drep[:, gg * 4:(gg + 1) * 4].unsqueeze(2).to_broadcast((128, 4, CK)),
                                op=ALU.mult).then_inc(vv, 1)
                            e.wait_ge(vv, k0)
                            e.tensor_tensor(out=ych[up][:, gg * 512:(gg + 1) * 512],
                                            in0=tmp_sb[:, 0:512], in1=pz[:, :],
                                            op=ALU.add).then_inc(vv, 1)
                        prog.on("vector", f_yev)
            ychv_done[u] = n["vv"]
            ymm_done[u] = n["mm"]
            for r in reg_finalize((u, b, c0, lds["sg"])):
                r()
            fin_done[u] = n["vv"]

        def f_fin(e, a=n["dyn"]):
            e.wait_ge(dyn, a)
        prog.on("sync", f_fin)
        prog.emit(block)
    return nc


def phase4b(nc, io):
    i_yn = io["d_yn"]; i_wouT = io["w_outT"]; i_g = io["d_g"]
    i_xr = io["xh"]; i_cv = io["cvec"]
    o_x1 = io["d_x1"]; o_xn1T = io["d_xn1T"]

    prog = Prog(); PHX[0] += 1
    with ExitStack() as st:
        _ctr = [0]
        def sb(shape, dt=F32):
            _ctr[0] += 1
            return st.enter_context(nc.sbuf_tensor("sb%d_%d" % (PHX[0], _ctr[0]), list(shape), dt))
        def ps(shape, dt=F32):
            _ctr[0] += 1
            return st.enter_context(nc.psum_tensor("ps%d_%d" % (PHX[0], _ctr[0]), list(shape), dt))
        def sem(name):
            return st.enter_context(nc.semaphore("s%d_%s" % (PHX[0], name)))

        xn1T = [sb((128, TPC), BF16) for _ in range(16)]
        ynb = [sb((128, 32 * 512), BF16) for _ in range(2)]
        wst = [sb((128, 32 * 128), BF16) for _ in range(2)]
        s0blk = sb((128, 512), BF16)
        s0T = sb((128, 4 * 16 * 128), BF16)
        xt = sb((128, DM))
        gt = sb((128, DM), BF16)
        x1f = sb((128, DM))
        sqs = sb((128, DM))
        sxn1 = sb((128, DM), BF16)
        sumc, sqc, meanc, nvc, stdc, rstdc = (sb((128, 1)) for _ in range(6))
        cv = sb((128, 8))
        identb = sb((128, 128), BF16)
        iota_p = sb((128, 1), mybir.dt.int32); iota_f = sb((128, 128), mybir.dt.int32)
        fidp = sb((128, 1)); fidf = sb((128, 128))
        ps0 = [ps((128, 512)) for _ in range(2)]
        ptr = [ps((128, 128), BF16) for _ in range(2)]

        ld = sem("ld"); gio = sem("gio"); wld = sem("wld")
        vv = sem("vv"); sv = sem("sv"); tv = sem("tv"); mmv = sem("mmv")
        dst = sem("dst")
        block = st.enter_context(nc.Block())

        n = dict(ld=0, wld=0, vv=0, sv=0, tv=0, mm=0, dst=0)

        def g_iota(e):
            e.iota(iota_p[:], pattern=[[0, 1]], base=0, channel_multiplier=1)
            e.iota(iota_f[:], pattern=[[1, 128]], base=0, channel_multiplier=0).then_inc(gio, 1)
        prog.on("gpsimd", g_iota)

        def s_cv(e):
            e.dma_start(out=cv[:, :], in_=i_cv.to_broadcast((128, 8))).then_inc(ld, 16)
        prog.on("sync", s_cv)
        n["ld"] = 16

        n["vv"] += 1
        def v_init(e):
            e.wait_ge(gio, 1)
            e.tensor_copy(out=fidp[:], in_=iota_p[:])
            e.tensor_copy(out=fidf[:], in_=iota_f[:])
            e.tensor_scalar(out=identb[:], in0=fidf[:], scalar1=fidp[:], scalar2=None,
                            op0=ALU.is_equal).then_inc(vv, 1)
        prog.on("vector", v_init)
        init_v = n["vv"]

        mm_after_tg = {}
        ev_after = [0]
        for tg in range(4):
            tok0 = tg * 512
            ybuf = tg % 2
            n["ld"] += 16
            def f_ynld(e, tok0=tok0, ybuf=ybuf, need_mm=mm_after_tg.get(tg - 2, 0)):
                e.wait_ge(mmv, need_mm)
                e.dma_start(out=ynb[ybuf][:, :].rearrange("i (ib t) -> i ib t", ib=32),
                            in_=i_yn.rearrange("(ib i) t -> i ib t", i=128)[:, :, tok0:tok0 + 512]
                            ).then_inc(ld, 16)
            prog.on("sync", f_ynld)
            ynldv = n["ld"]

            for db in range(16):
                dbu = tg * 16 + db
                wbuf = dbu % 2
                n["wld"] += 16
                def f_wld(e, db=db, wbuf=wbuf, need_mm=mm_after_tg.get(tg - 1, 0) if db < 2 else 0,
                          nm=max(0, n["mm"] - 1)):
                    e.wait_ge(mmv, nm)
                    e.dma_start(out=wst[wbuf][:, :].rearrange("i (ib d) -> i ib d", ib=32),
                                in_=i_wouT.rearrange("(ib i) d -> i ib d", i=128)[
                                    :, :, db * 128:(db + 1) * 128]).then_inc(wld, 16)
                prog.on("gpsimd", f_wld)
                wldv = n["wld"]

                pz = ps0[dbu % 2]
                n["mm"] += 1
                def f_mm(e, ybuf=ybuf, wbuf=wbuf, pz=pz, w1=ynldv, w2=wldv,
                         need_ev=ev_after[0]):
                    e.wait_ge(ld, w1)
                    e.wait_ge(wld, w2)
                    e.wait_ge(vv, need_ev)
                    last = None
                    for ib in range(32):
                        last = e.matmul(pz[:, :], wst[wbuf][:, ib * 128:(ib + 1) * 128],
                                        ynb[ybuf][:, ib * 512:(ib + 1) * 512],
                                        start=(ib == 0), stop=(ib == 31))
                    last.then_inc(mmv, 1)
                prog.on("tensor", f_mm)
                mmvv = n["mm"]

                n["vv"] += 1
                def f_ev(e, pz=pz, w=mmvv, wtr=n["tv"]):
                    e.wait_ge(mmv, w)
                    e.wait_ge(tv, wtr)
                    e.tensor_copy(out=s0blk[:, :], in_=pz[:, :]).then_inc(vv, 1)
                prog.on("vector", f_ev)
                evv = n["vv"]
                ev_after[0] = evv

                for tsub in range(4):
                    pj = ptr[tsub % 2]
                    n["tv"] += 1
                    def f_tr(e, tsub=tsub, pj=pj, w=evv, wev=max(0, n["vv"] - (1 if tsub >= 2 else 0))):
                        e.wait_ge(vv, wev)
                        e.transpose(pj[0:128, 0:128], s0blk[:, tsub * 128:(tsub + 1) * 128],
                                    identb[:]).then_inc(tv, 1)
                    prog.on("tensor", f_tr)
                    trv = n["tv"]
                    n["vv"] += 1
                    def f_trev(e, tsub=tsub, db=db, pj=pj, w=trv):
                        e.wait_ge(tv, w)
                        e.tensor_copy(out=s0T[:, (tsub * 16 + db) * 128:(tsub * 16 + db + 1) * 128],
                                      in_=pj[0:128, 0:128]).then_inc(vv, 1)
                    prog.on("vector", f_trev)
            s0T_v = n["vv"]
            mm_after_tg[tg] = n["mm"]

            for tsub in range(4):
                t0 = tok0 + tsub * 128
                b, pos0 = t0 // SPC, t0 % SPC
                n["ld"] += 32
                def f_xg(e, b=b, pos0=pos0, t0=t0, w=max(s0T_v, n["vv"])):
                    e.wait_ge(vv, w)
                    e.dma_start(out=xt[:, :],
                                in_=i_xr[b, HALO + pos0:HALO + pos0 + 128, :]).then_inc(ld, 16)
                    e.dma_start(out=gt[:, :], in_=i_g[t0:t0 + 128, :]).then_inc(ld, 16)
                prog.on("sync", f_xg)
                xgv = n["ld"]

                bse = n["vv"]
                n["vv"] += 3
                def f_res(e, tsub=tsub, bse=bse, w=xgv, wd=n["dst"]):
                    e.wait_ge(ld, w)
                    e.wait_ge(dst, wd)
                    e.tensor_tensor(out=x1f[:, :].rearrange("t (db d) -> t db d", db=16),
                                    in0=gt[:, :].rearrange("t (db d) -> t db d", db=16),
                                    in1=s0T[:, :].rearrange("t (ts db d) -> t ts db d",
                                                            ts=4, db=16)[:, tsub, :, :],
                                    op=ALU.mult).then_inc(vv, 1)
                    e.wait_ge(vv, bse + 1)
                    e.tensor_tensor(out=x1f[:, :], in0=x1f[:, :], in1=xt[:, :],
                                    op=ALU.add).then_inc(vv, 1)
                    e.wait_ge(vv, bse + 2)
                    e.reduce_sum(out=sumc[:, :], in_=x1f[:, :], axis=AX.X).then_inc(vv, 1)
                prog.on("vector", f_res)
                res_v = n["vv"]

                n["dst"] += 16
                def f_x1dma(e, b=b, pos0=pos0, w=res_v):
                    e.wait_ge(vv, w)
                    e.dma_start(out=o_x1[b, pos0:pos0 + 128, :], in_=x1f[:, :]).then_inc(dst, 16)
                prog.on("sync", f_x1dma)
                x1d_v = n["dst"]

                n["sv"] += 1
                def f_sq(e, w=res_v):
                    e.wait_ge(vv, w)
                    e.activation(out=sqs[:, :], in_=x1f[:, :], func=AF.Square,
                                 accum_out=sqc[:, :]).then_inc(sv, 1)
                prog.on("scalar", f_sq)
                sqv = n["sv"]

                bse2 = n["vv"]
                n["vv"] += 2
                def f_stats(e, w=sqv, bse2=bse2):
                    e.wait_ge(sv, w)
                    e.tensor_scalar(out=meanc[:, :], in0=sumc[:, :], scalar1=1.0 / DM,
                                    scalar2=None, op0=ALU.mult)
                    e.tensor_scalar(out=nvc[:, :], in0=sqc[:, :], scalar1=1.0 / DM,
                                    scalar2=None, op0=ALU.mult).then_inc(vv, 1)
                    e.wait_ge(vv, bse2 + 1)
                    e.scalar_tensor_tensor(out=nvc[:, :], in0=meanc[:, :], scalar=meanc[:, :],
                                           in1=nvc[:, :], op0=ALU.mult,
                                           op1=ALU.subtract).then_inc(vv, 1)
                prog.on("vector", f_stats)
                st_v = n["vv"]

                n["sv"] += 1
                def f_std(e, w=st_v):
                    e.wait_ge(vv, w)
                    e.activation(out=stdc[:, :], in_=nvc[:, :], func=AF.Sqrt, scale=-1.0,
                                 bias=cv[:, 0:1]).then_inc(sv, 1)
                prog.on("scalar", f_std)
                stdv = n["sv"]

                bse3 = n["vv"]
                n["vv"] += 2
                def f_xn1(e, w=stdv, wd=x1d_v, bse3=bse3):
                    e.wait_ge(sv, w)
                    e.wait_ge(dst, wd)
                    e.reciprocal(out=rstdc[:, :], in_=stdc[:, :]).then_inc(vv, 1)
                    e.wait_ge(vv, bse3 + 1)
                    e.tensor_scalar(out=sxn1[:, :], in0=x1f[:, :], scalar1=meanc[:, :],
                                    scalar2=rstdc[:, :], op0=ALU.subtract,
                                    op1=ALU.mult).then_inc(vv, 1)
                prog.on("vector", f_xn1)
                xn1_v = n["vv"]

                for kg in range(8):
                    pj = ptr[kg % 2]
                    n["tv"] += 2
                    def f_tr2(e, kg=kg, pj=pj, w=xn1_v, wev=n["vv"]):
                        e.wait_ge(vv, wev)
                        last = None
                        for q in range(2):
                            last = e.transpose(ptr[q][0:128, 0:128],
                                               sxn1[:, (kg * 2 + q) * 128:(kg * 2 + q + 1) * 128],
                                               identb[:])
                        last.then_inc(tv, 2)
                    prog.on("tensor", f_tr2)
                    trv2 = n["tv"]
                    n["vv"] += 1
                    def f_tre2(e, kg=kg, t0=t0, w=trv2):
                        e.wait_ge(tv, w)
                        last = None
                        for q in range(2):
                            last = e.tensor_copy(out=xn1T[kg * 2 + q][:, t0:t0 + 128],
                                                 in_=ptr[q][0:128, 0:128])
                        last.then_inc(vv, 1)
                    prog.on("vector", f_tre2)

        all_v = n["vv"]
        for k in range(16):
            n["dst"] += 16
            def f_spill(e, k=k, w=all_v):
                e.wait_ge(vv, w)
                e.dma_start(out=o_xn1T[k * 128:(k + 1) * 128, :], in_=xn1T[k][:, :]).then_inc(dst, 16)
            prog.on("sync", f_spill)

        def f_fin(e, a=n["dst"]):
            e.wait_ge(dst, a)
        prog.on("sync", f_fin)
        prog.emit(block)
    return nc


def phase5a(nc, io):
    i_xn1T = io["d_xn1T"]; i_wqkvT = io["wqkvT"]; i_fw = io["ffnW"]
    i_sc = io["fscale"]; i_cv = io["cvec"]
    o_qkv = io["d_qkv"]; o_g1 = io["d_g1"]; ofd = io["ofd_h"]

    prog = Prog(); PHX[0] += 1
    with ExitStack() as st:
        _ctr = [0]
        def sb(shape, dt=F32):
            _ctr[0] += 1
            return st.enter_context(nc.sbuf_tensor("sb%d_%d" % (PHX[0], _ctr[0]), list(shape), dt))
        def ps(shape, dt=F32):
            _ctr[0] += 1
            return st.enter_context(nc.psum_tensor("ps%d_%d" % (PHX[0], _ctr[0]), list(shape), dt))
        def sem(name):
            return st.enter_context(nc.semaphore("s%d_%s" % (PHX[0], name)))

        xn1T = [sb((128, TPC), BF16) for _ in range(16)]
        wqb = [sb((128, 16 * 128), BF16) for _ in range(2)]
        qst = sb((128, TPC), BF16)
        ffnb = [sb((128, 16 * 512), BF16) for _ in range(2)]
        scaleB = sb((128, DM))
        srow = sb((1, DM))
        ofst = sb((128, 512), BF16)
        ofrow = sb((128, DM), BF16)
        sqp = sb((128, 64))
        ss = sb((128, 1)); vn = sb((128, 1)); th = sb((128, 1)); dn = sb((128, 1))
        rc = sb((128, 1)); f1 = sb((128, 1)); onm = sb((128, 1)); rc2 = sb((128, 1))
        ff = sb((128, 1))
        g1pre = sb((128, DM))
        g1st = sb((128, DM), BF16)
        cv = sb((128, 8))
        onesf = sb((1, 128))
        psq = [ps((128, 512)) for _ in range(2)]
        psf = [ps((128, 512)) for _ in range(2)]
        psb = ps((128, 512))

        ld = sem("ld"); wld = sem("wld")
        vv = sem("vv"); sv = sem("sv"); mmv = sem("mmv"); dst = sem("dst")
        block = st.enter_context(nc.Block())

        n = dict(ld=0, wld=0, vv=0, sv=0, mm=0, dst=0)

        def s_loads(e):
            for k in range(16):
                e.dma_start(out=xn1T[k][:, :], in_=i_xn1T[k * 128:(k + 1) * 128, :]).then_inc(ld, 16)
            e.dma_start(out=srow[:, :], in_=i_sc).then_inc(ld, 16)
            e.dma_start(out=cv[:, :], in_=i_cv.to_broadcast((128, 8))).then_inc(ld, 16)
        prog.on("sync", s_loads)
        n["ld"] = 18 * 16
        ld0 = n["ld"]

        n["vv"] += 1
        def v_init(e):
            e.memset(onesf[:, :], 1.0).then_inc(vv, 1)
        prog.on("vector", v_init)
        init_v = n["vv"]

        ev_hist = {}
        for ob in range(24):
            wbuf = ob % 2
            n["wld"] += 16
            def f_wq(e, ob=ob, wbuf=wbuf, need=max(0, n["mm"] - 1)):
                e.wait_ge(mmv, need)
                e.dma_start(out=wqb[wbuf][:, :].rearrange("d (k o) -> d k o", k=16),
                            in_=i_wqkvT.rearrange("(k d) o -> d k o", d=128)[
                                :, :, ob * 128:(ob + 1) * 128]).then_inc(wld, 16)
            prog.on("gpsimd", f_wq)
            wqv = n["wld"]
            for tg in range(4):
                pz = psq[(ob * 4 + tg) % 2]
                n["mm"] += 1
                def f_qmm(e, wbuf=wbuf, tg=tg, pz=pz, w=wqv,
                          need_ev=ev_hist.get(ob * 4 + tg - 2, 0)):
                    e.wait_ge(wld, w)
                    e.wait_ge(ld, ld0)
                    e.wait_ge(vv, need_ev)
                    last = None
                    for k in range(16):
                        last = e.matmul(pz[:, :], wqb[wbuf][:, k * 128:(k + 1) * 128],
                                        xn1T[k][:, tg * 512:(tg + 1) * 512],
                                        start=(k == 0), stop=(k == 15))
                    last.then_inc(mmv, 1)
                prog.on("tensor", f_qmm)
                qmmv = n["mm"]
                n["vv"] += 1
                def f_qev(e, tg=tg, pz=pz, w=qmmv, wd=n["dst"]):
                    e.wait_ge(mmv, w)
                    e.wait_ge(dst, wd)
                    e.tensor_copy(out=qst[:, tg * 512:(tg + 1) * 512], in_=pz[:, :]).then_inc(vv, 1)
                prog.on("vector", f_qev)
                ev_hist[ob * 4 + tg] = n["vv"]
            n["dst"] += 16
            def f_qdma(e, ob=ob, w=n["vv"]):
                e.wait_ge(vv, w)
                e.dma_start(out=o_qkv[ob * 128:(ob + 1) * 128, :], in_=qst[:, :]).then_inc(dst, 16)
            prog.on("sync", f_qdma)
        qkv_done_mm = n["mm"]

        n["mm"] += 1
        def f_scmm(e, w=init_v):
            e.wait_ge(vv, w)
            e.wait_ge(ld, ld0)
            last = None
            for q in range(4):
                last = e.matmul(psb[:, :] if False else psb[:, 0:512], onesf[:, :],
                                srow[:, q * 512:(q + 1) * 512], start=True, stop=True)
            last.then_inc(mmv, 1)
        prog.on("tensor", None)
        prog.items["tensor"].pop()
        n["mm"] -= 1
        for q in range(4):
            n["mm"] += 1
            def f_scmm(e, q=q, w=init_v, wev=n["vv"]):
                e.wait_ge(vv, max(w, wev))
                e.wait_ge(ld, ld0)
                e.matmul(psb[:, 0:512], onesf[:, :], srow[:, q * 512:(q + 1) * 512],
                         start=True, stop=True).then_inc(mmv, 1)
            prog.on("tensor", f_scmm)
            scv = n["mm"]
            n["vv"] += 1
            def f_scev(e, q=q, w=scv):
                e.wait_ge(mmv, w)
                e.tensor_copy(out=scaleB[:, q * 512:(q + 1) * 512], in_=psb[:, 0:512]).then_inc(vv, 1)
            prog.on("vector", f_scev)
        scB_v = n["vv"]

        of_hist = {}
        for kg in range(4):
            fbuf = kg % 2
            n["wld"] += 16
            def f_fl(e, kg=kg, fbuf=fbuf, need=of_hist.get(kg - 2, qkv_done_mm)):
                e.wait_ge(mmv, need)
                e.dma_start(out=ffnb[fbuf][:, :].rearrange("nd (k c) -> nd k c", k=16),
                            in_=i_fw.rearrange("(k nd) c -> nd k c", nd=128)[
                                :, :, kg * 512:(kg + 1) * 512]).then_inc(wld, 16)
            prog.on("gpsimd", f_fl)
            flv = n["wld"]
            for tt in range(16):
                pz = psf[(kg * 16 + tt) % 2]
                n["mm"] += 1
                def f_fmm(e, fbuf=fbuf, tt=tt, pz=pz, w=flv, need_ev=max(0, n["vv"] - 1)):
                    e.wait_ge(wld, w)
                    e.wait_ge(vv, need_ev)
                    last = None
                    for k in range(16):
                        last = e.matmul(pz[:, :], xn1T[k][:, tt * 128:(tt + 1) * 128],
                                        ffnb[fbuf][:, k * 512:(k + 1) * 512],
                                        start=(k == 0), stop=(k == 15))
                    last.then_inc(mmv, 1)
                prog.on("tensor", f_fmm)
                fmmv = n["mm"]
                bsf = n["vv"]
                n["vv"] += 1
                def f_oev(e, kg=kg, tt=tt, pz=pz, w=fmmv, wsc=scB_v, wd=n["dst"]):
                    e.wait_ge(mmv, w)
                    e.wait_ge(vv, wsc)
                    e.wait_ge(dst, wd)
                    e.tensor_tensor(out=ofst[:, :], in0=pz[:, :],
                                    in1=scaleB[:, kg * 512:(kg + 1) * 512],
                                    op=ALU.mult).then_inc(vv, 1)
                prog.on("vector", f_oev)
                oevv = n["vv"]
                n["sv"] += 1
                def f_osq(e, kg=kg, tt=tt, w=oevv):
                    e.wait_ge(vv, w)
                    e.activation(out=g1pre[:, 0:512], in_=ofst[:, :], func=AF.Square,
                                 accum_out=sqp[:, tt * 4 + kg:tt * 4 + kg + 1]).then_inc(sv, 1)
                prog.on("scalar", f_osq)
                osqv = n["sv"]
                n["dst"] += 16
                def f_odma(e, kg=kg, tt=tt, w=oevv, ws=osqv):
                    e.wait_ge(vv, w)
                    e.wait_ge(sv, ws)
                    e.dma_start(out=ofd[tt * 128:(tt + 1) * 128, kg * 512:(kg + 1) * 512],
                                in_=ofst[:, :]).then_inc(dst, 16)
                prog.on("sync", f_odma)
            of_hist[kg] = n["mm"]
        ffn_dst = n["dst"]
        sq_done = n["sv"]

        g1_hist = {}
        for tt in range(16):
            n["ld"] += 16
            def f_ofl(e, tt=tt, w=ffn_dst, need_v=g1_hist.get(tt - 1, 0)):
                e.wait_ge(dst, w)
                e.wait_ge(vv, need_v)
                e.dma_start(out=ofrow[:, :], in_=ofd[tt * 128:(tt + 1) * 128, :]).then_inc(ld, 16)
            prog.on("sync", f_ofl)
            oflv = n["ld"]

            bsn = n["vv"]
            n["vv"] += 3
            def f_n1(e, tt=tt, bsn=bsn, ws=sq_done):
                e.wait_ge(sv, ws)
                e.tensor_tensor(out=ss[:, :], in0=sqp[:, tt * 4:tt * 4 + 1],
                                in1=sqp[:, tt * 4 + 1:tt * 4 + 2], op=ALU.add).then_inc(vv, 1)
                e.wait_ge(vv, bsn + 1)
                e.tensor_tensor(out=ss[:, :], in0=ss[:, :], in1=sqp[:, tt * 4 + 2:tt * 4 + 3],
                                op=ALU.add).then_inc(vv, 1)
                e.wait_ge(vv, bsn + 2)
                e.tensor_tensor(out=ss[:, :], in0=ss[:, :], in1=sqp[:, tt * 4 + 3:tt * 4 + 4],
                                op=ALU.add).then_inc(vv, 1)
            prog.on("vector", f_n1)
            n1v = n["vv"]

            n["sv"] += 1
            def f_vn(e, w=n1v):
                e.wait_ge(vv, w)
                e.activation(out=vn[:, :], in_=ss[:, :], func=AF.Sqrt, scale=1.0).then_inc(sv, 1)
            prog.on("scalar", f_vn)
            vnv = n["sv"]

            bsn2 = n["vv"]
            n["vv"] += 2
            def f_n2(e, bsn2=bsn2, w=vnv):
                e.wait_ge(sv, w)
                e.tensor_scalar(out=vn[:, :], in0=vn[:, :], scalar1=1e-7, scalar2=None,
                                op0=ALU.max).then_inc(vv, 1)
                e.wait_ge(vv, bsn2 + 1)
                e.tensor_scalar(out=dn[:, :], in0=vn[:, :], scalar1=1e-7, scalar2=None,
                                op0=ALU.add).then_inc(vv, 1)
            prog.on("vector", f_n2)
            n2v = n["vv"]

            n["sv"] += 1
            def f_th(e, w=n2v):
                e.wait_ge(vv, w)
                e.activation(out=th[:, :], in_=vn[:, :], func=AF.Tanh).then_inc(sv, 1)
            prog.on("scalar", f_th)
            thv = n["sv"]

            bsn3 = n["vv"]
            n["vv"] += 6
            def f_n3(e, bsn3=bsn3, w=thv):
                e.wait_ge(sv, w)
                e.reciprocal(out=rc[:, :], in_=dn[:, :]).then_inc(vv, 1)
                e.wait_ge(vv, bsn3 + 1)
                e.tensor_tensor(out=f1[:, :], in0=th[:, :], in1=rc[:, :], op=ALU.mult).then_inc(vv, 1)
                e.wait_ge(vv, bsn3 + 2)
                e.scalar_tensor_tensor(out=onm[:, :], in0=f1[:, :], scalar=1.0, in1=vn[:, :],
                                       op0=ALU.mult, op1=ALU.mult).then_inc(vv, 1)
                e.wait_ge(vv, bsn3 + 3)
                e.tensor_scalar(out=onm[:, :], in0=onm[:, :], scalar1=1e-7, scalar2=None,
                                op0=ALU.max).then_inc(vv, 1)
                e.wait_ge(vv, bsn3 + 4)
                e.reciprocal(out=rc2[:, :], in_=onm[:, :]).then_inc(vv, 1)
                e.wait_ge(vv, bsn3 + 5)
                e.tensor_scalar(out=rc2[:, :], in0=rc2[:, :], scalar1=0.99, scalar2=1.0,
                                op0=ALU.mult, op1=ALU.min).then_inc(vv, 1)
            prog.on("vector", f_n3)
            n3v = n["vv"]

            bsn4 = n["vv"]
            n["vv"] += 2
            def f_g1p(e, tt=tt, bsn4=bsn4, w=n3v, wl=oflv):
                e.wait_ge(vv, w)
                e.wait_ge(ld, wl)
                e.tensor_tensor(out=ff[:, :], in0=f1[:, :], in1=rc2[:, :], op=ALU.mult).then_inc(vv, 1)
                e.wait_ge(vv, bsn4 + 1)
                e.tensor_scalar(out=g1pre[:, :], in0=ofrow[:, :], scalar1=ff[:, :], scalar2=None,
                                op0=ALU.mult).then_inc(vv, 1)
            prog.on("vector", f_g1p)
            g1pv = n["vv"]

            n["sv"] += 1
            def f_g1s(e, w=g1pv, wd=n["dst"]):
                e.wait_ge(vv, w)
                e.wait_ge(dst, wd)
                e.activation(out=g1st[:, :], in_=g1pre[:, :], func=AF.Silu).then_inc(sv, 1)
            prog.on("scalar", f_g1s)
            g1sv = n["sv"]
            n["dst"] += 16
            def f_g1dma(e, tt=tt, w=g1sv):
                e.wait_ge(sv, w)
                e.dma_start(out=o_g1[tt * 128:(tt + 1) * 128, :], in_=g1st[:, :]).then_inc(dst, 16)
            prog.on("sync", f_g1dma)
            g1_hist[tt] = n["vv"]

        def f_fin(e, a=n["dst"]):
            e.wait_ge(dst, a)
        prog.on("sync", f_fin)
        prog.emit(block)
    return nc


def phase5b(nc, io):
    i_qkv = io["d_qkv"]; i_g1 = io["d_g1"]; i_x1 = io["d_x1"]
    i_woT = io["wo1T"]; i_sl = io["slrow"]
    o_out = io["outp"]

    prog = Prog(); PHX[0] += 1
    with ExitStack() as st:
        _ctr = [0]
        def sb(shape, dt=F32):
            _ctr[0] += 1
            return st.enter_context(nc.sbuf_tensor("sb%d_%d" % (PHX[0], _ctr[0]), list(shape), dt))
        def ps(shape, dt=F32):
            _ctr[0] += 1
            return st.enter_context(nc.psum_tensor("ps%d_%d" % (PHX[0], _ctr[0]), list(shape), dt))
        def sem(name):
            return st.enter_context(nc.semaphore("s%d_%s" % (PHX[0], name)))

        q1 = [sb((128, SPC), BF16) for _ in range(16)]
        kv = [sb((128, TPC), BF16) for _ in range(8)]
        o1f = [sb((128, SPC), BF16) for _ in range(16)]
        tmp0 = sb((128, SPC), BF16)
        tmp1 = sb((128, SPC), BF16)
        p10 = sb((1, SPC))
        dvt = sb((128, SPC), BF16)
        slrow = sb((1, 16))
        onesf = sb((1, 128))
        onescol = sb((128, 1), BF16)
        negcol = sb((128, 1), BF16)
        wst = [sb((128, 16 * 128), BF16) for _ in range(2)]
        s0blk = sb((128, 512), BF16)
        s1T = sb((128, 4 * 16 * 128), BF16)
        xt = sb((128, DM))
        gt = sb((128, DM), BF16)
        x2f = sb((128, DM))
        identb = sb((128, 128), BF16)
        iota_p = sb((128, 1), mybir.dt.int32); iota_f = sb((128, 128), mybir.dt.int32)
        fidp = sb((128, 1)); fidf = sb((128, 128))
        pdif = ps((1, 512))
        pb10 = ps((128, 512))
        pso = [ps((128, 512)) for _ in range(2)]
        ptr = [ps((128, 128), BF16) for _ in range(2)]

        ld = sem("ld"); wld = sem("wld"); gio = sem("gio")
        vv = sem("vv"); sv = sem("sv"); tv = sem("tv"); mmv = sem("mmv"); dst = sem("dst")
        block = st.enter_context(nc.Block())

        n = dict(ld=0, wld=0, vv=0, sv=0, tv=0, mm=0, dst=0)

        def g_iota(e):
            e.iota(iota_p[:], pattern=[[0, 1]], base=0, channel_multiplier=1)
            e.iota(iota_f[:], pattern=[[1, 128]], base=0, channel_multiplier=0).then_inc(gio, 1)
        prog.on("gpsimd", g_iota)

        def s_loads(e):
            for h in range(16):
                e.dma_start(out=q1[h][:, :], in_=i_qkv[h * 128:(h + 1) * 128, SPC:TPC]).then_inc(ld, 16)
            for z in range(8):
                e.dma_start(out=kv[z][:, :], in_=i_qkv[2048 + z * 128:2048 + (z + 1) * 128, :]).then_inc(ld, 16)
            e.dma_start(out=slrow[:, :], in_=i_sl).then_inc(ld, 16)
        prog.on("sync", s_loads)
        n["ld"] = 25 * 16
        ld0 = n["ld"]

        n["vv"] += 1
        def v_init(e):
            e.wait_ge(gio, 1)
            e.tensor_copy(out=fidp[:], in_=iota_p[:])
            e.tensor_copy(out=fidf[:], in_=iota_f[:])
            e.tensor_scalar(out=identb[:], in0=fidf[:], scalar1=fidp[:], scalar2=None,
                            op0=ALU.is_equal)
            e.memset(onesf[:, :], 1.0)
            e.memset(onescol[:, :], 1.0)
            e.memset(negcol[:, :], -1.0).then_inc(vv, 1)
        prog.on("vector", v_init)
        init_v = n["vv"]

        ATT_SCALE = float(1.0 / np.sqrt(128.0))
        for h in range(16):
            g = h // 4
            k0 = kv[2 * g]; v0 = kv[2 * g + 1]
            bsa = n["vv"]
            n["vv"] += 2
            def f_tt(e, h=h, k0=k0, v0=v0, bsa=bsa, w=init_v, wmm=n["mm"]):
                e.wait_ge(ld, ld0)
                e.wait_ge(vv, w)
                e.wait_ge(mmv, wmm)
                e.tensor_tensor(out=tmp0[:, :], in0=q1[h][:, :], in1=k0[:, 0:SPC],
                                op=ALU.mult).then_inc(vv, 1)
                e.wait_ge(vv, bsa + 1)
                e.tensor_tensor(out=tmp1[:, :], in0=q1[h][:, :], in1=k0[:, SPC:TPC],
                                op=ALU.mult).then_inc(vv, 1)
            prog.on("vector", f_tt)
            ttv = n["vv"]

            for half in range(2):
                n["mm"] += 1
                def f_dmm(e, half=half, w=ttv, ws=n["sv"]):
                    e.wait_ge(vv, w)
                    e.wait_ge(sv, ws)
                    e.matmul(pdif[0:1, 0:512], onescol[:, :], tmp0[:, half * 512:(half + 1) * 512],
                             start=True, stop=False)
                    e.matmul(pdif[0:1, 0:512], negcol[:, :], tmp1[:, half * 512:(half + 1) * 512],
                             start=False, stop=True).then_inc(mmv, 1)
                prog.on("tensor", f_dmm)
                dmmv = n["mm"]
                n["sv"] += 1
                def f_sig(e, h=h, half=half, w=dmmv):
                    e.wait_ge(mmv, w)
                    e.activation(out=p10[0:1, half * 512:(half + 1) * 512], in_=pdif[0:1, 0:512],
                                 func=AF.Sigmoid, scale=ATT_SCALE,
                                 bias=slrow[0:1, h:h + 1]).then_inc(sv, 1)
                prog.on("scalar", f_sig)
            sigv = n["sv"]

            bsb = n["vv"]
            n["vv"] += 1
            def f_dv(e, v0=v0, bsb=bsb):
                e.tensor_tensor(out=dvt[:, :], in0=v0[:, 0:SPC], in1=v0[:, SPC:TPC],
                                op=ALU.subtract).then_inc(vv, 1)
            prog.on("vector", f_dv)
            dvv = n["vv"]

            for half in range(2):
                n["mm"] += 1
                def f_bmm(e, half=half, w=sigv, wev=n["vv"]):
                    e.wait_ge(sv, w)
                    e.wait_ge(vv, wev)
                    e.matmul(pb10[:, 0:512], onesf[:, :], p10[0:1, half * 512:(half + 1) * 512],
                             start=True, stop=True).then_inc(mmv, 1)
                prog.on("tensor", f_bmm)
                bmmv = n["mm"]
                bsc = n["vv"]
                n["vv"] += 2
                def f_o1(e, h=h, half=half, v0=v0, w=bmmv, wd=dvv, bsc=bsc):
                    e.wait_ge(mmv, w)
                    e.wait_ge(vv, wd)
                    e.tensor_tensor(out=o1f[h][:, half * 512:(half + 1) * 512],
                                    in0=dvt[:, half * 512:(half + 1) * 512], in1=pb10[:, 0:512],
                                    op=ALU.mult).then_inc(vv, 1)
                    e.wait_ge(vv, bsc + 1)
                    e.tensor_tensor(out=o1f[h][:, half * 512:(half + 1) * 512],
                                    in0=o1f[h][:, half * 512:(half + 1) * 512],
                                    in1=v0[:, SPC + half * 512:SPC + (half + 1) * 512],
                                    op=ALU.add).then_inc(vv, 1)
                prog.on("vector", f_o1)
        att_v = n["vv"]
        att_mm = n["mm"]

        ev_after = [0]
        for tg in range(4):
            tok0 = tg * 512
            b = tok0 // SPC
            for db in range(16):
                dbu = tg * 16 + db
                wbuf = dbu % 2
                n["wld"] += 16
                def f_wl(e, db=db, wbuf=wbuf, nm=max(0, n["mm"] - att_mm - 1) + att_mm if n["mm"] > att_mm else att_mm):
                    e.wait_ge(mmv, nm)
                    e.dma_start(out=wst[wbuf][:, :].rearrange("o (k d) -> o k d", k=16),
                                in_=i_woT.rearrange("(k o) d -> o k d", o=128)[
                                    :, :, db * 128:(db + 1) * 128]).then_inc(wld, 16)
                prog.on("gpsimd", f_wl)
                wlv = n["wld"]

                pz = pso[dbu % 2]
                n["mm"] += 1
                def f_mm(e, tg=tg, db=db, wbuf=wbuf, pz=pz, w=wlv, wa=att_v,
                         need_ev=ev_after[0]):
                    e.wait_ge(wld, w)
                    e.wait_ge(vv, max(wa, need_ev))
                    last = None
                    for ob in range(16):
                        gg = ob // 4
                        if tg < 2:
                            mov = kv[2 * gg + 1][:, tg * 512:(tg + 1) * 512]
                        else:
                            mov = o1f[ob][:, (tg - 2) * 512:(tg - 1) * 512]
                        last = e.matmul(pz[:, :], wst[wbuf][:, ob * 128:(ob + 1) * 128], mov,
                                        start=(ob == 0), stop=(ob == 15))
                    last.then_inc(mmv, 1)
                prog.on("tensor", f_mm)
                mmvv = n["mm"]

                n["vv"] += 1
                def f_ev(e, pz=pz, w=mmvv, wtr=n["tv"]):
                    e.wait_ge(mmv, w)
                    e.wait_ge(tv, wtr)
                    e.tensor_copy(out=s0blk[:, :], in_=pz[:, :]).then_inc(vv, 1)
                prog.on("vector", f_ev)
                evv = n["vv"]
                ev_after[0] = evv

                for tsub in range(4):
                    pj = ptr[tsub % 2]
                    n["tv"] += 1
                    def f_tr(e, tsub=tsub, pj=pj, w=evv, wev=n["vv"]):
                        e.wait_ge(vv, wev)
                        e.transpose(pj[0:128, 0:128], s0blk[:, tsub * 128:(tsub + 1) * 128],
                                    identb[:]).then_inc(tv, 1)
                    prog.on("tensor", f_tr)
                    trv = n["tv"]
                    n["vv"] += 1
                    def f_trev(e, tsub=tsub, db=db, pj=pj, w=trv):
                        e.wait_ge(tv, w)
                        e.tensor_copy(out=s1T[:, (tsub * 16 + db) * 128:(tsub * 16 + db + 1) * 128],
                                      in_=pj[0:128, 0:128]).then_inc(vv, 1)
                    prog.on("vector", f_trev)
            s1T_v = n["vv"]

            for tsub in range(4):
                t0 = tok0 + tsub * 128
                b2, pos0 = t0 // SPC, t0 % SPC
                n["ld"] += 32
                def f_xg(e, b2=b2, pos0=pos0, t0=t0, w=n["vv"], wd=n["dst"]):
                    e.wait_ge(vv, w)
                    e.wait_ge(dst, wd)
                    e.dma_start(out=xt[:, :], in_=i_x1[b2, pos0:pos0 + 128, :]).then_inc(ld, 16)
                    e.dma_start(out=gt[:, :], in_=i_g1[t0:t0 + 128, :]).then_inc(ld, 16)
                prog.on("sync", f_xg)
                xgv = n["ld"]

                bsr = n["vv"]
                n["vv"] += 2
                def f_res(e, tsub=tsub, bsr=bsr, w=xgv, wd=n["dst"]):
                    e.wait_ge(ld, w)
                    e.wait_ge(dst, wd)
                    e.tensor_tensor(out=x2f[:, :].rearrange("t (db d) -> t db d", db=16),
                                    in0=gt[:, :].rearrange("t (db d) -> t db d", db=16),
                                    in1=s1T[:, :].rearrange("t (ts db d) -> t ts db d",
                                                            ts=4, db=16)[:, tsub, :, :],
                                    op=ALU.mult).then_inc(vv, 1)
                    e.wait_ge(vv, bsr + 1)
                    e.tensor_tensor(out=x2f[:, :], in0=x2f[:, :], in1=xt[:, :],
                                    op=ALU.add).then_inc(vv, 1)
                prog.on("vector", f_res)
                resv = n["vv"]

                n["dst"] += 16
                def f_odma(e, b2=b2, pos0=pos0, w=resv):
                    e.wait_ge(vv, w)
                    e.dma_start(out=o_out[b2, pos0:pos0 + 128, :], in_=x2f[:, :]).then_inc(dst, 16)
                prog.on("sync", f_odma)

        def f_fin(e, a=n["dst"]):
            e.wait_ge(dst, a)
        prog.on("sync", f_fin)
        prog.emit(block)
    return nc


def build_mega():
    nc = bass.Bass("TRN2", target_bir_lowering=False)
    io = {}

    def EI(nm, shp, dt):
        return nc.dram_tensor(nm, shp, dt, kind="ExternalInput").ap()

    io["xh"] = EI("xh", [BATCH, T2B, DM], F32)
    io["w_inT"] = EI("w_inT", [DM, PROJ], F32)
    io["conv_wr"] = EI("conv_wr", [128, 34 * 4], F32)
    io["conv_br"] = EI("conv_br", [128, 34], F32)
    io["dt_bias"] = EI("dt_bias", [NH, 1], F32)
    io["cvec"] = EI("cvec", [1, 8], F32)
    io["alog"] = EI("alog", [NH, 1], F32)
    io["w_gate"] = EI("w_gate", [DM, 8], F32)
    io["wi_0"] = EI("wi_0", [8, DM, 1024], F32)
    io["wi_1"] = EI("wi_1", [8, DM, 1024], F32)
    io["wo"] = EI("wo", [8, 1024, DM], F32)
    io["rmask16"] = EI("rmask16", [NH, 16], F32)
    io["invmask16"] = EI("invmask16", [NH, 16], F32)
    io["Drep"] = EI("Drep", [128, 32], F32)
    io["normw"] = EI("normw", [128, 32], F32)
    io["masknegT"] = EI("masknegT", [128, 128], F32)
    io["w_outT"] = EI("w_outT", [IDIM, DM], F32)
    io["wqkvT"] = EI("wqkvT", [DM, 3072], F32)
    io["ffnW"] = EI("ffnW", [DM, DM], F32)
    io["fscale"] = EI("fscale", [1, DM], F32)
    io["wo1T"] = EI("wo1T", [DM, DM], F32)
    io["slrow"] = EI("slrow", [1, 16], F32)

    io["outp"] = nc.dram_tensor("outp", [BATCH, SPC, DM], F32, kind="ExternalOutput").ap()

    def IN(nm, shp, dt, **kw):
        return nc.dram_tensor(nm, shp, dt, kind="Internal", **kw)

    io["d_xnT"] = IN("d_xnT", [DM, T2], BF16).ap()
    io["d_xnTlo"] = IN("d_xnTlo", [DM, T2], BF16).ap()
    io["d_sg"] = IN("d_sg", [IDIM, TPC], BF16).ap()
    io["d_xs"] = IN("d_xs", [IDIM, TPC], BF16).ap()
    io["d_bc"] = IN("d_bc", [2 * NSD, TPC], BF16).ap()
    io["d_dt"] = IN("d_dt", [NH, TPC], F32).ap()
    io["d_xp"] = IN("d_xp", [16, 128, IDIM], BF16).ap()
    io["d_sj"] = IN("d_sj", [16, 128, IDIM], F32).ap()
    io["spn_loc"] = IN("spn_loc", [BATCH * 128, IDIM], F32).ap()
    io["spn_all"] = IN("spn_all", [NCORES * BATCH * 128, IDIM], F32).ap()
    io["dsp_loc"] = IN("dsp_loc", [NH, 8], F32).ap()
    io["dsp_all"] = IN("dsp_all", [NCORES * NH, 8], F32).ap()
    io["d_g"] = IN("d_g", [TPC, DM], BF16).ap()
    io["d_dbg"] = IN("d_dbg", [128, 128], F32).ap()
    io["acsp_h"] = IN("acsp", [6, NH, TPC], BF16)
    io["d_yn"] = IN("d_yn", [IDIM, TPC], BF16).ap()
    io["d_x1"] = IN("d_x1", [BATCH, SPC, DM], F32).ap()
    io["d_xn1T"] = IN("d_xn1T", [DM, TPC], BF16).ap()
    io["d_qkv"] = IN("d_qkv", [3072, TPC], BF16).ap()
    io["d_g1"] = IN("d_g1", [TPC, DM], BF16).ap()
    io["ofd_h"] = IN("ofd", [TPC, DM], BF16)

    nc.all_engine_barrier()
    phase1(nc, io)
    nc.reset()
    phase2(nc, io)
    nc.reset()
    phase3(nc, io)
    nc.reset()
    phase4a(nc, io)
    nc.reset()
    phase4b(nc, io)
    nc.reset()
    phase5a(nc, io)
    nc.reset()
    phase5b(nc, io)
    return nc


def _shard_x(x):
    out = []
    xz = np.concatenate([np.zeros((BATCH, HALO, DM), np.float32), np.ascontiguousarray(x)], axis=1)
    for c in range(NCORES):
        out.append(np.ascontiguousarray(xz[:, c * SPC: c * SPC + T2B, :]))
    return out


_MEGA = {}


def kernel(**inputs):
    if "nc" not in _MEGA:
        _MEGA["nc"] = build_mega()
    nc = _MEGA["nc"]
    w_inT = np.ascontiguousarray(inputs["L0_attn_W_in"].T)
    conv_w = np.ascontiguousarray(
        inputs["L0_attn_conv_w"].reshape(34, 128, 4).transpose(1, 0, 2).reshape(128, 136))
    conv_b = np.ascontiguousarray(inputs["L0_attn_conv_b"].reshape(34, 128).T)
    dt_bias = np.ascontiguousarray(inputs["L0_attn_dt_bias"]).reshape(NH, 1)
    xs_sh = _shard_x(inputs["L0_attn_x"])
    cvec = np.array([[EPS_LN, 1.0, 0, 0, 0, 0, 0, 0]], np.float32)
    alog = np.ascontiguousarray(inputs["L0_attn_A_log"]).reshape(NH, 1)
    D = np.asarray(inputs["L0_attn_D"], np.float32)
    Drep = np.ascontiguousarray(np.concatenate(
        [np.tile(D[0::2], (64, 1)), np.tile(D[1::2], (64, 1))], axis=0))
    normw = np.ascontiguousarray(np.asarray(inputs["L0_attn_norm_w"], np.float32)
                                 .reshape(32, 128).T)
    mneg = np.where(np.arange(128)[None, :] >= np.arange(128)[:, None], 0.0,
                    -1e30).astype(np.float32).T.copy()
    w_outT = np.ascontiguousarray(inputs["L0_attn_W_out"].T)
    wqkvT = np.ascontiguousarray(inputs["L1_attn_wqkv"].T)
    wo1T = np.ascontiguousarray(inputs["L1_attn_wo"].T)
    ffnW = np.ascontiguousarray(inputs["L1_ffn_weight"])
    fscale = np.ascontiguousarray(inputs["L1_ffn_scale"]).reshape(1, DM)
    slopes = (2.0 ** (-0.5)) ** np.arange(1, 17)
    slrow = np.ascontiguousarray((-slopes).reshape(1, 16)).astype(np.float32)
    w_gate = np.ascontiguousarray(inputs["L0_ffn_w_gate"])
    wi_0 = np.ascontiguousarray(inputs["L0_ffn_wi_0"])
    wi_1 = np.ascontiguousarray(inputs["L0_ffn_wi_1"])
    wo = np.ascontiguousarray(inputs["L0_ffn_wo"])
    in_maps = []
    for c in range(NCORES):
        rm = np.zeros((NH, 16), np.float32)
        rm[:, 0:2 * c] = 1.0
        in_maps.append({
            "xh": xs_sh[c], "w_inT": w_inT, "conv_wr": conv_w, "conv_br": conv_b,
            "dt_bias": dt_bias, "cvec": cvec, "alog": alog,
            "w_gate": w_gate, "wi_0": wi_0, "wi_1": wi_1, "wo": wo,
            "rmask16": rm, "invmask16": np.ascontiguousarray(1.0 - rm),
            "Drep": Drep, "normw": normw, "masknegT": mneg,
            "w_outT": w_outT, "wqkvT": wqkvT, "ffnW": ffnW, "fscale": fscale,
            "wo1T": wo1T, "slrow": slrow})
    res = run_bass_kernel_spmd(nc, in_maps, core_ids=list(range(NCORES)))
    out = np.concatenate([np.asarray(res.results[c]["outp"]) for c in range(NCORES)], axis=1)
    return np.ascontiguousarray(out.astype(np.float32))
